# version 03
# baseline (speedup 1.0000x reference)
import numpy as np
import ml_dtypes
from contextlib import ExitStack
import concourse.bass as bass
import concourse.bass_isa as bass_isa
import concourse.mybir as mybir
from concourse.bass_utils import run_bass_kernel_spmd

F32 = mybir.dt.float32
BF16 = mybir.dt.bfloat16
E4 = mybir.dt.float8e4
AF = mybir.ActivationFunctionType
ALU = mybir.AluOpType
DR = mybir.MatmulPerfMode.DoubleRow
AXX = mybir.AxisListType.X

T = 2048
D = 2048
HID = 8192
NH, NKV, HD = 16, 4, 128
NT = T // 128
DC = D // 128
KTOT = 4096

DEBUG_STAGE = 0


class Prog:
    ENG = ("t", "v", "s", "g", "q")

    def __init__(self):
        self.ops = []
        self.barriers = []

    def op(self, eng, emit, reads=(), writes=(), dma=False):
        self.ops.append((eng, emit, tuple(reads), tuple(writes), dma))

    def barrier(self):
        self.barriers.append(len(self.ops))

    def emit(self, block, sems, dsems):
        n = len(self.ops)
        comp = [None] * n
        cnt = {e: 0 for e in self.ENG}
        dcnt = {e: 0 for e in self.ENG}
        for i, (eng, emit, reads, writes, dma) in enumerate(self.ops):
            if dma:
                dcnt[eng] += 1
                comp[i] = ("d", eng, 16 * dcnt[eng])
            else:
                cnt[eng] += 1
                comp[i] = ("c", eng, cnt[eng])
        last_writer = {}
        readers = {}
        waits = [[] for _ in range(n)]
        bar_idx = 0
        barrier_tokens = []
        run_c = {e: 0 for e in self.ENG}
        run_d = {e: 0 for e in self.ENG}
        for i, (eng, emit, reads, writes, dma) in enumerate(self.ops):
            while bar_idx < len(self.barriers) and self.barriers[bar_idx] == i:
                toks = []
                for e2 in self.ENG:
                    if run_c[e2]:
                        toks.append(("c", e2, run_c[e2]))
                    if run_d[e2]:
                        toks.append(("d", e2, run_d[e2]))
                barrier_tokens = toks
                last_writer = {}
                readers = {}
                bar_idx += 1
            w = list(barrier_tokens)
            for k in reads:
                if k in last_writer:
                    w.append(comp[last_writer[k]])
            for k in writes:
                if k in last_writer:
                    w.append(comp[last_writer[k]])
                for r in readers.get(k, ()):
                    w.append(comp[r])
            waits[i] = w
            for k in writes:
                last_writer[k] = i
                readers[k] = []
            for k in reads:
                readers.setdefault(k, []).append(i)
            kk, ee, vv = comp[i]
            (run_c if kk == "c" else run_d)[ee] = vv

        def body_for(ename):
            def body(e):
                water = {}
                for i, (eng, emit, reads, writes, dma) in enumerate(self.ops):
                    if eng != ename:
                        continue
                    for (k, e2, v) in sorted(set(waits[i])):
                        sem = sems[e2] if k == "c" else dsems[e2]
                        key = (k, e2)
                        if water.get(key, -1) >= v:
                            continue
                        e.wait_ge(sem, v)
                        water[key] = v
                    ins = emit(e)
                    if dma:
                        ins.then_inc(dsems[ename], 16)
                    else:
                        ins.then_inc(sems[ename], 1)
            return body
        block.tensor(body_for("t"))
        block.vector(body_for("v"))
        block.scalar(body_for("s"))
        block.gpsimd(body_for("g"))
        block.sync(body_for("q"))


def build_program(debug_stage=0):
    nc = bass.Bass("TRN2", target_bir_lowering=False, debug=False)
    x_in = nc.dram_tensor("x", [T, D], F32, kind="ExternalInput").ap()
    wq = nc.dram_tensor("wq", [D, D], F32, kind="ExternalInput").ap()
    wk = nc.dram_tensor("wk", [D, 512], F32, kind="ExternalInput").ap()
    wv = nc.dram_tensor("wv", [D, 512], F32, kind="ExternalInput").ap()
    wo = nc.dram_tensor("wo", [D, D], F32, kind="ExternalInput").ap()
    wg = nc.dram_tensor("wg", [2, D, HID], F32, kind="ExternalInput").ap()
    wl = nc.dram_tensor("wl", [HID, D], F32, kind="ExternalInput").ap()
    cosr = nc.dram_tensor("cosr", [T, 256], F32, kind="ExternalInput").ap()
    sinr = nc.dram_tensor("sinr", [T, 256], F32, kind="ExternalInput").ap()
    ccol_in = nc.dram_tensor("ccol", [128, 128], F32, kind="ExternalInput").ap()
    dmask_in = nc.dram_tensor("dmask", [128, 8, 512], BF16, kind="ExternalInput").ap()
    ident_in = nc.dram_tensor("identf", [128, 128], F32, kind="ExternalInput").ap()
    identb_in = nc.dram_tensor("identb", [128, 128], BF16, kind="ExternalInput").ap()
    ones_in = nc.dram_tensor("ones1", [128, 2], BF16, kind="ExternalInput").ap()
    onesf_in = nc.dram_tensor("onesf", [1, 128], F32, kind="ExternalInput").ap()

    y_out = nc.dram_tensor("y", [T, D], F32, kind="ExternalOutput").ap()

    r1 = nc.dram_tensor("r1", [T, D], F32, kind="Internal").ap()
    r2 = nc.dram_tensor("r2", [T, D], F32, kind="Internal").ap()
    x2t_d = nc.dram_tensor("x2t", [NT, DC, 128, 128], BF16, kind="Internal").ap()
    qt_d = nc.dram_tensor("qt_d", [NT, NH, 128, 128], BF16, kind="Internal").ap()
    wg8c = nc.dram_tensor("wg8c", [2, 16, 8, 128, 2, 512], E4, kind="Internal").ap()
    wl8c = nc.dram_tensor("wl8c", [4, 32, 128, 2, 512], E4, kind="Internal").ap()
    cin = nc.dram_tensor("cin", [1032, 2048], BF16, kind="Internal").ap()
    cout = nc.dram_tensor("cout", [2, 1032, 2048], BF16, kind="Internal").ap()

    es = ExitStack()
    SB = lambda nm, sh, dt: es.enter_context(nc.sbuf_tensor(nm, sh, dt))
    SBx = lambda st, nm, sh, dt: st.enter_context(nc.sbuf_tensor(nm, sh, dt))
    PS = lambda nm, sh, dt=F32: es.enter_context(nc.psum_tensor(nm, sh, dt))
    SEM = lambda nm: es.enter_context(nc.semaphore(nm))

    with es:
        xln8 = SB("xln8", [128, 8, 2, T], E4)
        xin = [SB(f"xin{i}", [128, D], F32) for i in range(2)]
        xl = [SB(f"xl{i}", [128, D], F32) for i in range(2)]
        stat = SB("stat", [128, 16], F32)
        ccol = SB("ccolS", [128, 128], F32)
        acol = SB("acolS", [128, 128], F32)
        onesf = SB("onesfS", [1, 128], F32)
        srow = SB("srowS", [1, 512], F32)
        cst = SB("cstS", [128, 2], F32)
        dm = SB("dmS", [128, 8, 512], BF16)
        identf = SB("identfS", [128, 128], F32)
        identb = SB("identbS", [128, 128], BF16)
        ones1 = SB("ones1S", [128, 2], BF16)
        zero1 = SB("zero1", [128, 1], F32)
        gate16 = [SB(f"gate16_{i}", [128, 512], BF16) for i in range(2)]
        tmp16 = [SB(f"tmp16_{i}", [128, 512], BF16) for i in range(2)]
        rpiece = [SB(f"rpiece{i}", [128, 512], F32) for i in range(2)]
        xres = [SB(f"xres{i}", [128, 512], F32) for i in range(2)]
        p = [PS(f"p{i}", [128, 512]) for i in range(7)]
        pt16 = PS("pt16", [128, 1024], BF16)
        sems = {e: SEM(f"sem_{e}") for e in Prog.ENG}
        dsems = {e: SEM(f"dsem_{e}") for e in Prog.ENG}
        rsem = SEM("rsem")
        block = es.enter_context(nc.Block())

        P = Prog()

        P.op("q", lambda e: e.dma_start(out=ccol[:], in_=ccol_in), writes=["ccol"], dma=True)
        P.op("q", lambda e: e.dma_start(out=dm[:], in_=dmask_in), writes=["dm"], dma=True)
        P.op("q", lambda e: e.dma_start(out=identf[:], in_=ident_in), writes=["identf"], dma=True)
        P.op("q", lambda e: e.dma_start(out=identb[:], in_=identb_in), writes=["identb"], dma=True)
        P.op("q", lambda e: e.dma_start(out=ones1[:], in_=ones_in), writes=["ones1"], dma=True)
        P.op("q", lambda e: e.dma_start(out=onesf[:], in_=onesf_in), writes=["onesf"], dma=True)
        P.op("v", lambda e: e.memset(zero1[:], 0.0), writes=["zero1"])
        P.op("v", lambda e: e.memset(cst[:, 0:1], 1.28e-4), writes=["cst"])
        P.op("v", lambda e: e.memset(cst[:, 1:2], 1e-6), writes=["cst"])

        def ln_tile(xin_t, xin_key, out_t, out_key, par=0, eps=1e-5):
            o = par * 8
            sk = f"stat{par}"
            ssx, smx, mneg, m2n2, rs = (stat[:, o + i:o + i + 1] for i in range(5))
            P.op("s", lambda e: e.activation(
                out_t[:], xin_t[:], AF.Square, accum_out=ssx),
                reads=[xin_key], writes=[out_key, sk])
            P.op("v", lambda e: e.reduce_sum(smx, xin_t[:], AXX),
                reads=[xin_key], writes=[sk])
            P.op("s", lambda e: e.mul(mneg, smx, -1.0 / D), reads=[sk], writes=[sk])
            P.op("s", lambda e: e.activation(m2n2, mneg, AF.Square), reads=[sk], writes=[sk])
            P.op("v", lambda e: e.tensor_scalar(m2n2, m2n2, -1.0, eps, ALU.mult, ALU.add),
                reads=[sk], writes=[sk])
            P.op("s", lambda e: e.activation(rs, ssx, AF.Sqrt, bias=m2n2, scale=1.0 / D),
                reads=[sk], writes=[sk])
            P.op("v", lambda e: e.reciprocal(rs, rs), reads=[sk], writes=[sk])
            P.op("v", lambda e: e.tensor_scalar(out_t[:], xin_t[:], mneg, rs, ALU.add, ALU.mult),
                reads=[xin_key, sk], writes=[out_key])

        def xpose_to_fp8(src_t, src_key, tc):
            for dq in range(4):
                bi = 5 + ((tc * 4 + dq) % 2)
                bank = p[bi]
                bk = f"p{bi}"
                def tp4(e, dq=dq, bank=bank, src_t=src_t):
                    for j in range(4):
                        dc = dq * 4 + j
                        ii = e.transpose(bank[:, j * 128:(j + 1) * 128],
                                         src_t[:, dc * 128:(dc + 1) * 128], identf[:])
                    return ii
                P.op("t", tp4, reads=[src_key, "identf"], writes=[bk])
                def ev4(e, dq=dq, bank=bank, tc=tc):
                    for j in range(4):
                        dc = dq * 4 + j
                        s_, a_ = dc // 2, dc % 2
                        ii = e.copy(xln8[:, s_, a_, tc * 128:(tc + 1) * 128],
                                    bank[:, j * 128:(j + 1) * 128])
                    return ii
                P.op("s", ev4, reads=[bk], writes=["xln8"])

        for tc in range(NT):
            b = tc % 2
            P.op("q", lambda e, tc=tc, b=b: e.dma_start(
                out=xin[b][:], in_=x_in[tc * 128:(tc + 1) * 128, :]),
                writes=[f"xin{b}"], dma=True)
            ln_tile(xin[b], f"xin{b}", xl[b], f"xl{b}", par=b)
            xpose_to_fp8(xl[b], f"xl{b}", tc)
        P.barrier()

        def ffn(base_dram, out_dram, cache_weights, wbuf, act8, wbk, a8k):
            for half in range(2):
                first = cache_weights and half == 0
                toff = half * 1024
                for hb in range(16):
                    wb = wbuf[hb % 2]
                    k = wbk[hb % 2]
                    for g in range(2):
                        dst = wb[:, g * 8:(g + 1) * 8]
                        cache = wg8c[g, hb].rearrange("s p a c -> p s a c")
                        if first:
                            src = wg[g][:, hb * 512:(hb + 1) * 512].rearrange(
                                "(s a p) c -> p s a c", s=8, a=2, p=128)
                            P.op("g", lambda e, src=src, dst=dst: e.dma_start(
                                out=dst, in_=src), writes=[k], dma=True)
                            P.op("g", lambda e, dst=dst, cache=cache: e.dma_start(
                                out=cache, in_=dst), reads=[k], dma=True)
                        else:
                            P.op("g", lambda e, dst=dst, cache=cache: e.dma_start(
                                out=dst, in_=cache), writes=[k], dma=True)
                    for hc in range(4):
                        hca = hb * 4 + hc
                        hs, aa = hca // 2, hca % 2
                        for blk in range(2):
                            t0 = toff + blk * 512
                            rot = (hca * 2 + blk) % 2
                            pg0, pg1 = p[rot * 2], p[rot * 2 + 1]
                            pk0, pk1 = f"p{rot*2}", f"p{rot*2+1}"
                            def mm_g(e, wb=wb, hc=hc, t0=t0, pg0=pg0, pg1=pg1):
                                for g in range(2):
                                    for s_ in range(8):
                                        ii = e.matmul(
                                            (pg0 if g == 0 else pg1)[:],
                                            wb[:, g * 8 + s_, :, hc * 128:(hc + 1) * 128],
                                            xln8[:, s_, :, t0:t0 + 512],
                                            start=(s_ == 0), stop=(s_ == 7),
                                            perf_mode=DR)
                                return ii
                            P.op("t", mm_g, reads=[k, "xln8"], writes=[pk0, pk1])
                            gk = f"gate16_{rot}"
                            P.op("s", lambda e, pg0=pg0, rot=rot: e.activation(
                                gate16[rot][:], pg0[:], AF.Gelu_apprx_tanh),
                                reads=[pk0], writes=[gk])
                            tk = f"tmp16_{rot}"
                            P.op("v", lambda e, pg1=pg1, rot=rot: e.tensor_tensor(
                                tmp16[rot][:], gate16[rot][:], pg1[:], ALU.mult),
                                reads=[gk, pk1], writes=[tk])
                            P.op("s", lambda e, rot=rot, hs=hs, aa=aa, blk=blk: e.copy(
                                act8[:, hs, aa, blk * 512:(blk + 1) * 512],
                                tmp16[rot][:]),
                                reads=[tk], writes=[a8k])
                for dob in range(4):
                    lb0, lb1 = wbuf[0], wbuf[1]
                    k0, k1 = wbk[0], wbk[1]
                    for half16, (lbx, kx) in enumerate(((lb0, k0), (lb1, k1))):
                        dst = lbx[:]
                        cache = wl8c[dob, half16 * 16:(half16 + 1) * 16].rearrange(
                            "hs p a c -> p hs a c")
                        if first:
                            src = wl[half16 * 4096:(half16 + 1) * 4096,
                                     dob * 512:(dob + 1) * 512].rearrange(
                                "(hs a p) c -> p hs a c", hs=16, a=2, p=128)
                            P.op("g", lambda e, src=src, dst=dst: e.dma_start(
                                out=dst, in_=src), writes=[kx], dma=True)
                            P.op("g", lambda e, dst=dst, cache=cache: e.dma_start(
                                out=cache, in_=dst), reads=[kx], dma=True)
                        else:
                            P.op("g", lambda e, dst=dst, cache=cache: e.dma_start(
                                out=dst, in_=cache), writes=[kx], dma=True)
                    for tcl in range(8):
                        tca = half * 8 + tcl
                        rot = tcl % 2
                        po = p[4 + rot]
                        pok = f"p{4+rot}"
                        def mm_l(e, tcl=tcl, po=po, lb0=lb0, lb1=lb1):
                            for hs in range(32):
                                lbx = lb0 if hs < 16 else lb1
                                ii = e.matmul(
                                    po[:], act8[:, hs, :, tcl * 128:(tcl + 1) * 128],
                                    lbx[:, hs % 16], start=(hs == 0), stop=(hs == 31),
                                    perf_mode=DR)
                            return ii
                        P.op("t", mm_l, reads=[k0, k1, a8k], writes=[pok])
                        xrk = f"xres{rot}"
                        P.op("q", lambda e, rot=rot, tca=tca, dob=dob: e.dma_start(
                            out=xres[rot][:], in_=base_dram[
                                tca * 128:(tca + 1) * 128, dob * 512:(dob + 1) * 512]),
                            writes=[xrk], dma=True)
                        rk = f"rpiece{rot}"
                        P.op("v", lambda e, rot=rot, po=po: e.scalar_tensor_tensor(
                            rpiece[rot][:], po[:], 0.5, xres[rot][:], ALU.mult, ALU.add),
                            reads=[pok, xrk], writes=[rk])
                        P.op("q", lambda e, rot=rot, tca=tca, dob=dob: e.dma_start(
                            out=out_dram[tca * 128:(tca + 1) * 128,
                                         dob * 512:(dob + 1) * 512],
                            in_=rpiece[rot][:]), reads=[rk], dma=True)

        es1 = ExitStack()
        act8 = es1.enter_context(nc.sbuf_tensor("act8", [128, 32, 2, 1024], E4))
        wbuf = [es1.enter_context(nc.sbuf_tensor(f"wbuf{i}", [128, 16, 2, 512], E4))
                for i in range(2)]
        ffn(x_in, r1, True, wbuf, act8, ["wbuf0", "wbuf1"], "act8")
        P.barrier()
        es1.close()

        if debug_stage == 1:
            for tc in range(NT):
                b = tc % 2
                P.op("q", lambda e, tc=tc, b=b: e.dma_start(
                    out=xin[b][:], in_=r1[tc * 128:(tc + 1) * 128, :]),
                    writes=[f"xin{b}"], dma=True)
                P.op("q", lambda e, tc=tc, b=b: e.dma_start(
                    out=y_out[tc * 128:(tc + 1) * 128, :], in_=xin[b][:]),
                    reads=[f"xin{b}"], dma=True)
            P.barrier()
        else:
            _build_attention(P, nc, locals())
            P.barrier()
            es3 = ExitStack()
            act8b = es3.enter_context(nc.sbuf_tensor("act8b", [128, 32, 2, 1024], E4))
            wbufb = [es3.enter_context(nc.sbuf_tensor(f"wbufb{i}", [128, 16, 2, 512], E4))
                     for i in range(2)]
            if debug_stage == 2:
                for tc in range(NT):
                    b = tc % 2
                    P.op("q", lambda e, tc=tc, b=b: e.dma_start(
                        out=xin[b][:], in_=r2[tc * 128:(tc + 1) * 128, :]),
                        writes=[f"xin{b}"], dma=True)
                    P.op("q", lambda e, tc=tc, b=b: e.dma_start(
                        out=y_out[tc * 128:(tc + 1) * 128, :], in_=xin[b][:]),
                        reads=[f"xin{b}"], dma=True)
                P.barrier()
            else:
                ffn(r2, y_out, False, wbufb, act8b, ["wbufb0", "wbufb1"], "act8b")
                P.barrier()
            es3.close()

        P.emit(block, sems, dsems)
        ndma_q = sum(1 for (eng, _, _, _, dma) in P.ops if eng == "q" and dma)
        ndma_g = sum(1 for (eng, _, _, _, dma) in P.ops if eng == "g" and dma)

        def final_wait(e):
            if ndma_q:
                e.wait_ge(dsems["q"], 16 * ndma_q)
            if ndma_g:
                e.wait_ge(dsems["g"], 16 * ndma_g)
        block.sync(final_wait)

    return nc


def _build_attention(P, nc, env):
    p = env["p"]; pt16 = env["pt16"]
    xin = env["xin"]; xl = env["xl"]; stat = env["stat"]
    xln8 = env["xln8"]; dm = env["dm"]; ccol = env["ccol"]; acol = env["acol"]
    identf = env["identf"]; onesf = env["onesf"]; ones1 = env["ones1"]
    srow = env["srow"]
    rpiece = env["rpiece"]; xres = env["xres"]
    r1 = env["r1"]; r2 = env["r2"]; x2t_d = env["x2t_d"]; qt_d = env["qt_d"]
    cin = env["cin"]; cout = env["cout"]
    cosr = env["cosr"]; sinr = env["sinr"]
    wq = env["wq"]; wk = env["wk"]; wv = env["wv"]; wo = env["wo"]
    ln_tile = env["ln_tile"]; xpose_to_fp8 = env["xpose_to_fp8"]
    SBx = env["SBx"]; cst = env["cst"]

    esB = ExitStack()
    x2s = [SBx(esB, f"x2s{i}", [128, DC, 128], BF16) for i in range(2)]
    x2tile = [SBx(esB, f"x2tile{i}", [128, DC, 128], BF16) for i in range(2)]
    wpro = [SBx(esB, f"wpro{i}", [128, 16, 512], BF16) for i in range(2)]
    cost = SBx(esB, "cost", [128, 256], F32)
    sint = SBx(esB, "sint", [128, 256], F32)
    pjf = [SBx(esB, f"pjf{i}", [128, 512], F32) for i in range(2)]
    rsc = [SBx(esB, f"rsc{i}", [128, 512], F32) for i in range(2)]
    tt1 = [SBx(esB, f"tt1_{i}", [128, 4, 64], F32) for i in range(2)]
    tt2 = [SBx(esB, f"tt2_{i}", [128, 4, 64], F32) for i in range(2)]
    ror = [SBx(esB, f"ror{i}", [128, 512], F32) for i in range(2)]
    ror2 = [SBx(esB, f"ror2_{i}", [128, 512], F32) for i in range(2)]
    stg = [SBx(esB, f"stg{i}", [128, 4, 128], BF16) for i in range(2)]
    cost2 = [SBx(esB, f"cost2_{i}", [128, 256], F32) for i in range(2)]
    sint2 = [SBx(esB, f"sint2_{i}", [128, 256], F32) for i in range(2)]
    aco = SBx(esB, "aco", [128, 4, 16], F32)
    ahi = SBx(esB, "ahi", [128, 64], BF16)
    alo32 = SBx(esB, "alo32", [128, 64], F32)
    alo = SBx(esB, "alo", [128, 64], BF16)

    def load_cossin(tc):
        P.op("q", lambda e, tc=tc: e.dma_start(
            out=cost[:], in_=cosr[tc * 128:(tc + 1) * 128, :]),
            writes=["cost"], dma=True)
        P.op("q", lambda e, tc=tc: e.dma_start(
            out=sint[:], in_=sinr[tc * 128:(tc + 1) * 128, :]),
            writes=["sint"], dma=True)

    def rope(src, src_key, dst, dst_key):
        sh = src[:].rearrange("p (h d) -> p h d", h=4)
        dh = dst[:].rearrange("p (h d) -> p h d", h=4)
        ch = cost[:].rearrange("p (h d) -> p h d", h=4)
        sn = sint[:].rearrange("p (h d) -> p h d", h=4)
        P.op("v", lambda e: e.tensor_mul(tt1[:], sh[:, :, 0:64], ch[:]),
            reads=[src_key, "cost"], writes=["tt1"])
        P.op("v", lambda e: e.tensor_mul(tt2[:], sh[:, :, 64:128], sn[:]),
            reads=[src_key, "sint"], writes=["tt2"])
        P.op("v", lambda e: e.tensor_add(dh[:, :, 0:64], tt1[:], tt2[:]),
            reads=["tt1", "tt2"], writes=[dst_key])
        P.op("v", lambda e: e.tensor_mul(tt1[:], sh[:, :, 64:128], ch[:]),
            reads=[src_key, "cost"], writes=["tt1"])
        P.op("v", lambda e: e.tensor_mul(tt2[:], sh[:, :, 0:64], sn[:]),
            reads=[src_key, "sint"], writes=["tt2"])
        P.op("v", lambda e: e.tensor_sub(dh[:, :, 64:128], tt1[:], tt2[:]),
            reads=["tt1", "tt2"], writes=[dst_key])

    def sumsq4(src_key):
        for h4 in range(4):
            P.op("s", lambda e, h4=h4: e.activation(
                rsc[:, h4 * 128:(h4 + 1) * 128], pjf[:, h4 * 128:(h4 + 1) * 128],
                AF.Square, accum_out=stat[:, h4:h4 + 1]),
                reads=[src_key], writes=["rsc", "stat"])

    for tc in range(NT):
        b = tc % 2
        P.op("q", lambda e, tc=tc, b=b: e.dma_start(
            out=xin[b][:], in_=r1[tc * 128:(tc + 1) * 128, :]),
            writes=[f"xin{b}"], dma=True)
        ln_tile(xin[b], f"xin{b}", xl[b], f"xl{b}", par=b)
        for dq in range(4):
            bi = 5 + ((tc * 4 + dq) % 2)
            bk = f"p{bi}"
            def tp4(e, dq=dq, bi=bi, b=b):
                for j in range(4):
                    dc = dq * 4 + j
                    ii = e.transpose(p[bi][:, j * 128:(j + 1) * 128],
                                     xl[b][:, dc * 128:(dc + 1) * 128], identf[:])
                return ii
            P.op("t", tp4, reads=[f"xl{b}", "identf"], writes=[bk])
            def ev4(e, dq=dq, bi=bi, b=b):
                for j in range(4):
                    dc = dq * 4 + j
                    ii = e.copy(x2s[b][:, dc, :], p[bi][:, j * 128:(j + 1) * 128])
                return ii
            P.op("s", ev4, reads=[bk], writes=[f"x2s{b}"])
        P.op("q", lambda e, tc=tc, b=b: e.dma_start(
            out=x2t_d[tc].rearrange("dc p c -> p dc c"), in_=x2s[b][:]),
            reads=[f"x2s{b}"], dma=True)
    P.barrier()

    P.op("g", lambda e: e.dma_start(
        out=wpro[0][:], in_=wk.rearrange("(vc p) c -> p vc c", p=128)),
        writes=["wpro0"], dma=True)
    P.op("g", lambda e: e.dma_start(
        out=wpro[1][:], in_=wv.rearrange("(vc p) c -> p vc c", p=128)),
        writes=["wpro1"], dma=True)

    def load_cossin2(tc, b):
        P.op("q", lambda e, tc=tc, b=b: e.dma_start(
            out=cost2[b][:], in_=cosr[tc * 128:(tc + 1) * 128, :]),
            writes=[f"cost2_{b}"], dma=True)
        P.op("q", lambda e, tc=tc, b=b: e.dma_start(
            out=sint2[b][:], in_=sinr[tc * 128:(tc + 1) * 128, :]),
            writes=[f"sint2_{b}"], dma=True)

    def rope2(b, src, src_key, dst, dst_key):
        sh = src[:].rearrange("p (h d) -> p h d", h=4)
        dh = dst[:].rearrange("p (h d) -> p h d", h=4)
        ch = cost2[b][:].rearrange("p (h d) -> p h d", h=4)
        sn = sint2[b][:].rearrange("p (h d) -> p h d", h=4)
        ck, sk2 = f"cost2_{b}", f"sint2_{b}"
        t1k, t2k = f"tt1_{b}", f"tt2_{b}"
        P.op("v", lambda e: e.tensor_mul(tt1[b][:], sh[:, :, 0:64], ch[:]),
            reads=[src_key, ck], writes=[t1k])
        P.op("v", lambda e: e.tensor_mul(tt2[b][:], sh[:, :, 64:128], sn[:]),
            reads=[src_key, sk2], writes=[t2k])
        P.op("v", lambda e: e.tensor_add(dh[:, :, 0:64], tt1[b][:], tt2[b][:]),
            reads=[t1k, t2k], writes=[dst_key])
        P.op("v", lambda e: e.tensor_mul(tt1[b][:], sh[:, :, 64:128], ch[:]),
            reads=[src_key, ck], writes=[t1k])
        P.op("v", lambda e: e.tensor_mul(tt2[b][:], sh[:, :, 0:64], sn[:]),
            reads=[src_key, sk2], writes=[t2k])
        P.op("v", lambda e: e.tensor_sub(dh[:, :, 64:128], tt1[b][:], tt2[b][:]),
            reads=[t1k, t2k], writes=[dst_key])

    def sumsq4b(b, src_key, outslice, outkey):
        for h4 in range(4):
            P.op("s", lambda e, h4=h4, b=b: e.activation(
                rsc[b][:, h4 * 128:(h4 + 1) * 128], pjf[b][:, h4 * 128:(h4 + 1) * 128],
                AF.Square, accum_out=stat[:, b * 8 + h4:b * 8 + h4 + 1]),
                reads=[src_key], writes=[f"rsc{b}", f"stat{b}"])

    for tc in range(NT):
        b = tc % 2
        xk = f"x2tile{b}"
        pjk = f"pjf{b}"
        rk = f"ror{b}"
        sgk = f"stg{b}"
        pK, pKk = (p[0], "p0") if b == 0 else (p[2], "p2")
        pV, pVk = (p[1], "p1") if b == 0 else (p[3], "p3")
        pT = 5 + b
        pTk = f"p{pT}"
        P.op("g", lambda e, tc=tc, b=b: e.dma_start(
            out=x2tile[b][:], in_=x2t_d[tc].rearrange("dc p c -> p dc c")),
            writes=[xk], dma=True)
        load_cossin2(tc, b)
        def mmk(e, b=b, pK=pK):
            for dc in range(DC):
                ii = e.matmul(pK[:], x2tile[b][:, dc, :], wpro[0][:, dc, :],
                              start=(dc == 0), stop=(dc == DC - 1))
            return ii
        P.op("t", mmk, reads=[xk, "wpro0"], writes=[pKk])
        P.op("s", lambda e, b=b, pK=pK: e.copy(pjf[b][:], pK[:]),
            reads=[pKk], writes=[pjk])
        def mmv(e, b=b, pV=pV):
            for dc in range(DC):
                ii = e.matmul(pV[:], x2tile[b][:, dc, :], wpro[1][:, dc, :],
                              start=(dc == 0), stop=(dc == DC - 1))
            return ii
        P.op("t", mmv, reads=[xk, "wpro1"], writes=[pVk])
        P.op("s", lambda e, b=b, pV=pV: e.copy(
            stg[b][:].rearrange("p h c -> p (h c)"), pV[:]),
            reads=[pVk], writes=[sgk])
        P.op("q", lambda e, tc=tc, b=b: e.dma_start(
            out=cin[512 + tc * 32:512 + (tc + 1) * 32, :].rearrange(
                "a (bb c) -> (a bb) c", bb=4),
            in_=stg[b][:].rearrange("p h c -> p (h c)")), reads=[sgk], dma=True)
        sumsq4b(b, pjk, None, None)
        P.op("s", lambda e, tc=tc, b=b: e.activation(
            aco[:, :, tc:tc + 1], stat[:, b * 8:b * 8 + 4], AF.Sqrt, bias=cst[:, 0:1]),
            reads=[f"stat{b}", "cst"], writes=["aco"])
        P.op("v", lambda e, tc=tc: e.reciprocal(aco[:, :, tc:tc + 1], aco[:, :, tc:tc + 1]),
            reads=["aco"], writes=["aco"])
        rope2(b, pjf[b], pjk, ror[b], rk)
        def tpk(e, b=b, pT=pT):
            for j in range(4):
                ii = e.transpose(p[pT][:, j * 128:(j + 1) * 128],
                                 ror[b][:, j * 128:(j + 1) * 128], identf[:])
            return ii
        P.op("t", tpk, reads=[rk, "identf"], writes=[pTk])
        def evk(e, b=b, pT=pT):
            for j in range(4):
                ii = e.copy(stg[b][:, j, :], p[pT][:, j * 128:(j + 1) * 128])
            return ii
        P.op("s", evk, reads=[pTk], writes=[sgk])
        P.op("q", lambda e, tc=tc, b=b: e.dma_start(
            out=cin[0:512, tc * 128:(tc + 1) * 128].rearrange(
                "(kv p) c -> p kv c", p=128),
            in_=stg[b][:]), reads=[sgk], dma=True)
    P.op("s", lambda e: e.copy(ahi[:], aco[:].rearrange("p a bb -> p (a bb)")),
        reads=["aco"], writes=["ahi"])
    P.op("v", lambda e: e.tensor_sub(alo32[:], aco[:].rearrange("p a bb -> p (a bb)"), ahi[:]),
        reads=["aco", "ahi"], writes=["alo32"])
    P.op("s", lambda e: e.copy(alo[:], alo32[:]), reads=["alo32"], writes=["alo"])
    P.op("q", lambda e: e.dma_start(
        out=cin[1024:1028, :].rearrange("r (pp f) -> (r pp) f", pp=32),
        in_=ahi[:]), reads=["ahi"], dma=True)
    P.op("q", lambda e: e.dma_start(
        out=cin[1028:1032, :].rearrange("r (pp f) -> (r pp) f", pp=32),
        in_=alo[:]), reads=["alo"], dma=True)

    for qdb in range(4):
        wslot = qdb % 2
        wpk = f"wpro{wslot}"
        P.op("g", lambda e, qdb=qdb, wslot=wslot: e.dma_start(
            out=wpro[wslot][:], in_=wq[:, qdb * 512:(qdb + 1) * 512].rearrange(
                "(vc p) c -> p vc c", p=128)),
            writes=[wpk], dma=True)
        for tc in range(NT):
            b = tc % 2
            xk = f"x2tile{b}"
            pjk = f"pjf{b}"
            rk = f"ror{b}"
            r2k = f"ror2_{b}"
            sgk = f"stg{b}"
            pQ, pQk = (p[0], "p0") if b == 0 else (p[2], "p2")
            pT = 5 + b
            pTk = f"p{pT}"
            P.op("g", lambda e, tc=tc, b=b: e.dma_start(
                out=x2tile[b][:], in_=x2t_d[tc].rearrange("dc p c -> p dc c")),
                writes=[xk], dma=True)
            load_cossin2(tc, b)
            def mmq(e, b=b, pQ=pQ, wslot=wslot):
                for dc in range(DC):
                    ii = e.matmul(pQ[:], x2tile[b][:, dc, :], wpro[wslot][:, dc, :],
                                  start=(dc == 0), stop=(dc == DC - 1))
                return ii
            P.op("t", mmq, reads=[xk, wpk], writes=[pQk])
            P.op("s", lambda e, b=b, pQ=pQ: e.copy(pjf[b][:], pQ[:]),
                reads=[pQk], writes=[pjk])
            sumsq4b(b, pjk, None, None)
            P.op("s", lambda e, b=b: e.activation(
                stat[:, b * 8 + 4:b * 8 + 8], stat[:, b * 8:b * 8 + 4], AF.Sqrt,
                bias=cst[:, 1:2], scale=1.0 / HD),
                reads=[f"stat{b}", "cst"], writes=[f"stat{b}"])
            P.op("v", lambda e, b=b: e.reciprocal(
                stat[:, b * 8 + 4:b * 8 + 8], stat[:, b * 8 + 4:b * 8 + 8]),
                reads=[f"stat{b}"], writes=[f"stat{b}"])
            rope2(b, pjf[b], pjk, ror[b], rk)
            def bq4(e, b=b):
                for h4 in range(4):
                    ii = e.tensor_scalar(
                        ror2[b][:, h4 * 128:(h4 + 1) * 128],
                        ror[b][:, h4 * 128:(h4 + 1) * 128],
                        stat[:, b * 8 + 4 + h4:b * 8 + 5 + h4], None, ALU.mult)
                return ii
            P.op("v", bq4, reads=[rk, f"stat{b}"], writes=[r2k])
            def tpq(e, b=b, pT=pT):
                for j in range(4):
                    ii = e.transpose(p[pT][:, j * 128:(j + 1) * 128],
                                     ror2[b][:, j * 128:(j + 1) * 128], identf[:])
                return ii
            P.op("t", tpq, reads=[r2k, "identf"], writes=[pTk])
            def evq(e, b=b, pT=pT):
                for j in range(4):
                    ii = e.copy(stg[b][:, j, :], p[pT][:, j * 128:(j + 1) * 128])
                return ii
            P.op("s", evq, reads=[pTk], writes=[sgk])
            P.op("q", lambda e, tc=tc, qdb=qdb, b=b: e.dma_start(
                out=qt_d[tc, qdb * 4:(qdb + 1) * 4].rearrange("q p c -> p q c"),
                in_=stg[b][:]), reads=[sgk], dma=True)
    P.barrier()
    esB.close()

    esC = ExitStack()
    vt = SBx(esC, "vt", [128, 32, 512], BF16)
    ktbuf = [SBx(esC, f"ktbuf{i}", [128, 4096], BF16) for i in range(2)]
    qtbuf = [SBx(esC, f"qtbuf{i}", [128, 4, 512], BF16) for i in range(2)]
    et = [SBx(esC, f"et{i}", [128, 512], BF16) for i in range(4)]
    s2f = [SBx(esC, f"s2f{i}", [128, 1024], F32) for i in range(2)]
    yt = SBx(esC, "yt", [128, 16, 512], BF16)
    rb = [SBx(esC, f"rb{i}", [128, 512], F32) for i in range(2)]
    wo16 = SBx(esC, "wo16", [128, 16, 512], BF16)
    ahl = [SBx(esC, f"ahl{i}", [128, 64], F32) for i in range(2)]

    groups = [[0, 1], [2, 3], [4, 5], [6, 7]]
    P.op("g", lambda e: e.collective_compute(
        "AllGather", ALU.bypass, groups, [cin], [cout]),
        reads=["cin"], writes=["cout"])
    for kc in range(32):
        slot, lt = kc // 16, kc % 16
        P.op("g", lambda e, kc=kc, slot=slot, lt=lt: e.dma_start(
            out=vt[:, kc, :],
            in_=cout[slot, 512 + lt * 32:512 + (lt + 1) * 32, :].rearrange(
                "a (b c) -> (a b) c", b=4)),
            reads=["cout"], writes=["vt"], dma=True)
    for slot in range(2):
        P.op("g", lambda e, slot=slot: e.dma_start(
            out=ahl[0][:], in_=cout[slot, 1024:1028, :].rearrange(
                "r (pp f) -> (r pp) f", pp=32)),
            reads=["cout"], writes=["ahl0"], dma=True)
        P.op("g", lambda e, slot=slot: e.dma_start(
            out=ahl[1][:], in_=cout[slot, 1028:1032, :].rearrange(
                "r (pp f) -> (r pp) f", pp=32)),
            reads=["cout"], writes=["ahl1"], dma=True)
        P.op("v", lambda e, slot=slot: e.tensor_add(
            acol[:].rearrange("p (a b) -> p a b", a=4)[:, :, slot * 16:(slot + 1) * 16],
            ahl[0][:].rearrange("p (a b) -> p a b", a=4),
            ahl[1][:].rearrange("p (a b) -> p a b", a=4)),
            reads=["ahl0", "ahl1"], writes=["acol"])

    for qb in range(4):
        for kv in range(4):
            cmb = qb * 4 + kv
            rot = cmb % 2
            kk = f"ktbuf{rot}"
            qk = f"qtbuf{rot}"
            for slot in range(2):
                P.op("g", lambda e, kv=kv, slot=slot, rot=rot: e.dma_start(
                    out=ktbuf[rot][:, slot * 2048:(slot + 1) * 2048],
                    in_=cout[slot, kv * 128:(kv + 1) * 128, :]),
                    reads=["cout"], writes=[kk], dma=True)
            for tq in range(4):
                P.op("g", lambda e, qb=qb, kv=kv, rot=rot, tq=tq: e.dma_start(
                    out=qtbuf[rot][:, :, tq * 128:(tq + 1) * 128],
                    in_=qt_d[qb * 4 + tq, kv * 4:(kv + 1) * 4].rearrange(
                        "q p c -> p q c")),
                    writes=[qk], dma=True)
            nkc = min(32, 4 * qb + 20)
            sbanks = (1, 2, 5, 6)
            for qh in range(4):
                py = p[0] if qh % 2 == 0 else p[3]
                pyk = "p0" if qh % 2 == 0 else "p3"
                for kc in range(nkc):
                    bi = sbanks[kc % 4]
                    ps, psk = p[bi], f"p{bi}"
                    ee = kc % 4
                    ek = f"et{ee}"
                    first = kc == 0
                    last = kc == nkc - 1
                    P.op("t", lambda e, ps=ps, rot=rot, kc=kc, qh=qh: e.matmul(
                        ps[:], ktbuf[rot][:, kc * 128:(kc + 1) * 128],
                        qtbuf[rot][:, qh, :], start=True, stop=True),
                        reads=[kk, qk], writes=[psk])
                    j = kc - 4 * qb
                    jj = kc - 16 - 4 * qb
                    dslot = j if 0 <= j < 4 else (4 + jj if 0 <= jj < 4 else None)
                    if dslot is not None:
                        sk = f"s2f{ee % 2}"
                        P.op("v", lambda e, ps=ps, ee=ee, dslot=dslot: e.tensor_add(
                            s2f[ee % 2][:, (ee // 2) * 512:(ee // 2 + 1) * 512],
                            ps[:], dm[:, dslot, :]),
                            reads=[psk, "dm"], writes=[sk])
                        src = s2f[ee % 2][:, (ee // 2) * 512:(ee // 2 + 1) * 512]
                        srck = sk
                    else:
                        src, srck = ps[:], psk
                    P.op("s", lambda e, src=src, ee=ee, qh=qh, kv=kv, kc=kc, qb=qb: e.activation(
                        et[ee][:], src, AF.Exp,
                        bias=ccol[:, qb * 32 + kc:qb * 32 + kc + 1],
                        scale=acol[:].rearrange("p (a bb) -> p a bb", a=4)[:, kv, kc:kc + 1]),
                        reads=[srck, "ccol", "acol"], writes=[ek])
                    P.op("t", lambda e, ee=ee, first=first, last=last, qh=qh: e.matmul(
                        p[4][32 * qh:32 * qh + 1, :], ones1[:, 0:1],
                        et[ee][:], start=first, stop=last,
                        tile_position=(0, 32 * qh)),
                        reads=[ek, "ones1"], writes=["p4"])
                    P.op("t", lambda e, ee=ee, first=first, last=last, kc=kc, kv=kv, py=py: e.matmul(
                        py[:], vt[:, kc, kv * 128:(kv + 1) * 128],
                        et[ee][:], start=first, stop=last),
                        reads=[ek, "vt"], writes=[pyk])
                rr = qh % 2
                P.op("s", lambda e, qh=qh: e.copy(srow[:], p[4][32 * qh:32 * qh + 1, :]),
                    reads=["p4"], writes=["srow"])
                P.op("v", lambda e: e.reciprocal(srow[:], srow[:]),
                    reads=["srow"], writes=["srow"])
                P.op("t", lambda e: e.matmul(p[2][:], onesf[:], srow[:],
                                             start=True, stop=True),
                    reads=["srow", "onesf"], writes=["p2"])
                P.op("s", lambda e, rr=rr: e.copy(rb[rr][:], p[2][:]),
                    reads=["p2"], writes=[f"rb{rr}"])
                P.op("v", lambda e, qh=qh, kv=kv, rr=rr, py=py, pyk=pyk: e.tensor_mul(
                    yt[:, kv * 4 + qh, :], py[:], rb[rr][:]),
                    reads=[pyk, f"rb{rr}"], writes=["yt"])
        for dob in range(4):
            P.op("g", lambda e, dob=dob: e.dma_start(
                out=wo16[:], in_=wo[:, dob * 512:(dob + 1) * 512].rearrange(
                    "(vc p) c -> p vc c", p=128)),
                writes=["wo16"], dma=True)
            for tcq in range(4):
                po = p[tcq]
                pok = f"p{tcq}"
                def mm_wo(e, tcq=tcq, po=po):
                    for vc in range(16):
                        ii = e.matmul(po[:], yt[:, vc, tcq * 128:(tcq + 1) * 128],
                                      wo16[:, vc, :], start=(vc == 0), stop=(vc == 15))
                    return ii
                P.op("t", mm_wo, reads=["yt", "wo16"], writes=[pok])
                rr = tcq % 2
                P.op("q", lambda e, qb=qb, tcq=tcq, dob=dob, rr=rr: e.dma_start(
                    out=xres[rr][:, 0:512], in_=r1[
                        qb * 512 + tcq * 128:qb * 512 + (tcq + 1) * 128,
                        dob * 512:(dob + 1) * 512]),
                    writes=[f"xres{rr}"], dma=True)
                P.op("v", lambda e, po=po, rr=rr: e.tensor_add(
                    rpiece[rr][:], po[:], xres[rr][:, 0:512]),
                    reads=[pok, f"xres{rr}"], writes=[f"rpiece{rr}"])
                P.op("q", lambda e, qb=qb, tcq=tcq, dob=dob, rr=rr: e.dma_start(
                    out=r2[qb * 512 + tcq * 128:qb * 512 + (tcq + 1) * 128,
                           dob * 512:(dob + 1) * 512],
                    in_=rpiece[rr][:]), reads=[f"rpiece{rr}"], dma=True)
    P.barrier()
    esC.close()

    for tc in range(NT):
        b = tc % 2
        P.op("q", lambda e, tc=tc, b=b: e.dma_start(
            out=xin[b][:], in_=r2[tc * 128:(tc + 1) * 128, :]),
            writes=[f"xin{b}"], dma=True)
        ln_tile(xin[b], f"xin{b}", xl[b], f"xl{b}", par=b)
        xpose_to_fp8(xl[b], f"xl{b}", tc)


def _host_inputs(core, full):
    b, h = core // 2, core % 2
    row0 = h * T
    x = np.ascontiguousarray(full["L0_attn_x"][b, row0:row0 + T, :])

    half = 64
    inv = 1.0 / (100000.0 ** (np.arange(half, dtype=np.float64) / half))
    pos = np.arange(row0, row0 + T, dtype=np.float64)[:, None] * inv[None, :]
    cos1 = np.cos(pos).astype(np.float32)
    sin1 = np.sin(pos).astype(np.float32)
    cosr = np.tile(cos1, (1, 4))
    sinr = np.tile(sin1, (1, 4))

    ccol = np.zeros((128, 4, 32), np.float32)
    for qb in range(4):
        for kc in range(32):
            if kc >= row0 // 128 + 4 * qb + 4:
                ccol[:, qb, kc] = -30000.0
    ccol = ccol.reshape(128, 128)
    dmask = np.zeros((128, 8, 512), ml_dtypes.bfloat16)
    r = np.arange(128)[:, None]
    c = np.arange(512)[None, :]
    for j in range(4):
        pat = np.where(j * 128 + r > c, -1e6, 0.0).astype(ml_dtypes.bfloat16)
        if h == 0:
            dmask[:, j, :] = pat
        dmask[:, 4 + j, :] = pat

    return {
        "x": x,
        "wq": full["L0_attn_wq"], "wk": full["L0_attn_wk"],
        "wv": full["L0_attn_wv"], "wo": full["L0_attn_wo"],
        "wg": full["L0_ffn_w_gating"], "wl": full["L0_ffn_w_linear"],
        "cosr": cosr, "sinr": sinr, "ccol": ccol, "dmask": np.asarray(dmask),
        "identf": np.eye(128, dtype=np.float32),
        "identb": np.asarray(np.eye(128, dtype=ml_dtypes.bfloat16)),
        "ones1": np.asarray(np.ones((128, 2), ml_dtypes.bfloat16)),
        "onesf": np.ones((1, 128), np.float32),
    }


def kernel(**inputs):
    nc = build_program(debug_stage=DEBUG_STAGE)
    in_maps = [_host_inputs(c, inputs) for c in range(8)]
    res = run_bass_kernel_spmd(nc, in_maps, core_ids=list(range(8)))
    out = np.empty((4, 4096, 2048), np.float32)
    for c in range(8):
        b, h = c // 2, c % 2
        out[b, h * T:(h + 1) * T, :] = res.results[c]["y"]
    return out


# version 10
# speedup vs baseline: 1.2699x; 1.2699x over previous
import numpy as np
import ml_dtypes
from contextlib import ExitStack
import concourse.bass as bass
import concourse.bass_isa as bass_isa
import concourse.mybir as mybir
from concourse.bass_utils import run_bass_kernel_spmd

F32 = mybir.dt.float32
BF16 = mybir.dt.bfloat16
E4 = mybir.dt.float8e4
AF = mybir.ActivationFunctionType
ALU = mybir.AluOpType
DR = mybir.MatmulPerfMode.DoubleRow
AXX = mybir.AxisListType.X

T = 2048
D = 2048
HID = 8192
NH, NKV, HD = 16, 4, 128
NT = T // 128
DC = D // 128
KTOT = 4096

DEBUG_STAGE = 0


class Prog:
    ENG = ("t", "v", "s", "g", "q")

    def __init__(self):
        self.ops = []
        self.barriers = []

    def op(self, eng, emit, reads=(), writes=(), dma=False):
        self.ops.append((eng, emit, tuple(reads), tuple(writes), dma))

    def barrier(self):
        self.barriers.append(len(self.ops))

    def emit(self, block, sems, dsems):
        n = len(self.ops)
        comp = [None] * n
        cnt = {e: 0 for e in self.ENG}
        dcnt = {e: 0 for e in self.ENG}
        for i, (eng, emit, reads, writes, dma) in enumerate(self.ops):
            if dma:
                dcnt[eng] += 1
                comp[i] = ("d", eng, 16 * dcnt[eng])
            else:
                cnt[eng] += 1
                comp[i] = ("c", eng, cnt[eng])
        last_writer = {}
        readers = {}
        waits = [[] for _ in range(n)]
        bar_idx = 0
        barrier_tokens = []
        run_c = {e: 0 for e in self.ENG}
        run_d = {e: 0 for e in self.ENG}
        for i, (eng, emit, reads, writes, dma) in enumerate(self.ops):
            while bar_idx < len(self.barriers) and self.barriers[bar_idx] == i:
                toks = []
                for e2 in self.ENG:
                    if run_c[e2]:
                        toks.append(("c", e2, run_c[e2]))
                    if run_d[e2]:
                        toks.append(("d", e2, run_d[e2]))
                barrier_tokens = toks
                last_writer = {}
                readers = {}
                bar_idx += 1
            w = list(barrier_tokens)
            for k in reads:
                if k in last_writer:
                    w.append(comp[last_writer[k]])
            for k in writes:
                if k in last_writer:
                    w.append(comp[last_writer[k]])
                for r in readers.get(k, ()):
                    w.append(comp[r])
            waits[i] = w
            for k in writes:
                last_writer[k] = i
                readers[k] = []
            for k in reads:
                readers.setdefault(k, []).append(i)
            kk, ee, vv = comp[i]
            (run_c if kk == "c" else run_d)[ee] = vv

        def body_for(ename):
            def body(e):
                water = {}
                for i, (eng, emit, reads, writes, dma) in enumerate(self.ops):
                    if eng != ename:
                        continue
                    for (k, e2, v) in sorted(set(waits[i])):
                        sem = sems[e2] if k == "c" else dsems[e2]
                        key = (k, e2)
                        if water.get(key, -1) >= v:
                            continue
                        e.wait_ge(sem, v)
                        water[key] = v
                    ins = emit(e)
                    if dma:
                        ins.then_inc(dsems[ename], 16)
                    else:
                        ins.then_inc(sems[ename], 1)
            return body
        block.tensor(body_for("t"))
        block.vector(body_for("v"))
        block.scalar(body_for("s"))
        block.gpsimd(body_for("g"))
        block.sync(body_for("q"))


def build_program(debug_stage=0):
    nc = bass.Bass("TRN2", target_bir_lowering=False, debug=False)
    x_in = nc.dram_tensor("x", [T, D], F32, kind="ExternalInput").ap()
    wq = nc.dram_tensor("wq", [D, D], F32, kind="ExternalInput").ap()
    wk = nc.dram_tensor("wk", [D, 512], F32, kind="ExternalInput").ap()
    wv = nc.dram_tensor("wv", [D, 512], F32, kind="ExternalInput").ap()
    wo = nc.dram_tensor("wo", [D, D], F32, kind="ExternalInput").ap()
    wg = nc.dram_tensor("wg", [2, D, HID], F32, kind="ExternalInput").ap()
    wl = nc.dram_tensor("wl", [HID, D], F32, kind="ExternalInput").ap()
    cosr = nc.dram_tensor("cosr", [T, 256], F32, kind="ExternalInput").ap()
    sinr = nc.dram_tensor("sinr", [T, 256], F32, kind="ExternalInput").ap()
    ccol_in = nc.dram_tensor("ccol", [128, 128], F32, kind="ExternalInput").ap()
    dmask_in = nc.dram_tensor("dmask", [128, 8, 512], BF16, kind="ExternalInput").ap()
    ident_in = nc.dram_tensor("identf", [128, 128], F32, kind="ExternalInput").ap()
    identb_in = nc.dram_tensor("identb", [128, 128], BF16, kind="ExternalInput").ap()
    ones_in = nc.dram_tensor("ones1", [128, 2], BF16, kind="ExternalInput").ap()
    onesf_in = nc.dram_tensor("onesf", [1, 128], F32, kind="ExternalInput").ap()

    y_out = nc.dram_tensor("y", [T, D], F32, kind="ExternalOutput").ap()

    r1 = nc.dram_tensor("r1", [T, D], F32, kind="Internal").ap()
    r2 = nc.dram_tensor("r2", [T, D], F32, kind="Internal").ap()
    x2t_d = nc.dram_tensor("x2t", [NT, DC, 128, 128], BF16, kind="Internal").ap()
    qt_d = nc.dram_tensor("qt_d", [NT, NH, 128, 128], BF16, kind="Internal").ap()
    wg8c = nc.dram_tensor("wg8c", [2, 16, 8, 128, 2, 512], E4, kind="Internal").ap()
    wl8c = nc.dram_tensor("wl8c", [4, 32, 128, 2, 512], E4, kind="Internal").ap()
    cin = nc.dram_tensor("cin", [1032, 2048], BF16, kind="Internal").ap()
    wqc_d = nc.dram_tensor("wqc", [4, 128, 16, 512], BF16, kind="Internal").ap()
    cout = nc.dram_tensor("cout", [2, 1032, 2048], BF16, kind="Internal").ap()

    es = ExitStack()
    SB = lambda nm, sh, dt: es.enter_context(nc.sbuf_tensor(nm, sh, dt))
    SBx = lambda st, nm, sh, dt: st.enter_context(nc.sbuf_tensor(nm, sh, dt))
    PS = lambda nm, sh, dt=F32: es.enter_context(nc.psum_tensor(nm, sh, dt))
    SEM = lambda nm: es.enter_context(nc.semaphore(nm))

    with es:
        xln8 = SB("xln8", [128, 8, 2, T], E4)
        xin = [SB(f"xin{i}", [128, D], F32) for i in range(2)]
        xl = [SB(f"xl{i}", [128, D], F32) for i in range(2)]
        stat = SB("stat", [128, 16], F32)
        ccol = SB("ccolS", [128, 128], F32)
        acol = SB("acolS", [128, 128], F32)
        onesf = SB("onesfS", [1, 128], F32)
        srow = SB("srowS", [1, 512], F32)
        cst = SB("cstS", [128, 4], F32)
        dm = SB("dmS", [128, 8, 512], BF16)
        identf = SB("identfS", [128, 128], F32)
        identb = SB("identbS", [128, 128], BF16)
        ones1 = SB("ones1S", [128, 2], BF16)
        zero1 = SB("zero1", [128, 1], F32)
        gate16 = [SB(f"gate16_{i}", [128, 512], BF16) for i in range(2)]
        tmp16 = [SB(f"tmp16_{i}", [128, 512], BF16) for i in range(2)]
        rpiece = [SB(f"rpiece{i}", [128, 512], F32) for i in range(2)]
        xres = [SB(f"xres{i}", [128, 512], F32) for i in range(2)]
        p = [PS(f"p{i}", [128, 512]) for i in range(7)]
        pt16 = PS("pt16", [128, 1024], BF16)
        sems = {e: SEM(f"sem_{e}") for e in Prog.ENG}
        dsems = {e: SEM(f"dsem_{e}") for e in Prog.ENG}
        rsem = SEM("rsem")
        block = es.enter_context(nc.Block())

        P = Prog()

        P.op("q", lambda e: e.dma_start(out=ccol[:], in_=ccol_in), writes=["ccol"], dma=True)
        P.op("q", lambda e: e.dma_start(out=dm[:], in_=dmask_in), writes=["dm"], dma=True)
        P.op("q", lambda e: e.dma_start(out=identf[:], in_=ident_in), writes=["identf"], dma=True)
        P.op("q", lambda e: e.dma_start(out=identb[:], in_=identb_in), writes=["identb"], dma=True)
        P.op("q", lambda e: e.dma_start(out=ones1[:], in_=ones_in), writes=["ones1"], dma=True)
        P.op("q", lambda e: e.dma_start(out=onesf[:], in_=onesf_in), writes=["onesf"], dma=True)
        P.op("v", lambda e: e.memset(zero1[:], 0.0), writes=["zero1"])
        P.op("v", lambda e: e.memset(cst[:, 0:1], 1.28e-4), writes=["cst"])
        P.op("v", lambda e: e.memset(cst[:, 1:2], 1e-6), writes=["cst"])
        P.op("v", lambda e: e.memset(cst[:, 2:3], 1e-5), writes=["cst"])

        def ln_tile(xin_t, xin_key, out_t, out_key, par=0, eps=1e-5):
            o = par * 8
            sk = f"stat{par}"
            ssx, smx, mneg, m2n2, rs = (stat[:, o + i:o + i + 1] for i in range(5))
            P.op("s", lambda e: e.activation(
                out_t[:], xin_t[:], AF.Identity, accum_out=smx),
                reads=[xin_key], writes=[out_key, sk])
            P.op("s", lambda e: e.activation(
                out_t[:], xin_t[:], AF.Square, accum_out=ssx),
                reads=[xin_key], writes=[out_key, sk])
            P.op("s", lambda e: e.mul(mneg, smx, -1.0 / D), reads=[sk], writes=[sk])
            P.op("s", lambda e: e.activation(m2n2, smx, AF.Square, scale=1.0 / D),
                reads=[sk], writes=[sk])
            P.op("s", lambda e: e.activation(m2n2, m2n2, AF.Identity,
                                             scale=-1.0, bias=cst[:, 2:3]),
                reads=[sk, "cst"], writes=[sk])
            P.op("s", lambda e: e.activation(rs, ssx, AF.Sqrt, bias=m2n2, scale=1.0 / D),
                reads=[sk], writes=[sk])
            P.op("v", lambda e: e.reciprocal(rs, rs), reads=[sk], writes=[sk])
            P.op("v", lambda e: e.tensor_scalar(out_t[:], xin_t[:], mneg, rs, ALU.add, ALU.mult),
                reads=[xin_key, sk], writes=[out_key])

        def xpose_to_fp8(src_t, src_key, tc):
            for dq in range(4):
                bi = 5 + ((tc * 4 + dq) % 2)
                bank = p[bi]
                bk = f"p{bi}"
                def tp4(e, dq=dq, bank=bank, src_t=src_t):
                    for j in range(4):
                        dc = dq * 4 + j
                        ii = e.transpose(bank[:, j * 128:(j + 1) * 128],
                                         src_t[:, dc * 128:(dc + 1) * 128], identf[:])
                    return ii
                P.op("t", tp4, reads=[src_key, "identf"], writes=[bk])
                def ev4(e, dq=dq, bank=bank, tc=tc):
                    for j in range(4):
                        dc = dq * 4 + j
                        s_, a_ = dc // 2, dc % 2
                        ii = e.copy(xln8[:, s_, a_, tc * 128:(tc + 1) * 128],
                                    bank[:, j * 128:(j + 1) * 128])
                    return ii
                P.op("s", ev4, reads=[bk], writes=["xln8"])

        for tc in range(NT):
            b = tc % 2
            P.op("q", lambda e, tc=tc, b=b: e.dma_start(
                out=xin[b][:], in_=x_in[tc * 128:(tc + 1) * 128, :]),
                writes=[f"xin{b}"], dma=True)
            ln_tile(xin[b], f"xin{b}", xl[b], f"xl{b}", par=b)
            xpose_to_fp8(xl[b], f"xl{b}", tc)
        P.barrier()

        def ffn(base_dram, out_dram, cache_weights, wbuf, act8, wbk, a8k):
            for half in range(2):
                first = cache_weights and half == 0
                toff = half * 1024
                for hb in range(16):
                    wb = wbuf[hb % 2]
                    k = wbk[hb % 2]
                    for g in range(2):
                        dst = wb[:, g * 8:(g + 1) * 8]
                        cache = wg8c[g, hb].rearrange("s p a c -> p s a c")
                        if first:
                            src = wg[g][:, hb * 512:(hb + 1) * 512].rearrange(
                                "(s a p) c -> p s a c", s=8, a=2, p=128)
                            P.op("g", lambda e, src=src, dst=dst: e.dma_start(
                                out=dst, in_=src), writes=[k], dma=True)
                            P.op("g", lambda e, dst=dst, cache=cache: e.dma_start(
                                out=cache, in_=dst), reads=[k], dma=True)
                        else:
                            P.op("g", lambda e, dst=dst, cache=cache: e.dma_start(
                                out=dst, in_=cache), writes=[k], dma=True)
                    for hc in range(4):
                        hca = hb * 4 + hc
                        hs, aa = hca // 2, hca % 2
                        for blk in range(2):
                            t0 = toff + blk * 512
                            rot = (hca * 2 + blk) % 2
                            pg0, pg1 = p[rot * 2], p[rot * 2 + 1]
                            pk0, pk1 = f"p{rot*2}", f"p{rot*2+1}"
                            def mm_g(e, wb=wb, hc=hc, t0=t0, pg0=pg0, pg1=pg1):
                                for g in range(2):
                                    for s_ in range(8):
                                        ii = e.matmul(
                                            (pg0 if g == 0 else pg1)[:],
                                            wb[:, g * 8 + s_, :, hc * 128:(hc + 1) * 128],
                                            xln8[:, s_, :, t0:t0 + 512],
                                            start=(s_ == 0), stop=(s_ == 7),
                                            perf_mode=DR)
                                return ii
                            P.op("t", mm_g, reads=[k, "xln8"], writes=[pk0, pk1])
                            gk = f"gate16_{rot}"
                            P.op("s", lambda e, pg0=pg0, rot=rot: e.activation(
                                gate16[rot][:], pg0[:], AF.Gelu_apprx_tanh),
                                reads=[pk0], writes=[gk])
                            tk = f"tmp16_{rot}"
                            P.op("v", lambda e, pg1=pg1, rot=rot: e.tensor_tensor(
                                tmp16[rot][:], gate16[rot][:], pg1[:], ALU.mult),
                                reads=[gk, pk1], writes=[tk])
                            P.op("s", lambda e, rot=rot, hs=hs, aa=aa, blk=blk: e.copy(
                                act8[:, hs, aa, blk * 512:(blk + 1) * 512],
                                tmp16[rot][:]),
                                reads=[tk], writes=[a8k])
                for dob in range(4):
                    lb0, lb1 = wbuf[(2 * dob) % 3], wbuf[(2 * dob + 1) % 3]
                    k0, k1 = wbk[(2 * dob) % 3], wbk[(2 * dob + 1) % 3]
                    for half16, (lbx, kx) in enumerate(((lb0, k0), (lb1, k1))):
                        dst = lbx[:]
                        cache = wl8c[dob, half16 * 16:(half16 + 1) * 16].rearrange(
                            "hs p a c -> p hs a c")
                        if first:
                            src = wl[half16 * 4096:(half16 + 1) * 4096,
                                     dob * 512:(dob + 1) * 512].rearrange(
                                "(hs a p) c -> p hs a c", hs=16, a=2, p=128)
                            P.op("g", lambda e, src=src, dst=dst: e.dma_start(
                                out=dst, in_=src), writes=[kx], dma=True)
                            P.op("g", lambda e, dst=dst, cache=cache: e.dma_start(
                                out=cache, in_=dst), reads=[kx], dma=True)
                        else:
                            P.op("g", lambda e, dst=dst, cache=cache: e.dma_start(
                                out=dst, in_=cache), writes=[kx], dma=True)
                    for tcl in range(8):
                        tca = half * 8 + tcl
                        rot = tcl % 2
                        po = p[4 + rot]
                        pok = f"p{4+rot}"
                        def mm_l(e, tcl=tcl, po=po, lb0=lb0, lb1=lb1):
                            for hs in range(32):
                                lbx = lb0 if hs < 16 else lb1
                                ii = e.matmul(
                                    po[:], act8[:, hs, :, tcl * 128:(tcl + 1) * 128],
                                    lbx[:, hs % 16], start=(hs == 0), stop=(hs == 31),
                                    perf_mode=DR)
                            return ii
                        P.op("t", mm_l, reads=[k0, k1, a8k], writes=[pok])
                        xrk = f"xres{rot}"
                        P.op("q", lambda e, rot=rot, tca=tca, dob=dob: e.dma_start(
                            out=xres[rot][:], in_=base_dram[
                                tca * 128:(tca + 1) * 128, dob * 512:(dob + 1) * 512]),
                            writes=[xrk], dma=True)
                        rk = f"rpiece{rot}"
                        P.op("v", lambda e, rot=rot, po=po: e.scalar_tensor_tensor(
                            rpiece[rot][:], po[:], 0.5, xres[rot][:], ALU.mult, ALU.add),
                            reads=[pok, xrk], writes=[rk])
                        P.op("q", lambda e, rot=rot, tca=tca, dob=dob: e.dma_start(
                            out=out_dram[tca * 128:(tca + 1) * 128,
                                         dob * 512:(dob + 1) * 512],
                            in_=rpiece[rot][:]), reads=[rk], dma=True)

        es1 = ExitStack()
        act8 = es1.enter_context(nc.sbuf_tensor("act8", [128, 32, 2, 1024], E4))
        wbuf = [es1.enter_context(nc.sbuf_tensor(f"wbuf{i}", [128, 16, 2, 512], E4))
                for i in range(3)]
        ffn(x_in, r1, True, wbuf, act8, ["wbuf0", "wbuf1", "wbuf2"], "act8")
        P.barrier()
        es1.close()

        if debug_stage == 1:
            for tc in range(NT):
                b = tc % 2
                P.op("q", lambda e, tc=tc, b=b: e.dma_start(
                    out=xin[b][:], in_=r1[tc * 128:(tc + 1) * 128, :]),
                    writes=[f"xin{b}"], dma=True)
                P.op("q", lambda e, tc=tc, b=b: e.dma_start(
                    out=y_out[tc * 128:(tc + 1) * 128, :], in_=xin[b][:]),
                    reads=[f"xin{b}"], dma=True)
            P.barrier()
        else:
            _build_attention(P, nc, locals())
            P.barrier()
            es3 = ExitStack()
            act8b = es3.enter_context(nc.sbuf_tensor("act8b", [128, 32, 2, 1024], E4))
            wbufb = [es3.enter_context(nc.sbuf_tensor(f"wbufb{i}", [128, 16, 2, 512], E4))
                     for i in range(3)]
            if debug_stage == 2:
                for tc in range(NT):
                    b = tc % 2
                    P.op("q", lambda e, tc=tc, b=b: e.dma_start(
                        out=xin[b][:], in_=r2[tc * 128:(tc + 1) * 128, :]),
                        writes=[f"xin{b}"], dma=True)
                    P.op("q", lambda e, tc=tc, b=b: e.dma_start(
                        out=y_out[tc * 128:(tc + 1) * 128, :], in_=xin[b][:]),
                        reads=[f"xin{b}"], dma=True)
                P.barrier()
            else:
                ffn(r2, y_out, False, wbufb, act8b, ["wbufb0", "wbufb1", "wbufb2"], "act8b")
                P.barrier()
            es3.close()

        P.emit(block, sems, dsems)
        ndma_q = sum(1 for (eng, _, _, _, dma) in P.ops if eng == "q" and dma)
        ndma_g = sum(1 for (eng, _, _, _, dma) in P.ops if eng == "g" and dma)

        def final_wait(e):
            if ndma_q:
                e.wait_ge(dsems["q"], 16 * ndma_q)
            if ndma_g:
                e.wait_ge(dsems["g"], 16 * ndma_g)
        block.sync(final_wait)

    return nc


def _build_attention(P, nc, env):
    p = env["p"]; pt16 = env["pt16"]
    xin = env["xin"]; xl = env["xl"]; stat = env["stat"]
    xln8 = env["xln8"]; dm = env["dm"]; ccol = env["ccol"]; acol = env["acol"]
    identf = env["identf"]; onesf = env["onesf"]; ones1 = env["ones1"]
    srow = env["srow"]
    rpiece = env["rpiece"]; xres = env["xres"]
    r1 = env["r1"]; r2 = env["r2"]; x2t_d = env["x2t_d"]; qt_d = env["qt_d"]
    cin = env["cin"]; cout = env["cout"]; wqc_d = env["wqc_d"]
    cosr = env["cosr"]; sinr = env["sinr"]
    wq = env["wq"]; wk = env["wk"]; wv = env["wv"]; wo = env["wo"]
    ln_tile = env["ln_tile"]; xpose_to_fp8 = env["xpose_to_fp8"]
    SBx = env["SBx"]; cst = env["cst"]

    esB = ExitStack()
    x2s = [SBx(esB, f"x2s{i}", [128, DC, 128], BF16) for i in range(2)]
    x2tile = [SBx(esB, f"x2tile{i}", [128, DC, 128], BF16) for i in range(2)]
    wpro = [SBx(esB, f"wpro{i}", [128, 16, 512], BF16) for i in range(2)]
    cost = SBx(esB, "cost", [128, 256], F32)
    sint = SBx(esB, "sint", [128, 256], F32)
    pjf = [SBx(esB, f"pjf{i}", [128, 512], F32) for i in range(2)]
    rsc = [SBx(esB, f"rsc{i}", [128, 512], F32) for i in range(2)]
    tt1 = [SBx(esB, f"tt1_{i}", [128, 4, 64], F32) for i in range(2)]
    tt2 = [SBx(esB, f"tt2_{i}", [128, 4, 64], F32) for i in range(2)]
    ror = [SBx(esB, f"ror{i}", [128, 512], F32) for i in range(2)]
    ror2 = [SBx(esB, f"ror2_{i}", [128, 512], F32) for i in range(2)]
    stg = [SBx(esB, f"stg{i}", [128, 4, 128], BF16) for i in range(2)]
    cost2 = [SBx(esB, f"cost2_{i}", [128, 256], F32) for i in range(2)]
    sint2 = [SBx(esB, f"sint2_{i}", [128, 256], F32) for i in range(2)]
    aco = SBx(esB, "aco", [128, 4, 16], F32)
    wqb = SBx(esB, "wqb", [128, 16, 512], BF16)
    ahi = SBx(esB, "ahi", [128, 64], BF16)
    alo32 = SBx(esB, "alo32", [128, 64], F32)
    alo = SBx(esB, "alo", [128, 64], BF16)

    def load_cossin(tc):
        P.op("q", lambda e, tc=tc: e.dma_start(
            out=cost[:], in_=cosr[tc * 128:(tc + 1) * 128, :]),
            writes=["cost"], dma=True)
        P.op("q", lambda e, tc=tc: e.dma_start(
            out=sint[:], in_=sinr[tc * 128:(tc + 1) * 128, :]),
            writes=["sint"], dma=True)

    def rope(src, src_key, dst, dst_key):
        sh = src[:].rearrange("p (h d) -> p h d", h=4)
        dh = dst[:].rearrange("p (h d) -> p h d", h=4)
        ch = cost[:].rearrange("p (h d) -> p h d", h=4)
        sn = sint[:].rearrange("p (h d) -> p h d", h=4)
        P.op("v", lambda e: e.tensor_mul(tt1[:], sh[:, :, 0:64], ch[:]),
            reads=[src_key, "cost"], writes=["tt1"])
        P.op("v", lambda e: e.tensor_mul(tt2[:], sh[:, :, 64:128], sn[:]),
            reads=[src_key, "sint"], writes=["tt2"])
        P.op("v", lambda e: e.tensor_add(dh[:, :, 0:64], tt1[:], tt2[:]),
            reads=["tt1", "tt2"], writes=[dst_key])
        P.op("v", lambda e: e.tensor_mul(tt1[:], sh[:, :, 64:128], ch[:]),
            reads=[src_key, "cost"], writes=["tt1"])
        P.op("v", lambda e: e.tensor_mul(tt2[:], sh[:, :, 0:64], sn[:]),
            reads=[src_key, "sint"], writes=["tt2"])
        P.op("v", lambda e: e.tensor_sub(dh[:, :, 64:128], tt1[:], tt2[:]),
            reads=["tt1", "tt2"], writes=[dst_key])

    def sumsq4(src_key):
        for h4 in range(4):
            P.op("s", lambda e, h4=h4: e.activation(
                rsc[:, h4 * 128:(h4 + 1) * 128], pjf[:, h4 * 128:(h4 + 1) * 128],
                AF.Square, accum_out=stat[:, h4:h4 + 1]),
                reads=[src_key], writes=["rsc", "stat"])

    for tc in range(NT):
        b = tc % 2
        P.op("q", lambda e, tc=tc, b=b: e.dma_start(
            out=xin[b][:], in_=r1[tc * 128:(tc + 1) * 128, :]),
            writes=[f"xin{b}"], dma=True)
        ln_tile(xin[b], f"xin{b}", xl[b], f"xl{b}", par=b)
        for dq in range(4):
            bi = 5 + ((tc * 4 + dq) % 2)
            bk = f"p{bi}"
            def tp4(e, dq=dq, bi=bi, b=b):
                for j in range(4):
                    dc = dq * 4 + j
                    ii = e.transpose(p[bi][:, j * 128:(j + 1) * 128],
                                     xl[b][:, dc * 128:(dc + 1) * 128], identf[:])
                return ii
            P.op("t", tp4, reads=[f"xl{b}", "identf"], writes=[bk])
            def ev4(e, dq=dq, bi=bi, b=b):
                for j in range(4):
                    dc = dq * 4 + j
                    ii = e.copy(x2s[b][:, dc, :], p[bi][:, j * 128:(j + 1) * 128])
                return ii
            P.op("s", ev4, reads=[bk], writes=[f"x2s{b}"])
        P.op("q", lambda e, tc=tc, b=b: e.dma_start(
            out=x2t_d[tc].rearrange("dc p c -> p dc c"), in_=x2s[b][:]),
            reads=[f"x2s{b}"], writes=["x2td"], dma=True)
    P.barrier()

    P.op("g", lambda e: e.dma_start(
        out=wpro[0][:], in_=wk.rearrange("(vc p) c -> p vc c", p=128)),
        writes=["wpro0"], dma=True)
    P.op("g", lambda e: e.dma_start(
        out=wpro[1][:], in_=wv.rearrange("(vc p) c -> p vc c", p=128)),
        writes=["wpro1"], dma=True)

    def load_cossin2(tc, b):
        P.op("q", lambda e, tc=tc, b=b: e.dma_start(
            out=cost2[b][:], in_=cosr[tc * 128:(tc + 1) * 128, :]),
            writes=[f"cost2_{b}"], dma=True)
        P.op("q", lambda e, tc=tc, b=b: e.dma_start(
            out=sint2[b][:], in_=sinr[tc * 128:(tc + 1) * 128, :]),
            writes=[f"sint2_{b}"], dma=True)

    def rope2(b, src, src_key, dst, dst_key):
        sh = src[:].rearrange("p (h d) -> p h d", h=4)
        dh = dst[:].rearrange("p (h d) -> p h d", h=4)
        ch = cost2[b][:].rearrange("p (h d) -> p h d", h=4)
        sn = sint2[b][:].rearrange("p (h d) -> p h d", h=4)
        ck, sk2 = f"cost2_{b}", f"sint2_{b}"
        t1k, t2k = f"tt1_{b}", f"tt2_{b}"
        P.op("v", lambda e: e.tensor_mul(tt1[b][:], sh[:, :, 0:64], ch[:]),
            reads=[src_key, ck], writes=[t1k])
        P.op("v", lambda e: e.tensor_mul(tt2[b][:], sh[:, :, 64:128], sn[:]),
            reads=[src_key, sk2], writes=[t2k])
        P.op("v", lambda e: e.tensor_add(dh[:, :, 0:64], tt1[b][:], tt2[b][:]),
            reads=[t1k, t2k], writes=[dst_key])
        P.op("v", lambda e: e.tensor_mul(tt1[b][:], sh[:, :, 64:128], ch[:]),
            reads=[src_key, ck], writes=[t1k])
        P.op("v", lambda e: e.tensor_mul(tt2[b][:], sh[:, :, 0:64], sn[:]),
            reads=[src_key, sk2], writes=[t2k])
        P.op("v", lambda e: e.tensor_sub(dh[:, :, 64:128], tt1[b][:], tt2[b][:]),
            reads=[t1k, t2k], writes=[dst_key])

    def sumsq4b(b, src_key, outslice, outkey):
        for h4 in range(4):
            P.op("s", lambda e, h4=h4, b=b: e.activation(
                rsc[b][:, h4 * 128:(h4 + 1) * 128], pjf[b][:, h4 * 128:(h4 + 1) * 128],
                AF.Square, accum_out=stat[:, b * 8 + h4:b * 8 + h4 + 1]),
                reads=[src_key], writes=[f"rsc{b}", f"stat{b}"])

    for tc in range(NT):
        if tc % 4 == 2:
            qdb = tc // 4
            wslot = qdb % 2
            P.op("g", lambda e, qdb=qdb: e.dma_start(
                out=wqb[:], in_=wq[:, qdb * 512:(qdb + 1) * 512].rearrange(
                    "(vc p) c -> p vc c", p=128)),
                writes=["wqb"], dma=True)
            P.op("g", lambda e, qdb=qdb: e.dma_start(
                out=wqc_d[qdb], in_=wqb[:]),
                reads=["wqb"], writes=[f"wqc{qdb}"], dma=True)
        b = tc % 2
        xk = f"x2tile{b}"
        pjk = f"pjf{b}"
        rk = f"ror{b}"
        sgk = f"stg{b}"
        pK, pKk = (p[0], "p0") if b == 0 else (p[2], "p2")
        pV, pVk = (p[1], "p1") if b == 0 else (p[3], "p3")
        pT = 5 + b
        pTk = f"p{pT}"
        P.op("g", lambda e, tc=tc, b=b: e.dma_start(
            out=x2tile[b][:], in_=x2t_d[tc].rearrange("dc p c -> p dc c")),
            reads=["x2td"], writes=[xk], dma=True)
        load_cossin2(tc, b)
        def mmk(e, b=b, pK=pK):
            for dc in range(DC):
                ii = e.matmul(pK[:], x2tile[b][:, dc, :], wpro[0][:, dc, :],
                              start=(dc == 0), stop=(dc == DC - 1))
            return ii
        P.op("t", mmk, reads=[xk, "wpro0"], writes=[pKk])
        P.op("s", lambda e, b=b, pK=pK: e.copy(pjf[b][:], pK[:]),
            reads=[pKk], writes=[pjk])
        def mmv(e, b=b, pV=pV):
            for dc in range(DC):
                ii = e.matmul(pV[:], x2tile[b][:, dc, :], wpro[1][:, dc, :],
                              start=(dc == 0), stop=(dc == DC - 1))
            return ii
        P.op("t", mmv, reads=[xk, "wpro1"], writes=[pVk])
        P.op("s", lambda e, b=b, pV=pV: e.copy(
            stg[b][:].rearrange("p h c -> p (h c)"), pV[:]),
            reads=[pVk], writes=[sgk])
        P.op("q", lambda e, tc=tc, b=b: e.dma_start(
            out=cin[512 + tc * 32:512 + (tc + 1) * 32, :].rearrange(
                "a (bb c) -> (a bb) c", bb=4),
            in_=stg[b][:].rearrange("p h c -> p (h c)")), reads=[sgk], writes=["cin"], dma=True)
        sumsq4b(b, pjk, None, None)
        P.op("s", lambda e, tc=tc, b=b: e.activation(
            aco[:, :, tc:tc + 1], stat[:, b * 8:b * 8 + 4], AF.Sqrt, bias=cst[:, 0:1]),
            reads=[f"stat{b}", "cst"], writes=["aco"])
        P.op("v", lambda e, tc=tc: e.reciprocal(aco[:, :, tc:tc + 1], aco[:, :, tc:tc + 1]),
            reads=["aco"], writes=["aco"])
        rope2(b, pjf[b], pjk, ror[b], rk)
        def tpk(e, b=b, pT=pT):
            for j in range(4):
                ii = e.transpose(p[pT][:, j * 128:(j + 1) * 128],
                                 ror[b][:, j * 128:(j + 1) * 128], identf[:])
            return ii
        P.op("t", tpk, reads=[rk, "identf"], writes=[pTk])
        def evk(e, b=b, pT=pT):
            for j in range(4):
                ii = e.copy(stg[b][:, j, :], p[pT][:, j * 128:(j + 1) * 128])
            return ii
        P.op("s", evk, reads=[pTk], writes=[sgk])
        P.op("q", lambda e, tc=tc, b=b: e.dma_start(
            out=cin[0:512, tc * 128:(tc + 1) * 128].rearrange(
                "(kv p) c -> p kv c", p=128),
            in_=stg[b][:]), reads=[sgk], writes=["cin"], dma=True)
    P.op("s", lambda e: e.copy(ahi[:], aco[:].rearrange("p a bb -> p (a bb)")),
        reads=["aco"], writes=["ahi"])
    P.op("v", lambda e: e.tensor_sub(alo32[:], aco[:].rearrange("p a bb -> p (a bb)"), ahi[:]),
        reads=["aco", "ahi"], writes=["alo32"])
    P.op("s", lambda e: e.copy(alo[:], alo32[:]), reads=["alo32"], writes=["alo"])
    P.op("q", lambda e: e.dma_start(
        out=cin[1024:1028, :].rearrange("r (pp f) -> (r pp) f", pp=32),
        in_=ahi[:]), reads=["ahi"], writes=["cin"], dma=True)
    P.op("q", lambda e: e.dma_start(
        out=cin[1028:1032, :].rearrange("r (pp f) -> (r pp) f", pp=32),
        in_=alo[:]), reads=["alo"], writes=["cin"], dma=True)

    groups = [[0, 1], [2, 3], [4, 5], [6, 7]]
    P.op("g", lambda e: e.collective_compute(
        "AllGather", ALU.bypass, groups, [cin], [cout]),
        reads=["cin"], writes=["cout"])

    for qdb in range(4):
        wslot = qdb % 2
        wpk = f"wpro{wslot}"
        P.op("q", lambda e, qdb=qdb, wslot=wslot: e.dma_start(
            out=wpro[wslot][:], in_=wqc_d[qdb]),
            reads=[f"wqc{qdb}"], writes=[wpk], dma=True)
        for tc in range(NT):
            b = tc % 2
            xk = f"x2tile{b}"
            pjk = f"pjf{b}"
            rk = f"ror{b}"
            r2k = f"ror2_{b}"
            sgk = f"stg{b}"
            pQ, pQk = (p[0], "p0") if b == 0 else (p[2], "p2")
            pT = 5 + b
            pTk = f"p{pT}"
            P.op("q", lambda e, tc=tc, b=b: e.dma_start(
                out=x2tile[b][:], in_=x2t_d[tc].rearrange("dc p c -> p dc c")),
                reads=["x2td"], writes=[xk], dma=True)
            load_cossin2(tc, b)
            def mmq(e, b=b, pQ=pQ, wslot=wslot):
                for dc in range(DC):
                    ii = e.matmul(pQ[:], x2tile[b][:, dc, :], wpro[wslot][:, dc, :],
                                  start=(dc == 0), stop=(dc == DC - 1))
                return ii
            P.op("t", mmq, reads=[xk, wpk], writes=[pQk])
            P.op("s", lambda e, b=b, pQ=pQ: e.copy(pjf[b][:], pQ[:]),
                reads=[pQk], writes=[pjk])
            sumsq4b(b, pjk, None, None)
            P.op("s", lambda e, b=b: e.activation(
                stat[:, b * 8 + 4:b * 8 + 8], stat[:, b * 8:b * 8 + 4], AF.Sqrt,
                bias=cst[:, 1:2], scale=1.0 / HD),
                reads=[f"stat{b}", "cst"], writes=[f"stat{b}"])
            P.op("v", lambda e, b=b: e.reciprocal(
                stat[:, b * 8 + 4:b * 8 + 8], stat[:, b * 8 + 4:b * 8 + 8]),
                reads=[f"stat{b}"], writes=[f"stat{b}"])
            rope2(b, pjf[b], pjk, ror[b], rk)
            def bq4(e, b=b):
                for h4 in range(4):
                    ii = e.tensor_scalar(
                        ror2[b][:, h4 * 128:(h4 + 1) * 128],
                        ror[b][:, h4 * 128:(h4 + 1) * 128],
                        stat[:, b * 8 + 4 + h4:b * 8 + 5 + h4], None, ALU.mult)
                return ii
            P.op("v", bq4, reads=[rk, f"stat{b}"], writes=[r2k])
            def tpq(e, b=b, pT=pT):
                for j in range(4):
                    ii = e.transpose(p[pT][:, j * 128:(j + 1) * 128],
                                     ror2[b][:, j * 128:(j + 1) * 128], identf[:])
                return ii
            P.op("t", tpq, reads=[r2k, "identf"], writes=[pTk])
            def evq(e, b=b, pT=pT):
                for j in range(4):
                    ii = e.copy(stg[b][:, j, :], p[pT][:, j * 128:(j + 1) * 128])
                return ii
            P.op("s", evq, reads=[pTk], writes=[sgk])
            P.op("q", lambda e, tc=tc, qdb=qdb, b=b: e.dma_start(
                out=qt_d[tc, qdb * 4:(qdb + 1) * 4].rearrange("q p c -> p q c"),
                in_=stg[b][:]), reads=[sgk], writes=["qtd"], dma=True)
    P.barrier()
    esB.close()

    esC = ExitStack()
    vt = SBx(esC, "vt", [128, 32, 512], BF16)
    ktbuf = [SBx(esC, f"ktbuf{i}", [128, 4096], BF16) for i in range(2)]
    qtbuf = [SBx(esC, f"qtbuf{i}", [128, 4, 512], BF16) for i in range(2)]
    et = [SBx(esC, f"et{i}", [128, 512], BF16) for i in range(4)]
    s2f = [SBx(esC, f"s2f{i}", [128, 1024], F32) for i in range(2)]
    yt = SBx(esC, "yt", [128, 16, 512], BF16)
    rb = [SBx(esC, f"rb{i}", [128, 512], F32) for i in range(2)]
    wo16 = SBx(esC, "wo16", [128, 16, 512], BF16)
    ahl = [SBx(esC, f"ahl{i}", [128, 64], F32) for i in range(2)]
    esum = [SBx(esC, f"esum{i}", [128, 512], F32) for i in range(2)]
    onesfb = SBx(esC, "onesfbS", [128, 1], F32)

    P.op("v", lambda e: e.memset(onesfb[:], 1.0), writes=["onesfb"])
    for kc in range(32):
        slot, lt = kc // 16, kc % 16
        P.op("g", lambda e, kc=kc, slot=slot, lt=lt: e.dma_start(
            out=vt[:, kc, :],
            in_=cout[slot, 512 + lt * 32:512 + (lt + 1) * 32, :].rearrange(
                "a (b c) -> (a b) c", b=4)),
            reads=["cout"], writes=["vt"], dma=True)
    for slot in range(2):
        P.op("g", lambda e, slot=slot: e.dma_start(
            out=ahl[0][:], in_=cout[slot, 1024:1028, :].rearrange(
                "r (pp f) -> (r pp) f", pp=32)),
            reads=["cout"], writes=["ahl0"], dma=True)
        P.op("g", lambda e, slot=slot: e.dma_start(
            out=ahl[1][:], in_=cout[slot, 1028:1032, :].rearrange(
                "r (pp f) -> (r pp) f", pp=32)),
            reads=["cout"], writes=["ahl1"], dma=True)
        P.op("v", lambda e, slot=slot: e.tensor_add(
            acol[:].rearrange("p (a b) -> p a b", a=4)[:, :, slot * 16:(slot + 1) * 16],
            ahl[0][:].rearrange("p (a b) -> p a b", a=4),
            ahl[1][:].rearrange("p (a b) -> p a b", a=4)),
            reads=["ahl0", "ahl1"], writes=["acol"])

    for qb in range(4):
        for kv in range(4):
            cmb = qb * 4 + kv
            rot = cmb % 2
            kk = f"ktbuf{rot}"
            qk = f"qtbuf{rot}"
            for slot in range(2):
                P.op("g", lambda e, kv=kv, slot=slot, rot=rot: e.dma_start(
                    out=ktbuf[rot][:, slot * 2048:(slot + 1) * 2048],
                    in_=cout[slot, kv * 128:(kv + 1) * 128, :]),
                    reads=["cout"], writes=[kk], dma=True)
            for tq in range(4):
                P.op("g", lambda e, qb=qb, kv=kv, rot=rot, tq=tq: e.dma_start(
                    out=qtbuf[rot][:, :, tq * 128:(tq + 1) * 128],
                    in_=qt_d[qb * 4 + tq, kv * 4:(kv + 1) * 4].rearrange(
                        "q p c -> p q c")),
                    reads=["qtd"], writes=[qk], dma=True)
            nkc = min(32, 4 * qb + 20)
            sbanks = (1, 2, 4, 5, 6)
            LAG = 3
            for qh in range(4):
                py = p[0] if qh % 2 == 0 else p[3]
                pyk = "p0" if qh % 2 == 0 else "p3"
                esk = f"esum{qh % 2}"
                esb = esum[qh % 2]
                def emit_scores(e_kc, rot=rot, qh=qh):
                    bi = sbanks[e_kc % 5]
                    ps, psk = p[bi], f"p{bi}"
                    P.op("t", lambda e, ps=ps, rot=rot, kc=e_kc, qh=qh: e.matmul(
                        ps[:], ktbuf[rot][:, kc * 128:(kc + 1) * 128],
                        qtbuf[rot][:, qh, :], start=True, stop=True),
                        reads=[kk, qk], writes=[psk])
                def emit_tail(kc, rot=rot, qh=qh, qb=qb, kv=kv):
                    bi = sbanks[kc % 5]
                    ps, psk = p[bi], f"p{bi}"
                    ee = kc % 4
                    ek = f"et{ee}"
                    j = kc - 4 * qb
                    jj = kc - 16 - 4 * qb
                    dslot = j if 0 <= j < 4 else (4 + jj if 0 <= jj < 4 else None)
                    if dslot is not None:
                        sk = f"s2f{ee % 2}"
                        P.op("v", lambda e, ps=ps, ee=ee, dslot=dslot: e.tensor_add(
                            s2f[ee % 2][:, (ee // 2) * 512:(ee // 2 + 1) * 512],
                            ps[:], dm[:, dslot, :]),
                            reads=[psk, "dm"], writes=[sk])
                        src = s2f[ee % 2][:, (ee // 2) * 512:(ee // 2 + 1) * 512]
                        srck = sk
                    else:
                        src, srck = ps[:], psk
                    P.op("s", lambda e, src=src, ee=ee, kv=kv, kc=kc, qb=qb: e.activation(
                        et[ee][:], src, AF.Exp,
                        bias=ccol[:, qb * 32 + kc:qb * 32 + kc + 1],
                        scale=acol[:].rearrange("p (a bb) -> p a bb", a=4)[:, kv, kc:kc + 1]),
                        reads=[srck, "ccol", "acol"], writes=[ek])
                    if kc == 0:
                        P.op("v", lambda e, ee=ee: e.tensor_copy(esb[:], et[ee][:]),
                            reads=[ek], writes=[esk])
                    else:
                        P.op("v", lambda e, ee=ee: e.tensor_add(esb[:], esb[:], et[ee][:]),
                            reads=[ek, esk], writes=[esk])
                    P.op("t", lambda e, ee=ee, kc=kc, kv=kv, py=py: e.matmul(
                        py[:], vt[:, kc, kv * 128:(kv + 1) * 128],
                        et[ee][:], start=(kc == 0), stop=(kc == nkc - 1)),
                        reads=[ek, "vt"], writes=[pyk])
                for i in range(nkc + LAG):
                    if i < nkc:
                        emit_scores(i)
                    if i >= LAG:
                        emit_tail(i - LAG)
                rr = qh % 2
                srow_r = 32 * (qh % 3)
                P.op("t", lambda e, srow_r=srow_r: e.matmul(
                    p[4][srow_r:srow_r + 1, :], onesfb[:], esb[:],
                    start=True, stop=True),
                    reads=[esk, "onesfb"], writes=["p4"])
                P.op("s", lambda e, srow_r=srow_r: e.copy(srow[:], p[4][srow_r:srow_r + 1, :]),
                    reads=["p4"], writes=["srow"])
                P.op("v", lambda e: e.reciprocal(srow[:], srow[:]),
                    reads=["srow"], writes=["srow"])
                P.op("t", lambda e: e.matmul(p[2][:], onesf[:], srow[:],
                                             start=True, stop=True),
                    reads=["srow", "onesf"], writes=["p2"])
                P.op("s", lambda e, rr=rr: e.copy(rb[rr][:], p[2][:]),
                    reads=["p2"], writes=[f"rb{rr}"])
                P.op("v", lambda e, qh=qh, kv=kv, rr=rr, py=py, pyk=pyk: e.tensor_mul(
                    yt[:, kv * 4 + qh, :], py[:], rb[rr][:]),
                    reads=[pyk, f"rb{rr}"], writes=["yt"])
        for dob in range(4):
            P.op("g", lambda e, dob=dob: e.dma_start(
                out=wo16[:], in_=wo[:, dob * 512:(dob + 1) * 512].rearrange(
                    "(vc p) c -> p vc c", p=128)),
                writes=["wo16"], dma=True)
            for tcq in range(4):
                po = p[tcq]
                pok = f"p{tcq}"
                def mm_wo(e, tcq=tcq, po=po):
                    for vc in range(16):
                        ii = e.matmul(po[:], yt[:, vc, tcq * 128:(tcq + 1) * 128],
                                      wo16[:, vc, :], start=(vc == 0), stop=(vc == 15))
                    return ii
                P.op("t", mm_wo, reads=["yt", "wo16"], writes=[pok])
                rr = tcq % 2
                P.op("q", lambda e, qb=qb, tcq=tcq, dob=dob, rr=rr: e.dma_start(
                    out=xres[rr][:, 0:512], in_=r1[
                        qb * 512 + tcq * 128:qb * 512 + (tcq + 1) * 128,
                        dob * 512:(dob + 1) * 512]),
                    writes=[f"xres{rr}"], dma=True)
                P.op("v", lambda e, po=po, rr=rr: e.tensor_add(
                    rpiece[rr][:], po[:], xres[rr][:, 0:512]),
                    reads=[pok, f"xres{rr}"], writes=[f"rpiece{rr}"])
                P.op("q", lambda e, qb=qb, tcq=tcq, dob=dob, rr=rr: e.dma_start(
                    out=r2[qb * 512 + tcq * 128:qb * 512 + (tcq + 1) * 128,
                           dob * 512:(dob + 1) * 512],
                    in_=rpiece[rr][:]), reads=[f"rpiece{rr}"], dma=True)
    P.barrier()
    esC.close()

    for tc in range(NT):
        b = tc % 2
        P.op("q", lambda e, tc=tc, b=b: e.dma_start(
            out=xin[b][:], in_=r2[tc * 128:(tc + 1) * 128, :]),
            writes=[f"xin{b}"], dma=True)
        ln_tile(xin[b], f"xin{b}", xl[b], f"xl{b}", par=b)
        xpose_to_fp8(xl[b], f"xl{b}", tc)


def _host_inputs(core, full):
    b, h = core // 2, core % 2
    row0 = h * T
    x = np.ascontiguousarray(full["L0_attn_x"][b, row0:row0 + T, :])

    half = 64
    inv = 1.0 / (100000.0 ** (np.arange(half, dtype=np.float64) / half))
    pos = np.arange(row0, row0 + T, dtype=np.float64)[:, None] * inv[None, :]
    cos1 = np.cos(pos).astype(np.float32)
    sin1 = np.sin(pos).astype(np.float32)
    cosr = np.tile(cos1, (1, 4))
    sinr = np.tile(sin1, (1, 4))

    ccol = np.zeros((128, 4, 32), np.float32)
    for qb in range(4):
        for kc in range(32):
            if kc >= row0 // 128 + 4 * qb + 4:
                ccol[:, qb, kc] = -30000.0
    ccol = ccol.reshape(128, 128)
    dmask = np.zeros((128, 8, 512), ml_dtypes.bfloat16)
    r = np.arange(128)[:, None]
    c = np.arange(512)[None, :]
    for j in range(4):
        pat = np.where(j * 128 + r > c, -1e6, 0.0).astype(ml_dtypes.bfloat16)
        if h == 0:
            dmask[:, j, :] = pat
        dmask[:, 4 + j, :] = pat

    return {
        "x": x,
        "wq": full["L0_attn_wq"], "wk": full["L0_attn_wk"],
        "wv": full["L0_attn_wv"], "wo": full["L0_attn_wo"],
        "wg": full["L0_ffn_w_gating"], "wl": full["L0_ffn_w_linear"],
        "cosr": cosr, "sinr": sinr, "ccol": ccol, "dmask": np.asarray(dmask),
        "identf": np.eye(128, dtype=np.float32),
        "identb": np.asarray(np.eye(128, dtype=ml_dtypes.bfloat16)),
        "ones1": np.asarray(np.ones((128, 2), ml_dtypes.bfloat16)),
        "onesf": np.ones((1, 128), np.float32),
    }


def kernel(**inputs):
    nc = build_program(debug_stage=DEBUG_STAGE)
    in_maps = [_host_inputs(c, inputs) for c in range(8)]
    res = run_bass_kernel_spmd(nc, in_maps, core_ids=list(range(8)))
    out = np.empty((4, 4096, 2048), np.float32)
    for c in range(8):
        b, h = c // 2, c % 2
        out[b, h * T:(h + 1) * T, :] = res.results[c]["y"]
    return out


# version 12
# speedup vs baseline: 1.2904x; 1.0161x over previous
import numpy as np
import ml_dtypes
from contextlib import ExitStack
import concourse.bass as bass
import concourse.bass_isa as bass_isa
import concourse.mybir as mybir
from concourse.bass_utils import run_bass_kernel_spmd

F32 = mybir.dt.float32
BF16 = mybir.dt.bfloat16
E4 = mybir.dt.float8e4
AF = mybir.ActivationFunctionType
ALU = mybir.AluOpType
DR = mybir.MatmulPerfMode.DoubleRow
AXX = mybir.AxisListType.X

T = 2048
D = 2048
HID = 8192
NH, NKV, HD = 16, 4, 128
NT = T // 128
DC = D // 128
KTOT = 4096

DEBUG_STAGE = 0


class Prog:
    ENG = ("t", "v", "s", "g", "q")

    def __init__(self):
        self.ops = []
        self.barriers = []

    def op(self, eng, emit, reads=(), writes=(), dma=False):
        self.ops.append((eng, emit, tuple(reads), tuple(writes), dma))

    def barrier(self):
        self.barriers.append(len(self.ops))

    def emit(self, block, sems, dsems):
        n = len(self.ops)
        comp = [None] * n
        cnt = {e: 0 for e in self.ENG}
        dcnt = {e: 0 for e in self.ENG}
        for i, (eng, emit, reads, writes, dma) in enumerate(self.ops):
            if dma:
                dcnt[eng] += 1
                comp[i] = ("d", eng, 16 * dcnt[eng])
            else:
                cnt[eng] += 1
                comp[i] = ("c", eng, cnt[eng])
        last_writer = {}
        readers = {}
        waits = [[] for _ in range(n)]
        bar_idx = 0
        barrier_tokens = []
        run_c = {e: 0 for e in self.ENG}
        run_d = {e: 0 for e in self.ENG}
        for i, (eng, emit, reads, writes, dma) in enumerate(self.ops):
            while bar_idx < len(self.barriers) and self.barriers[bar_idx] == i:
                toks = []
                for e2 in self.ENG:
                    if run_c[e2]:
                        toks.append(("c", e2, run_c[e2]))
                    if run_d[e2]:
                        toks.append(("d", e2, run_d[e2]))
                barrier_tokens = toks
                last_writer = {}
                readers = {}
                bar_idx += 1
            w = list(barrier_tokens)
            for k in reads:
                if k in last_writer:
                    w.append(comp[last_writer[k]])
            for k in writes:
                if k in last_writer:
                    w.append(comp[last_writer[k]])
                for r in readers.get(k, ()):
                    w.append(comp[r])
            waits[i] = w
            for k in writes:
                last_writer[k] = i
                readers[k] = []
            for k in reads:
                readers.setdefault(k, []).append(i)
            kk, ee, vv = comp[i]
            (run_c if kk == "c" else run_d)[ee] = vv

        def body_for(ename):
            def body(e):
                water = {}
                for i, (eng, emit, reads, writes, dma) in enumerate(self.ops):
                    if eng != ename:
                        continue
                    for (k, e2, v) in sorted(set(waits[i])):
                        sem = sems[e2] if k == "c" else dsems[e2]
                        key = (k, e2)
                        if water.get(key, -1) >= v:
                            continue
                        e.wait_ge(sem, v)
                        water[key] = v
                    ins = emit(e)
                    if dma:
                        ins.then_inc(dsems[ename], 16)
                    else:
                        ins.then_inc(sems[ename], 1)
            return body
        block.tensor(body_for("t"))
        block.vector(body_for("v"))
        block.scalar(body_for("s"))
        block.gpsimd(body_for("g"))
        block.sync(body_for("q"))


def build_program(debug_stage=0):
    nc = bass.Bass("TRN2", target_bir_lowering=False, debug=False)
    x_in = nc.dram_tensor("x", [T, D], F32, kind="ExternalInput").ap()
    wq = nc.dram_tensor("wq", [D, D], F32, kind="ExternalInput").ap()
    wk = nc.dram_tensor("wk", [D, 512], F32, kind="ExternalInput").ap()
    wv = nc.dram_tensor("wv", [D, 512], F32, kind="ExternalInput").ap()
    wo = nc.dram_tensor("wo", [D, D], F32, kind="ExternalInput").ap()
    wg = nc.dram_tensor("wg", [2, D, HID], F32, kind="ExternalInput").ap()
    wl = nc.dram_tensor("wl", [HID, D], F32, kind="ExternalInput").ap()
    cosr = nc.dram_tensor("cosr", [T, 256], F32, kind="ExternalInput").ap()
    sinr = nc.dram_tensor("sinr", [T, 256], F32, kind="ExternalInput").ap()
    ccol_in = nc.dram_tensor("ccol", [128, 128], F32, kind="ExternalInput").ap()
    dmask_in = nc.dram_tensor("dmask", [128, 8, 512], BF16, kind="ExternalInput").ap()
    ident_in = nc.dram_tensor("identf", [128, 128], F32, kind="ExternalInput").ap()
    identb_in = nc.dram_tensor("identb", [128, 128], BF16, kind="ExternalInput").ap()
    ones_in = nc.dram_tensor("ones1", [128, 2], BF16, kind="ExternalInput").ap()
    onesf_in = nc.dram_tensor("onesf", [1, 128], F32, kind="ExternalInput").ap()

    y_out = nc.dram_tensor("y", [T, D], F32, kind="ExternalOutput").ap()

    r1 = nc.dram_tensor("r1", [T, D], F32, kind="Internal").ap()
    r2 = nc.dram_tensor("r2", [T, D], F32, kind="Internal").ap()
    x2t_d = nc.dram_tensor("x2t", [NT, DC, 128, 128], BF16, kind="Internal").ap()
    qt_d = nc.dram_tensor("qt_d", [NT, NH, 128, 128], BF16, kind="Internal").ap()
    wg8c = nc.dram_tensor("wg8c", [2, 16, 8, 128, 2, 512], E4, kind="Internal").ap()
    wl8c = nc.dram_tensor("wl8c", [4, 32, 128, 2, 512], E4, kind="Internal").ap()
    cin = nc.dram_tensor("cin", [1032, 2048], BF16, kind="Internal").ap()
    wqc_d = nc.dram_tensor("wqc", [4, 128, 16, 512], BF16, kind="Internal").ap()
    cout = nc.dram_tensor("cout", [2, 1032, 2048], BF16, kind="Internal").ap()

    es = ExitStack()
    SB = lambda nm, sh, dt: es.enter_context(nc.sbuf_tensor(nm, sh, dt))
    SBx = lambda st, nm, sh, dt: st.enter_context(nc.sbuf_tensor(nm, sh, dt))
    PS = lambda nm, sh, dt=F32: es.enter_context(nc.psum_tensor(nm, sh, dt))
    SEM = lambda nm: es.enter_context(nc.semaphore(nm))

    with es:
        xln8 = SB("xln8", [128, 8, 2, T], E4)
        xin = [SB(f"xin{i}", [128, D], F32) for i in range(2)]
        xl = [SB(f"xl{i}", [128, D], F32) for i in range(2)]
        stat = SB("stat", [128, 16], F32)
        ccol = SB("ccolS", [128, 128], F32)
        acol = SB("acolS", [128, 128], F32)
        onesf = SB("onesfS", [1, 128], F32)
        srow = SB("srowS", [1, 512], F32)
        cst = SB("cstS", [128, 4], F32)
        dm = SB("dmS", [128, 8, 512], BF16)
        identf = SB("identfS", [128, 128], F32)
        identb = SB("identbS", [128, 128], BF16)
        ones1 = SB("ones1S", [128, 2], BF16)
        zero1 = SB("zero1", [128, 1], F32)
        gate16 = [SB(f"gate16_{i}", [128, 512], BF16) for i in range(2)]
        tmp16 = [SB(f"tmp16_{i}", [128, 512], BF16) for i in range(2)]
        rpiece = [SB(f"rpiece{i}", [128, 512], F32) for i in range(2)]
        xres = [SB(f"xres{i}", [128, 512], F32) for i in range(2)]
        p = [PS(f"p{i}", [128, 512]) for i in range(7)]
        pt16 = PS("pt16", [128, 1024], BF16)
        sems = {e: SEM(f"sem_{e}") for e in Prog.ENG}
        dsems = {e: SEM(f"dsem_{e}") for e in Prog.ENG}
        rsem = SEM("rsem")
        block = es.enter_context(nc.Block())

        P = Prog()

        P.op("q", lambda e: e.dma_start(out=ccol[:], in_=ccol_in), writes=["ccol"], dma=True)
        P.op("q", lambda e: e.dma_start(out=dm[:], in_=dmask_in), writes=["dm"], dma=True)
        P.op("q", lambda e: e.dma_start(out=identf[:], in_=ident_in), writes=["identf"], dma=True)
        P.op("q", lambda e: e.dma_start(out=identb[:], in_=identb_in), writes=["identb"], dma=True)
        P.op("q", lambda e: e.dma_start(out=ones1[:], in_=ones_in), writes=["ones1"], dma=True)
        P.op("q", lambda e: e.dma_start(out=onesf[:], in_=onesf_in), writes=["onesf"], dma=True)
        P.op("v", lambda e: e.memset(zero1[:], 0.0), writes=["zero1"])
        P.op("v", lambda e: e.memset(cst[:, 0:1], 1.28e-4), writes=["cst"])
        P.op("v", lambda e: e.memset(cst[:, 1:2], 1e-6), writes=["cst"])
        P.op("v", lambda e: e.memset(cst[:, 2:3], 1e-5), writes=["cst"])

        def ln_tile(xin_t, xin_key, out_t, out_key, par=0, eps=1e-5):
            o = par * 8
            sk = f"stat{par}"
            ssx, smx, mneg, m2n2, rs = (stat[:, o + i:o + i + 1] for i in range(5))
            P.op("s", lambda e: e.activation(
                out_t[:], xin_t[:], AF.Identity, accum_out=smx),
                reads=[xin_key], writes=[out_key, sk])
            P.op("s", lambda e: e.activation(
                out_t[:], xin_t[:], AF.Square, accum_out=ssx),
                reads=[xin_key], writes=[out_key, sk])
            P.op("s", lambda e: e.mul(mneg, smx, -1.0 / D), reads=[sk], writes=[sk])
            P.op("s", lambda e: e.activation(m2n2, smx, AF.Square, scale=1.0 / D),
                reads=[sk], writes=[sk])
            P.op("s", lambda e: e.activation(m2n2, m2n2, AF.Identity,
                                             scale=-1.0, bias=cst[:, 2:3]),
                reads=[sk, "cst"], writes=[sk])
            P.op("s", lambda e: e.activation(rs, ssx, AF.Sqrt, bias=m2n2, scale=1.0 / D),
                reads=[sk], writes=[sk])
            P.op("v", lambda e: e.reciprocal(rs, rs), reads=[sk], writes=[sk])
            P.op("v", lambda e: e.tensor_scalar(out_t[:], xin_t[:], mneg, rs, ALU.add, ALU.mult),
                reads=[xin_key, sk], writes=[out_key])

        def xpose_to_fp8(src_t, src_key, tc):
            for dq in range(4):
                bi = 5 + ((tc * 4 + dq) % 2)
                bank = p[bi]
                bk = f"p{bi}"
                def tp4(e, dq=dq, bank=bank, src_t=src_t):
                    for j in range(4):
                        dc = dq * 4 + j
                        ii = e.transpose(bank[:, j * 128:(j + 1) * 128],
                                         src_t[:, dc * 128:(dc + 1) * 128], identf[:])
                    return ii
                P.op("t", tp4, reads=[src_key, "identf"], writes=[bk])
                def ev4(e, dq=dq, bank=bank, tc=tc):
                    for j in range(4):
                        dc = dq * 4 + j
                        s_, a_ = dc // 2, dc % 2
                        ii = e.copy(xln8[:, s_, a_, tc * 128:(tc + 1) * 128],
                                    bank[:, j * 128:(j + 1) * 128])
                    return ii
                P.op("s", ev4, reads=[bk], writes=["xln8"])

        for tc in range(NT):
            b = tc % 2
            P.op("q", lambda e, tc=tc, b=b: e.dma_start(
                out=xin[b][:], in_=x_in[tc * 128:(tc + 1) * 128, :]),
                writes=[f"xin{b}"], dma=True)
            ln_tile(xin[b], f"xin{b}", xl[b], f"xl{b}", par=b)
            xpose_to_fp8(xl[b], f"xl{b}", tc)
        P.barrier()

        def ffn(base_dram, out_dram, cache_weights, wbuf, act8, wbk, a8k):
            for half in range(2):
                first = cache_weights and half == 0
                toff = half * 1024
                for hb in range(16):
                    wb = wbuf[hb % 2]
                    k = wbk[hb % 2]
                    for g in range(2):
                        dst = wb[:, g * 8:(g + 1) * 8]
                        cache = wg8c[g, hb].rearrange("s p a c -> p s a c")
                        if first:
                            src = wg[g][:, hb * 512:(hb + 1) * 512].rearrange(
                                "(s a p) c -> p s a c", s=8, a=2, p=128)
                            P.op("g", lambda e, src=src, dst=dst: e.dma_start(
                                out=dst, in_=src), writes=[k], dma=True)
                            P.op("g", lambda e, dst=dst, cache=cache: e.dma_start(
                                out=cache, in_=dst), reads=[k], dma=True)
                        else:
                            P.op("g", lambda e, dst=dst, cache=cache: e.dma_start(
                                out=dst, in_=cache), writes=[k], dma=True)
                    for hc in range(4):
                        hca = hb * 4 + hc
                        hs, aa = hca // 2, hca % 2
                        for blk in range(2):
                            t0 = toff + blk * 512
                            rot = (hca * 2 + blk) % 2
                            pg0, pg1 = p[rot * 2], p[rot * 2 + 1]
                            pk0, pk1 = f"p{rot*2}", f"p{rot*2+1}"
                            def mm_g(e, wb=wb, hc=hc, t0=t0, pg0=pg0, pg1=pg1):
                                for g in range(2):
                                    for s_ in range(8):
                                        ii = e.matmul(
                                            (pg0 if g == 0 else pg1)[:],
                                            wb[:, g * 8 + s_, :, hc * 128:(hc + 1) * 128],
                                            xln8[:, s_, :, t0:t0 + 512],
                                            start=(s_ == 0), stop=(s_ == 7),
                                            perf_mode=DR)
                                return ii
                            P.op("t", mm_g, reads=[k, "xln8"], writes=[pk0, pk1])
                            gk = f"gate16_{rot}"
                            P.op("s", lambda e, pg0=pg0, rot=rot: e.activation(
                                gate16[rot][:], pg0[:], AF.Gelu_apprx_tanh),
                                reads=[pk0], writes=[gk])
                            tk = f"tmp16_{rot}"
                            P.op("v", lambda e, pg1=pg1, rot=rot: e.tensor_tensor(
                                tmp16[rot][:], gate16[rot][:], pg1[:], ALU.mult),
                                reads=[gk, pk1], writes=[tk])
                            P.op("s", lambda e, rot=rot, hs=hs, aa=aa, blk=blk: e.copy(
                                act8[:, hs, aa, blk * 512:(blk + 1) * 512],
                                tmp16[rot][:]),
                                reads=[tk], writes=[a8k])
                for dob in range(4):
                    lb0, lb1 = wbuf[(2 * dob) % 3], wbuf[(2 * dob + 1) % 3]
                    k0, k1 = wbk[(2 * dob) % 3], wbk[(2 * dob + 1) % 3]
                    for half16, (lbx, kx) in enumerate(((lb0, k0), (lb1, k1))):
                        dst = lbx[:]
                        cache = wl8c[dob, half16 * 16:(half16 + 1) * 16].rearrange(
                            "hs p a c -> p hs a c")
                        if first:
                            src = wl[half16 * 4096:(half16 + 1) * 4096,
                                     dob * 512:(dob + 1) * 512].rearrange(
                                "(hs a p) c -> p hs a c", hs=16, a=2, p=128)
                            P.op("g", lambda e, src=src, dst=dst: e.dma_start(
                                out=dst, in_=src), writes=[kx], dma=True)
                            P.op("g", lambda e, dst=dst, cache=cache: e.dma_start(
                                out=cache, in_=dst), reads=[kx], dma=True)
                        else:
                            P.op("g", lambda e, dst=dst, cache=cache: e.dma_start(
                                out=dst, in_=cache), writes=[kx], dma=True)
                    for tcl in range(8):
                        tca = half * 8 + tcl
                        rot = tcl % 2
                        po = p[4 + rot]
                        pok = f"p{4+rot}"
                        def mm_l(e, tcl=tcl, po=po, lb0=lb0, lb1=lb1):
                            for hs in range(32):
                                lbx = lb0 if hs < 16 else lb1
                                ii = e.matmul(
                                    po[:], act8[:, hs, :, tcl * 128:(tcl + 1) * 128],
                                    lbx[:, hs % 16], start=(hs == 0), stop=(hs == 31),
                                    perf_mode=DR)
                            return ii
                        P.op("t", mm_l, reads=[k0, k1, a8k], writes=[pok])
                        xrk = f"xres{rot}"
                        P.op("q", lambda e, rot=rot, tca=tca, dob=dob: e.dma_start(
                            out=xres[rot][:], in_=base_dram[
                                tca * 128:(tca + 1) * 128, dob * 512:(dob + 1) * 512]),
                            writes=[xrk], dma=True)
                        rk = f"rpiece{rot}"
                        P.op("v", lambda e, rot=rot, po=po: e.scalar_tensor_tensor(
                            rpiece[rot][:], po[:], 0.5, xres[rot][:], ALU.mult, ALU.add),
                            reads=[pok, xrk], writes=[rk])
                        P.op("q", lambda e, rot=rot, tca=tca, dob=dob: e.dma_start(
                            out=out_dram[tca * 128:(tca + 1) * 128,
                                         dob * 512:(dob + 1) * 512],
                            in_=rpiece[rot][:]), reads=[rk], dma=True)

        es1 = ExitStack()
        act8 = es1.enter_context(nc.sbuf_tensor("act8", [128, 32, 2, 1024], E4))
        wbuf = [es1.enter_context(nc.sbuf_tensor(f"wbuf{i}", [128, 16, 2, 512], E4))
                for i in range(3)]
        ffn(x_in, r1, True, wbuf, act8, ["wbuf0", "wbuf1", "wbuf2"], "act8")
        P.barrier()
        es1.close()

        if debug_stage == 1:
            for tc in range(NT):
                b = tc % 2
                P.op("q", lambda e, tc=tc, b=b: e.dma_start(
                    out=xin[b][:], in_=r1[tc * 128:(tc + 1) * 128, :]),
                    writes=[f"xin{b}"], dma=True)
                P.op("q", lambda e, tc=tc, b=b: e.dma_start(
                    out=y_out[tc * 128:(tc + 1) * 128, :], in_=xin[b][:]),
                    reads=[f"xin{b}"], dma=True)
            P.barrier()
        else:
            _build_attention(P, nc, locals())
            P.barrier()
            es3 = ExitStack()
            act8b = es3.enter_context(nc.sbuf_tensor("act8b", [128, 32, 2, 1024], E4))
            wbufb = [es3.enter_context(nc.sbuf_tensor(f"wbufb{i}", [128, 16, 2, 512], E4))
                     for i in range(3)]
            if debug_stage == 2:
                for tc in range(NT):
                    b = tc % 2
                    P.op("q", lambda e, tc=tc, b=b: e.dma_start(
                        out=xin[b][:], in_=r2[tc * 128:(tc + 1) * 128, :]),
                        writes=[f"xin{b}"], dma=True)
                    P.op("q", lambda e, tc=tc, b=b: e.dma_start(
                        out=y_out[tc * 128:(tc + 1) * 128, :], in_=xin[b][:]),
                        reads=[f"xin{b}"], dma=True)
                P.barrier()
            else:
                ffn(r2, y_out, False, wbufb, act8b, ["wbufb0", "wbufb1", "wbufb2"], "act8b")
                P.barrier()
            es3.close()

        P.emit(block, sems, dsems)
        ndma_q = sum(1 for (eng, _, _, _, dma) in P.ops if eng == "q" and dma)
        ndma_g = sum(1 for (eng, _, _, _, dma) in P.ops if eng == "g" and dma)

        def final_wait(e):
            if ndma_q:
                e.wait_ge(dsems["q"], 16 * ndma_q)
            if ndma_g:
                e.wait_ge(dsems["g"], 16 * ndma_g)
        block.sync(final_wait)

    return nc


def _build_attention(P, nc, env):
    p = env["p"]; pt16 = env["pt16"]
    xin = env["xin"]; xl = env["xl"]; stat = env["stat"]
    xln8 = env["xln8"]; dm = env["dm"]; ccol = env["ccol"]; acol = env["acol"]
    identf = env["identf"]; onesf = env["onesf"]; ones1 = env["ones1"]
    srow = env["srow"]
    rpiece = env["rpiece"]; xres = env["xres"]
    r1 = env["r1"]; r2 = env["r2"]; x2t_d = env["x2t_d"]; qt_d = env["qt_d"]
    cin = env["cin"]; cout = env["cout"]; wqc_d = env["wqc_d"]
    cosr = env["cosr"]; sinr = env["sinr"]
    wq = env["wq"]; wk = env["wk"]; wv = env["wv"]; wo = env["wo"]
    ln_tile = env["ln_tile"]; xpose_to_fp8 = env["xpose_to_fp8"]
    SBx = env["SBx"]; cst = env["cst"]

    esB = ExitStack()
    x2s = [SBx(esB, f"x2s{i}", [128, DC, 128], BF16) for i in range(2)]
    x2tile = [SBx(esB, f"x2tile{i}", [128, DC, 128], BF16) for i in range(2)]
    wpro = [SBx(esB, f"wpro{i}", [128, 16, 512], BF16) for i in range(2)]
    cost = SBx(esB, "cost", [128, 256], F32)
    sint = SBx(esB, "sint", [128, 256], F32)
    pjf = [SBx(esB, f"pjf{i}", [128, 512], F32) for i in range(2)]
    rsc = [SBx(esB, f"rsc{i}", [128, 512], F32) for i in range(2)]
    tt1 = [SBx(esB, f"tt1_{i}", [128, 4, 64], F32) for i in range(2)]
    tt2 = [SBx(esB, f"tt2_{i}", [128, 4, 64], F32) for i in range(2)]
    ror = [SBx(esB, f"ror{i}", [128, 512], F32) for i in range(2)]
    ror2 = [SBx(esB, f"ror2_{i}", [128, 512], F32) for i in range(2)]
    stg = [SBx(esB, f"stg{i}", [128, 4, 128], BF16) for i in range(2)]
    cost2 = [SBx(esB, f"cost2_{i}", [128, 256], F32) for i in range(2)]
    sint2 = [SBx(esB, f"sint2_{i}", [128, 256], F32) for i in range(2)]
    aco = SBx(esB, "aco", [128, 4, 16], F32)
    wqb = SBx(esB, "wqb", [128, 16, 512], BF16)
    ahi = SBx(esB, "ahi", [128, 64], BF16)
    alo32 = SBx(esB, "alo32", [128, 64], F32)
    alo = SBx(esB, "alo", [128, 64], BF16)

    def load_cossin(tc):
        P.op("q", lambda e, tc=tc: e.dma_start(
            out=cost[:], in_=cosr[tc * 128:(tc + 1) * 128, :]),
            writes=["cost"], dma=True)
        P.op("q", lambda e, tc=tc: e.dma_start(
            out=sint[:], in_=sinr[tc * 128:(tc + 1) * 128, :]),
            writes=["sint"], dma=True)

    def rope(src, src_key, dst, dst_key):
        sh = src[:].rearrange("p (h d) -> p h d", h=4)
        dh = dst[:].rearrange("p (h d) -> p h d", h=4)
        ch = cost[:].rearrange("p (h d) -> p h d", h=4)
        sn = sint[:].rearrange("p (h d) -> p h d", h=4)
        P.op("v", lambda e: e.tensor_mul(tt1[:], sh[:, :, 0:64], ch[:]),
            reads=[src_key, "cost"], writes=["tt1"])
        P.op("v", lambda e: e.tensor_mul(tt2[:], sh[:, :, 64:128], sn[:]),
            reads=[src_key, "sint"], writes=["tt2"])
        P.op("v", lambda e: e.tensor_add(dh[:, :, 0:64], tt1[:], tt2[:]),
            reads=["tt1", "tt2"], writes=[dst_key])
        P.op("v", lambda e: e.tensor_mul(tt1[:], sh[:, :, 64:128], ch[:]),
            reads=[src_key, "cost"], writes=["tt1"])
        P.op("v", lambda e: e.tensor_mul(tt2[:], sh[:, :, 0:64], sn[:]),
            reads=[src_key, "sint"], writes=["tt2"])
        P.op("v", lambda e: e.tensor_sub(dh[:, :, 64:128], tt1[:], tt2[:]),
            reads=["tt1", "tt2"], writes=[dst_key])

    def sumsq4(src_key):
        for h4 in range(4):
            P.op("s", lambda e, h4=h4: e.activation(
                rsc[:, h4 * 128:(h4 + 1) * 128], pjf[:, h4 * 128:(h4 + 1) * 128],
                AF.Square, accum_out=stat[:, h4:h4 + 1]),
                reads=[src_key], writes=["rsc", "stat"])

    for tc in range(NT):
        b = tc % 2
        P.op("q", lambda e, tc=tc, b=b: e.dma_start(
            out=xin[b][:], in_=r1[tc * 128:(tc + 1) * 128, :]),
            writes=[f"xin{b}"], dma=True)
        ln_tile(xin[b], f"xin{b}", xl[b], f"xl{b}", par=b)
        for dq in range(4):
            bi = 5 + ((tc * 4 + dq) % 2)
            bk = f"p{bi}"
            def tp4(e, dq=dq, bi=bi, b=b):
                for j in range(4):
                    dc = dq * 4 + j
                    ii = e.transpose(p[bi][:, j * 128:(j + 1) * 128],
                                     xl[b][:, dc * 128:(dc + 1) * 128], identf[:])
                return ii
            P.op("t", tp4, reads=[f"xl{b}", "identf"], writes=[bk])
            def ev4(e, dq=dq, bi=bi, b=b):
                for j in range(4):
                    dc = dq * 4 + j
                    ii = e.copy(x2s[b][:, dc, :], p[bi][:, j * 128:(j + 1) * 128])
                return ii
            P.op("s", ev4, reads=[bk], writes=[f"x2s{b}"])
        P.op("q", lambda e, tc=tc, b=b: e.dma_start(
            out=x2t_d[tc].rearrange("dc p c -> p dc c"), in_=x2s[b][:]),
            reads=[f"x2s{b}"], writes=["x2td"], dma=True)
    P.barrier()

    P.op("g", lambda e: e.dma_start(
        out=wpro[0][:], in_=wk.rearrange("(vc p) c -> p vc c", p=128)),
        writes=["wpro0"], dma=True)
    P.op("g", lambda e: e.dma_start(
        out=wpro[1][:], in_=wv.rearrange("(vc p) c -> p vc c", p=128)),
        writes=["wpro1"], dma=True)

    def load_cossin2(tc, b):
        P.op("q", lambda e, tc=tc, b=b: e.dma_start(
            out=cost2[b][:], in_=cosr[tc * 128:(tc + 1) * 128, :]),
            writes=[f"cost2_{b}"], dma=True)
        P.op("q", lambda e, tc=tc, b=b: e.dma_start(
            out=sint2[b][:], in_=sinr[tc * 128:(tc + 1) * 128, :]),
            writes=[f"sint2_{b}"], dma=True)

    def rope2(b, src, src_key, dst, dst_key):
        sh = src[:].rearrange("p (h d) -> p h d", h=4)
        dh = dst[:].rearrange("p (h d) -> p h d", h=4)
        ch = cost2[b][:].rearrange("p (h d) -> p h d", h=4)
        sn = sint2[b][:].rearrange("p (h d) -> p h d", h=4)
        ck, sk2 = f"cost2_{b}", f"sint2_{b}"
        t1k, t2k = f"tt1_{b}", f"tt2_{b}"
        P.op("v", lambda e: e.tensor_mul(tt1[b][:], sh[:, :, 0:64], ch[:]),
            reads=[src_key, ck], writes=[t1k])
        P.op("v", lambda e: e.tensor_mul(tt2[b][:], sh[:, :, 64:128], sn[:]),
            reads=[src_key, sk2], writes=[t2k])
        P.op("v", lambda e: e.tensor_add(dh[:, :, 0:64], tt1[b][:], tt2[b][:]),
            reads=[t1k, t2k], writes=[dst_key])
        P.op("v", lambda e: e.tensor_mul(tt1[b][:], sh[:, :, 64:128], ch[:]),
            reads=[src_key, ck], writes=[t1k])
        P.op("v", lambda e: e.tensor_mul(tt2[b][:], sh[:, :, 0:64], sn[:]),
            reads=[src_key, sk2], writes=[t2k])
        P.op("v", lambda e: e.tensor_sub(dh[:, :, 64:128], tt1[b][:], tt2[b][:]),
            reads=[t1k, t2k], writes=[dst_key])

    def sumsq4b(b, src_key, outslice, outkey):
        for h4 in range(4):
            P.op("s", lambda e, h4=h4, b=b: e.activation(
                rsc[b][:, h4 * 128:(h4 + 1) * 128], pjf[b][:, h4 * 128:(h4 + 1) * 128],
                AF.Square, accum_out=stat[:, b * 8 + h4:b * 8 + h4 + 1]),
                reads=[src_key], writes=[f"rsc{b}", f"stat{b}"])

    kv_back = [None]
    for tc in range(NT):
        if tc % 4 == 2:
            qdb = tc // 4
            wslot = qdb % 2
            P.op("g", lambda e, qdb=qdb: e.dma_start(
                out=wqb[:], in_=wq[:, qdb * 512:(qdb + 1) * 512].rearrange(
                    "(vc p) c -> p vc c", p=128)),
                writes=["wqb"], dma=True)
            P.op("g", lambda e, qdb=qdb: e.dma_start(
                out=wqc_d[qdb], in_=wqb[:]),
                reads=["wqb"], writes=[f"wqc{qdb}"], dma=True)
        b = tc % 2
        xk = f"x2tile{b}"
        pjk = f"pjf{b}"
        rk = f"ror{b}"
        sgk = f"stg{b}"
        pK, pKk = (p[0], "p0") if b == 0 else (p[2], "p2")
        pV, pVk = (p[1], "p1") if b == 0 else (p[3], "p3")
        pT = 5 + b
        pTk = f"p{pT}"
        P.op("g", lambda e, tc=tc, b=b: e.dma_start(
            out=x2tile[b][:], in_=x2t_d[tc].rearrange("dc p c -> p dc c")),
            reads=["x2td"], writes=[xk], dma=True)
        load_cossin2(tc, b)
        def mmk(e, b=b, pK=pK):
            for dc in range(DC):
                ii = e.matmul(pK[:], x2tile[b][:, dc, :], wpro[0][:, dc, :],
                              start=(dc == 0), stop=(dc == DC - 1))
            return ii
        P.op("t", mmk, reads=[xk, "wpro0"], writes=[pKk])
        P.op("s", lambda e, b=b, pK=pK: e.copy(pjf[b][:], pK[:]),
            reads=[pKk], writes=[pjk])
        def mmv(e, b=b, pV=pV):
            for dc in range(DC):
                ii = e.matmul(pV[:], x2tile[b][:, dc, :], wpro[1][:, dc, :],
                              start=(dc == 0), stop=(dc == DC - 1))
            return ii
        P.op("t", mmv, reads=[xk, "wpro1"], writes=[pVk])
        P.op("s", lambda e, b=b, pV=pV: e.copy(
            stg[b][:].rearrange("p h c -> p (h c)"), pV[:]),
            reads=[pVk], writes=[sgk])
        P.op("q", lambda e, tc=tc, b=b: e.dma_start(
            out=cin[512 + tc * 32:512 + (tc + 1) * 32, :].rearrange(
                "a (bb c) -> (a bb) c", bb=4),
            in_=stg[b][:].rearrange("p h c -> p (h c)")), reads=[sgk], writes=["cin"], dma=True)
        sumsq4b(b, pjk, None, None)
        P.op("s", lambda e, tc=tc, b=b: e.activation(
            aco[:, :, tc:tc + 1], stat[:, b * 8:b * 8 + 4], AF.Sqrt, bias=cst[:, 0:1]),
            reads=[f"stat{b}", "cst"], writes=["aco"])
        P.op("v", lambda e, tc=tc: e.reciprocal(aco[:, :, tc:tc + 1], aco[:, :, tc:tc + 1]),
            reads=["aco"], writes=["aco"])
        rope2(b, pjf[b], pjk, ror[b], rk)
        def back_kv(tc=tc, b=b, pT=pT, pTk=pTk, rk=rk, sgk=sgk):
            def tpk(e, b=b, pT=pT):
                for j in range(4):
                    ii = e.transpose(p[pT][:, j * 128:(j + 1) * 128],
                                     ror[b][:, j * 128:(j + 1) * 128], identf[:])
                return ii
            P.op("t", tpk, reads=[rk, "identf"], writes=[pTk])
            def evk(e, b=b, pT=pT):
                for j in range(4):
                    ii = e.copy(stg[b][:, j, :], p[pT][:, j * 128:(j + 1) * 128])
                return ii
            P.op("s", evk, reads=[pTk], writes=[sgk])
            P.op("q", lambda e, tc=tc, b=b: e.dma_start(
                out=cin[0:512, tc * 128:(tc + 1) * 128].rearrange(
                    "(kv p) c -> p kv c", p=128),
                in_=stg[b][:]), reads=[sgk], writes=["cin"], dma=True)
        if kv_back[0] is not None:
            kv_back[0]()
        kv_back[0] = back_kv
    if kv_back[0] is not None:
        kv_back[0]()
        kv_back[0] = None
    P.op("s", lambda e: e.copy(ahi[:], aco[:].rearrange("p a bb -> p (a bb)")),
        reads=["aco"], writes=["ahi"])
    P.op("v", lambda e: e.tensor_sub(alo32[:], aco[:].rearrange("p a bb -> p (a bb)"), ahi[:]),
        reads=["aco", "ahi"], writes=["alo32"])
    P.op("s", lambda e: e.copy(alo[:], alo32[:]), reads=["alo32"], writes=["alo"])
    P.op("q", lambda e: e.dma_start(
        out=cin[1024:1028, :].rearrange("r (pp f) -> (r pp) f", pp=32),
        in_=ahi[:]), reads=["ahi"], writes=["cin"], dma=True)
    P.op("q", lambda e: e.dma_start(
        out=cin[1028:1032, :].rearrange("r (pp f) -> (r pp) f", pp=32),
        in_=alo[:]), reads=["alo"], writes=["cin"], dma=True)

    groups = [[0, 1], [2, 3], [4, 5], [6, 7]]
    P.op("g", lambda e: e.collective_compute(
        "AllGather", ALU.bypass, groups, [cin], [cout]),
        reads=["cin"], writes=["cout"])

    for qdb in range(4):
        wslot = qdb % 2
        wpk = f"wpro{wslot}"
        P.op("q", lambda e, qdb=qdb, wslot=wslot: e.dma_start(
            out=wpro[wslot][:], in_=wqc_d[qdb]),
            reads=[f"wqc{qdb}"], writes=[wpk], dma=True)
        for tc in range(NT):
            b = tc % 2
            xk = f"x2tile{b}"
            pjk = f"pjf{b}"
            rk = f"ror{b}"
            r2k = f"ror2_{b}"
            sgk = f"stg{b}"
            pQ, pQk = (p[0], "p0") if b == 0 else (p[2], "p2")
            pT = 5 + b
            pTk = f"p{pT}"
            P.op("q", lambda e, tc=tc, b=b: e.dma_start(
                out=x2tile[b][:], in_=x2t_d[tc].rearrange("dc p c -> p dc c")),
                reads=["x2td"], writes=[xk], dma=True)
            load_cossin2(tc, b)
            def mmq(e, b=b, pQ=pQ, wslot=wslot):
                for dc in range(DC):
                    ii = e.matmul(pQ[:], x2tile[b][:, dc, :], wpro[wslot][:, dc, :],
                                  start=(dc == 0), stop=(dc == DC - 1))
                return ii
            P.op("t", mmq, reads=[xk, wpk], writes=[pQk])
            P.op("s", lambda e, b=b, pQ=pQ: e.copy(pjf[b][:], pQ[:]),
                reads=[pQk], writes=[pjk])
            sumsq4b(b, pjk, None, None)
            P.op("s", lambda e, b=b: e.activation(
                stat[:, b * 8 + 4:b * 8 + 8], stat[:, b * 8:b * 8 + 4], AF.Sqrt,
                bias=cst[:, 1:2], scale=1.0 / HD),
                reads=[f"stat{b}", "cst"], writes=[f"stat{b}"])
            P.op("v", lambda e, b=b: e.reciprocal(
                stat[:, b * 8 + 4:b * 8 + 8], stat[:, b * 8 + 4:b * 8 + 8]),
                reads=[f"stat{b}"], writes=[f"stat{b}"])
            rope2(b, pjf[b], pjk, ror[b], rk)
            def bq4(e, b=b):
                for h4 in range(4):
                    ii = e.tensor_scalar(
                        ror2[b][:, h4 * 128:(h4 + 1) * 128],
                        ror[b][:, h4 * 128:(h4 + 1) * 128],
                        stat[:, b * 8 + 4 + h4:b * 8 + 5 + h4], None, ALU.mult)
                return ii
            P.op("v", bq4, reads=[rk, f"stat{b}"], writes=[r2k])
            def back_q(tc=tc, qdb=qdb, b=b, pT=pT, pTk=pTk, r2k=r2k, sgk=sgk):
                def tpq(e, b=b, pT=pT):
                    for j in range(4):
                        ii = e.transpose(p[pT][:, j * 128:(j + 1) * 128],
                                         ror2[b][:, j * 128:(j + 1) * 128], identf[:])
                    return ii
                P.op("t", tpq, reads=[r2k, "identf"], writes=[pTk])
                def evq(e, b=b, pT=pT):
                    for j in range(4):
                        ii = e.copy(stg[b][:, j, :], p[pT][:, j * 128:(j + 1) * 128])
                    return ii
                P.op("s", evq, reads=[pTk], writes=[sgk])
                P.op("q", lambda e, tc=tc, qdb=qdb, b=b: e.dma_start(
                    out=qt_d[tc, qdb * 4:(qdb + 1) * 4].rearrange("q p c -> p q c"),
                    in_=stg[b][:]), reads=[sgk], writes=["qtd"], dma=True)
            if kv_back[0] is not None:
                kv_back[0]()
            kv_back[0] = back_q
    if kv_back[0] is not None:
        kv_back[0]()
        kv_back[0] = None
    P.barrier()
    esB.close()

    esC = ExitStack()
    vt = SBx(esC, "vt", [128, 32, 512], BF16)
    ktbuf = [SBx(esC, f"ktbuf{i}", [128, 4096], BF16) for i in range(2)]
    qtbuf = [SBx(esC, f"qtbuf{i}", [128, 4, 512], BF16) for i in range(2)]
    et = [SBx(esC, f"et{i}", [128, 512], BF16) for i in range(4)]
    s2f = [SBx(esC, f"s2f{i}", [128, 1024], F32) for i in range(2)]
    yt = SBx(esC, "yt", [128, 16, 512], BF16)
    rb = [SBx(esC, f"rb{i}", [128, 512], F32) for i in range(2)]
    wo16 = SBx(esC, "wo16", [128, 16, 512], BF16)
    ahl = [SBx(esC, f"ahl{i}", [128, 64], F32) for i in range(2)]
    esum = [SBx(esC, f"esum{i}", [128, 512], F32) for i in range(2)]
    onesfb = SBx(esC, "onesfbS", [128, 1], F32)

    P.op("v", lambda e: e.memset(onesfb[:], 1.0), writes=["onesfb"])
    for kc in range(32):
        slot, lt = kc // 16, kc % 16
        P.op("g", lambda e, kc=kc, slot=slot, lt=lt: e.dma_start(
            out=vt[:, kc, :],
            in_=cout[slot, 512 + lt * 32:512 + (lt + 1) * 32, :].rearrange(
                "a (b c) -> (a b) c", b=4)),
            reads=["cout"], writes=["vt"], dma=True)
    for slot in range(2):
        P.op("g", lambda e, slot=slot: e.dma_start(
            out=ahl[0][:], in_=cout[slot, 1024:1028, :].rearrange(
                "r (pp f) -> (r pp) f", pp=32)),
            reads=["cout"], writes=["ahl0"], dma=True)
        P.op("g", lambda e, slot=slot: e.dma_start(
            out=ahl[1][:], in_=cout[slot, 1028:1032, :].rearrange(
                "r (pp f) -> (r pp) f", pp=32)),
            reads=["cout"], writes=["ahl1"], dma=True)
        P.op("v", lambda e, slot=slot: e.tensor_add(
            acol[:].rearrange("p (a b) -> p a b", a=4)[:, :, slot * 16:(slot + 1) * 16],
            ahl[0][:].rearrange("p (a b) -> p a b", a=4),
            ahl[1][:].rearrange("p (a b) -> p a b", a=4)),
            reads=["ahl0", "ahl1"], writes=["acol"])

    for qb in range(4):
        for kv in range(4):
            cmb = qb * 4 + kv
            rot = cmb % 2
            kk = f"ktbuf{rot}"
            qk = f"qtbuf{rot}"
            for slot in range(2):
                P.op("g", lambda e, kv=kv, slot=slot, rot=rot: e.dma_start(
                    out=ktbuf[rot][:, slot * 2048:(slot + 1) * 2048],
                    in_=cout[slot, kv * 128:(kv + 1) * 128, :]),
                    reads=["cout"], writes=[kk], dma=True)
            for tq in range(4):
                P.op("g", lambda e, qb=qb, kv=kv, rot=rot, tq=tq: e.dma_start(
                    out=qtbuf[rot][:, :, tq * 128:(tq + 1) * 128],
                    in_=qt_d[qb * 4 + tq, kv * 4:(kv + 1) * 4].rearrange(
                        "q p c -> p q c")),
                    reads=["qtd"], writes=[qk], dma=True)
            nkc = min(32, 4 * qb + 20)
            sbanks = (1, 2, 4, 5, 6)
            LAG = 3
            for qh in range(4):
                py = p[0] if qh % 2 == 0 else p[3]
                pyk = "p0" if qh % 2 == 0 else "p3"
                esk = f"esum{qh % 2}"
                esb = esum[qh % 2]
                def emit_scores(e_kc, rot=rot, qh=qh):
                    bi = sbanks[e_kc % 5]
                    ps, psk = p[bi], f"p{bi}"
                    P.op("t", lambda e, ps=ps, rot=rot, kc=e_kc, qh=qh: e.matmul(
                        ps[:], ktbuf[rot][:, kc * 128:(kc + 1) * 128],
                        qtbuf[rot][:, qh, :], start=True, stop=True),
                        reads=[kk, qk], writes=[psk])
                def emit_tail(kc, rot=rot, qh=qh, qb=qb, kv=kv):
                    bi = sbanks[kc % 5]
                    ps, psk = p[bi], f"p{bi}"
                    ee = kc % 4
                    ek = f"et{ee}"
                    j = kc - 4 * qb
                    jj = kc - 16 - 4 * qb
                    dslot = j if 0 <= j < 4 else (4 + jj if 0 <= jj < 4 else None)
                    if dslot is not None:
                        sk = f"s2f{ee % 2}"
                        P.op("v", lambda e, ps=ps, ee=ee, dslot=dslot: e.tensor_add(
                            s2f[ee % 2][:, (ee // 2) * 512:(ee // 2 + 1) * 512],
                            ps[:], dm[:, dslot, :]),
                            reads=[psk, "dm"], writes=[sk])
                        src = s2f[ee % 2][:, (ee // 2) * 512:(ee // 2 + 1) * 512]
                        srck = sk
                    else:
                        src, srck = ps[:], psk
                    P.op("s", lambda e, src=src, ee=ee, kv=kv, kc=kc, qb=qb: e.activation(
                        et[ee][:], src, AF.Exp,
                        bias=ccol[:, qb * 32 + kc:qb * 32 + kc + 1],
                        scale=acol[:].rearrange("p (a bb) -> p a bb", a=4)[:, kv, kc:kc + 1]),
                        reads=[srck, "ccol", "acol"], writes=[ek])
                    if kc == 0:
                        P.op("v", lambda e, ee=ee: e.tensor_copy(esb[:], et[ee][:]),
                            reads=[ek], writes=[esk])
                    else:
                        P.op("v", lambda e, ee=ee: e.tensor_add(esb[:], esb[:], et[ee][:]),
                            reads=[ek, esk], writes=[esk])
                    P.op("t", lambda e, ee=ee, kc=kc, kv=kv, py=py: e.matmul(
                        py[:], vt[:, kc, kv * 128:(kv + 1) * 128],
                        et[ee][:], start=(kc == 0), stop=(kc == nkc - 1)),
                        reads=[ek, "vt"], writes=[pyk])
                for i in range(nkc + LAG):
                    if i < nkc:
                        emit_scores(i)
                    if i >= LAG:
                        emit_tail(i - LAG)
                rr = qh % 2
                srow_r = 32 * (qh % 3)
                P.op("t", lambda e, srow_r=srow_r: e.matmul(
                    p[4][srow_r:srow_r + 1, :], onesfb[:], esb[:],
                    start=True, stop=True),
                    reads=[esk, "onesfb"], writes=["p4"])
                P.op("s", lambda e, srow_r=srow_r: e.copy(srow[:], p[4][srow_r:srow_r + 1, :]),
                    reads=["p4"], writes=["srow"])
                P.op("v", lambda e: e.reciprocal(srow[:], srow[:]),
                    reads=["srow"], writes=["srow"])
                P.op("t", lambda e: e.matmul(p[2][:], onesf[:], srow[:],
                                             start=True, stop=True),
                    reads=["srow", "onesf"], writes=["p2"])
                P.op("s", lambda e, rr=rr: e.copy(rb[rr][:], p[2][:]),
                    reads=["p2"], writes=[f"rb{rr}"])
                P.op("v", lambda e, qh=qh, kv=kv, rr=rr, py=py, pyk=pyk: e.tensor_mul(
                    yt[:, kv * 4 + qh, :], py[:], rb[rr][:]),
                    reads=[pyk, f"rb{rr}"], writes=["yt"])
        for dob in range(4):
            P.op("g", lambda e, dob=dob: e.dma_start(
                out=wo16[:], in_=wo[:, dob * 512:(dob + 1) * 512].rearrange(
                    "(vc p) c -> p vc c", p=128)),
                writes=["wo16"], dma=True)
            for tcq in range(4):
                po = p[tcq]
                pok = f"p{tcq}"
                def mm_wo(e, tcq=tcq, po=po):
                    for vc in range(16):
                        ii = e.matmul(po[:], yt[:, vc, tcq * 128:(tcq + 1) * 128],
                                      wo16[:, vc, :], start=(vc == 0), stop=(vc == 15))
                    return ii
                P.op("t", mm_wo, reads=["yt", "wo16"], writes=[pok])
                rr = tcq % 2
                P.op("q", lambda e, qb=qb, tcq=tcq, dob=dob, rr=rr: e.dma_start(
                    out=xres[rr][:, 0:512], in_=r1[
                        qb * 512 + tcq * 128:qb * 512 + (tcq + 1) * 128,
                        dob * 512:(dob + 1) * 512]),
                    writes=[f"xres{rr}"], dma=True)
                P.op("v", lambda e, po=po, rr=rr: e.tensor_add(
                    rpiece[rr][:], po[:], xres[rr][:, 0:512]),
                    reads=[pok, f"xres{rr}"], writes=[f"rpiece{rr}"])
                P.op("q", lambda e, qb=qb, tcq=tcq, dob=dob, rr=rr: e.dma_start(
                    out=r2[qb * 512 + tcq * 128:qb * 512 + (tcq + 1) * 128,
                           dob * 512:(dob + 1) * 512],
                    in_=rpiece[rr][:]), reads=[f"rpiece{rr}"], dma=True)
    P.barrier()
    esC.close()

    for tc in range(NT):
        b = tc % 2
        P.op("q", lambda e, tc=tc, b=b: e.dma_start(
            out=xin[b][:], in_=r2[tc * 128:(tc + 1) * 128, :]),
            writes=[f"xin{b}"], dma=True)
        ln_tile(xin[b], f"xin{b}", xl[b], f"xl{b}", par=b)
        xpose_to_fp8(xl[b], f"xl{b}", tc)


def _host_inputs(core, full):
    b, h = core // 2, core % 2
    row0 = h * T
    x = np.ascontiguousarray(full["L0_attn_x"][b, row0:row0 + T, :])

    half = 64
    inv = 1.0 / (100000.0 ** (np.arange(half, dtype=np.float64) / half))
    pos = np.arange(row0, row0 + T, dtype=np.float64)[:, None] * inv[None, :]
    cos1 = np.cos(pos).astype(np.float32)
    sin1 = np.sin(pos).astype(np.float32)
    cosr = np.tile(cos1, (1, 4))
    sinr = np.tile(sin1, (1, 4))

    ccol = np.zeros((128, 4, 32), np.float32)
    for qb in range(4):
        for kc in range(32):
            if kc >= row0 // 128 + 4 * qb + 4:
                ccol[:, qb, kc] = -30000.0
    ccol = ccol.reshape(128, 128)
    dmask = np.zeros((128, 8, 512), ml_dtypes.bfloat16)
    r = np.arange(128)[:, None]
    c = np.arange(512)[None, :]
    for j in range(4):
        pat = np.where(j * 128 + r > c, -1e6, 0.0).astype(ml_dtypes.bfloat16)
        if h == 0:
            dmask[:, j, :] = pat
        dmask[:, 4 + j, :] = pat

    return {
        "x": x,
        "wq": full["L0_attn_wq"], "wk": full["L0_attn_wk"],
        "wv": full["L0_attn_wv"], "wo": full["L0_attn_wo"],
        "wg": full["L0_ffn_w_gating"], "wl": full["L0_ffn_w_linear"],
        "cosr": cosr, "sinr": sinr, "ccol": ccol, "dmask": np.asarray(dmask),
        "identf": np.eye(128, dtype=np.float32),
        "identb": np.asarray(np.eye(128, dtype=ml_dtypes.bfloat16)),
        "ones1": np.asarray(np.ones((128, 2), ml_dtypes.bfloat16)),
        "onesf": np.ones((1, 128), np.float32),
    }


def kernel(**inputs):
    nc = build_program(debug_stage=DEBUG_STAGE)
    in_maps = [_host_inputs(c, inputs) for c in range(8)]
    res = run_bass_kernel_spmd(nc, in_maps, core_ids=list(range(8)))
    out = np.empty((4, 4096, 2048), np.float32)
    for c in range(8):
        b, h = c // 2, c % 2
        out[b, h * T:(h + 1) * T, :] = res.results[c]["y"]
    return out


# version 13
# speedup vs baseline: 1.3196x; 1.0226x over previous
import numpy as np
import ml_dtypes
from contextlib import ExitStack
import concourse.bass as bass
import concourse.bass_isa as bass_isa
import concourse.mybir as mybir
from concourse.bass_utils import run_bass_kernel_spmd

F32 = mybir.dt.float32
BF16 = mybir.dt.bfloat16
E4 = mybir.dt.float8e4
AF = mybir.ActivationFunctionType
ALU = mybir.AluOpType
DR = mybir.MatmulPerfMode.DoubleRow
AXX = mybir.AxisListType.X

T = 2048
D = 2048
HID = 8192
NH, NKV, HD = 16, 4, 128
NT = T // 128
DC = D // 128
KTOT = 4096

DEBUG_STAGE = 0


class Prog:
    ENG = ("t", "v", "s", "g", "q")

    def __init__(self):
        self.ops = []
        self.barriers = []

    def op(self, eng, emit, reads=(), writes=(), dma=False):
        self.ops.append((eng, emit, tuple(reads), tuple(writes), dma))

    def barrier(self):
        self.barriers.append(len(self.ops))

    def emit(self, block, sems, dsems):
        n = len(self.ops)
        comp = [None] * n
        cnt = {e: 0 for e in self.ENG}
        dcnt = {e: 0 for e in self.ENG}
        for i, (eng, emit, reads, writes, dma) in enumerate(self.ops):
            if dma:
                dcnt[eng] += 1
                comp[i] = ("d", eng, 16 * dcnt[eng])
            else:
                cnt[eng] += 1
                comp[i] = ("c", eng, cnt[eng])
        last_writer = {}
        readers = {}
        waits = [[] for _ in range(n)]
        bar_idx = 0
        barrier_tokens = []
        run_c = {e: 0 for e in self.ENG}
        run_d = {e: 0 for e in self.ENG}
        for i, (eng, emit, reads, writes, dma) in enumerate(self.ops):
            while bar_idx < len(self.barriers) and self.barriers[bar_idx] == i:
                toks = []
                for e2 in self.ENG:
                    if run_c[e2]:
                        toks.append(("c", e2, run_c[e2]))
                    if run_d[e2]:
                        toks.append(("d", e2, run_d[e2]))
                barrier_tokens = toks
                last_writer = {}
                readers = {}
                bar_idx += 1
            w = list(barrier_tokens)
            for k in reads:
                if k in last_writer:
                    w.append(comp[last_writer[k]])
            for k in writes:
                if k in last_writer:
                    w.append(comp[last_writer[k]])
                for r in readers.get(k, ()):
                    w.append(comp[r])
            waits[i] = w
            for k in writes:
                last_writer[k] = i
                readers[k] = []
            for k in reads:
                readers.setdefault(k, []).append(i)
            kk, ee, vv = comp[i]
            (run_c if kk == "c" else run_d)[ee] = vv

        def body_for(ename):
            def body(e):
                water = {}
                for i, (eng, emit, reads, writes, dma) in enumerate(self.ops):
                    if eng != ename:
                        continue
                    for (k, e2, v) in sorted(set(waits[i])):
                        sem = sems[e2] if k == "c" else dsems[e2]
                        key = (k, e2)
                        if water.get(key, -1) >= v:
                            continue
                        e.wait_ge(sem, v)
                        water[key] = v
                    ins = emit(e)
                    if dma:
                        ins.then_inc(dsems[ename], 16)
                    else:
                        ins.then_inc(sems[ename], 1)
            return body
        block.tensor(body_for("t"))
        block.vector(body_for("v"))
        block.scalar(body_for("s"))
        block.gpsimd(body_for("g"))
        block.sync(body_for("q"))


def build_program(debug_stage=0):
    nc = bass.Bass("TRN2", target_bir_lowering=False, debug=False)
    x_in = nc.dram_tensor("x", [T, D], F32, kind="ExternalInput").ap()
    wq = nc.dram_tensor("wq", [D, D], F32, kind="ExternalInput").ap()
    wk = nc.dram_tensor("wk", [D, 512], F32, kind="ExternalInput").ap()
    wv = nc.dram_tensor("wv", [D, 512], F32, kind="ExternalInput").ap()
    wo = nc.dram_tensor("wo", [D, D], F32, kind="ExternalInput").ap()
    wg = nc.dram_tensor("wg", [2, D, HID], F32, kind="ExternalInput").ap()
    wl = nc.dram_tensor("wl", [HID, D], F32, kind="ExternalInput").ap()
    cosr = nc.dram_tensor("cosr", [T, 256], F32, kind="ExternalInput").ap()
    sinr = nc.dram_tensor("sinr", [T, 256], F32, kind="ExternalInput").ap()
    ccol_in = nc.dram_tensor("ccol", [128, 128], F32, kind="ExternalInput").ap()
    dmask_in = nc.dram_tensor("dmask", [128, 8, 512], BF16, kind="ExternalInput").ap()
    ident_in = nc.dram_tensor("identf", [128, 128], F32, kind="ExternalInput").ap()
    identb_in = nc.dram_tensor("identb", [128, 128], BF16, kind="ExternalInput").ap()
    ones_in = nc.dram_tensor("ones1", [128, 2], BF16, kind="ExternalInput").ap()
    onesf_in = nc.dram_tensor("onesf", [1, 128], F32, kind="ExternalInput").ap()

    y_out = nc.dram_tensor("y", [T, D], F32, kind="ExternalOutput").ap()

    r1 = nc.dram_tensor("r1", [T, D], F32, kind="Internal").ap()
    r2 = nc.dram_tensor("r2", [T, D], F32, kind="Internal").ap()
    x2t_d = nc.dram_tensor("x2t", [NT, DC, 128, 128], BF16, kind="Internal").ap()
    qt_d = nc.dram_tensor("qt_d", [NT, NH, 128, 128], BF16, kind="Internal").ap()
    wg8c = nc.dram_tensor("wg8c", [2, 16, 8, 128, 2, 512], E4, kind="Internal").ap()
    wl8c = nc.dram_tensor("wl8c", [4, 32, 128, 2, 512], E4, kind="Internal").ap()
    cin = nc.dram_tensor("cin", [1032, 2048], BF16, kind="Internal").ap()
    wqc_d = nc.dram_tensor("wqc", [4, 128, 16, 512], BF16, kind="Internal").ap()
    cout = nc.dram_tensor("cout", [2, 1032, 2048], BF16, kind="Internal").ap()

    es = ExitStack()
    SB = lambda nm, sh, dt: es.enter_context(nc.sbuf_tensor(nm, sh, dt))
    SBx = lambda st, nm, sh, dt: st.enter_context(nc.sbuf_tensor(nm, sh, dt))
    PS = lambda nm, sh, dt=F32: es.enter_context(nc.psum_tensor(nm, sh, dt))
    SEM = lambda nm: es.enter_context(nc.semaphore(nm))

    with es:
        xln8 = SB("xln8", [128, 8, 2, T], E4)
        xin = [SB(f"xin{i}", [128, D], F32) for i in range(2)]
        xl = [SB(f"xl{i}", [128, D], F32) for i in range(2)]
        stat = SB("stat", [128, 16], F32)
        ccol = SB("ccolS", [128, 128], F32)
        acol = SB("acolS", [128, 128], F32)
        onesf = SB("onesfS", [1, 128], F32)
        srow = [SB(f"srowS{i}", [1, 512], F32) for i in range(2)]
        cst = SB("cstS", [128, 4], F32)
        dm = SB("dmS", [128, 8, 512], BF16)
        identf = SB("identfS", [128, 128], F32)
        identb = SB("identbS", [128, 128], BF16)
        ones1 = SB("ones1S", [128, 2], BF16)
        zero1 = SB("zero1", [128, 1], F32)
        gate16 = [SB(f"gate16_{i}", [128, 512], BF16) for i in range(2)]
        tmp16 = [SB(f"tmp16_{i}", [128, 512], BF16) for i in range(2)]
        rpiece = [SB(f"rpiece{i}", [128, 512], F32) for i in range(2)]
        xres = [SB(f"xres{i}", [128, 512], F32) for i in range(2)]
        p = [PS(f"p{i}", [128, 512]) for i in range(7)]
        pt16 = PS("pt16", [128, 1024], BF16)
        sems = {e: SEM(f"sem_{e}") for e in Prog.ENG}
        dsems = {e: SEM(f"dsem_{e}") for e in Prog.ENG}
        rsem = SEM("rsem")
        block = es.enter_context(nc.Block())

        P = Prog()

        P.op("q", lambda e: e.dma_start(out=ccol[:], in_=ccol_in), writes=["ccol"], dma=True)
        P.op("q", lambda e: e.dma_start(out=dm[:], in_=dmask_in), writes=["dm"], dma=True)
        P.op("q", lambda e: e.dma_start(out=identf[:], in_=ident_in), writes=["identf"], dma=True)
        P.op("q", lambda e: e.dma_start(out=identb[:], in_=identb_in), writes=["identb"], dma=True)
        P.op("q", lambda e: e.dma_start(out=ones1[:], in_=ones_in), writes=["ones1"], dma=True)
        P.op("q", lambda e: e.dma_start(out=onesf[:], in_=onesf_in), writes=["onesf"], dma=True)
        P.op("v", lambda e: e.memset(zero1[:], 0.0), writes=["zero1"])
        P.op("v", lambda e: e.memset(cst[:, 0:1], 1.28e-4), writes=["cst"])
        P.op("v", lambda e: e.memset(cst[:, 1:2], 1e-6), writes=["cst"])
        P.op("v", lambda e: e.memset(cst[:, 2:3], 1e-5), writes=["cst"])

        def ln_tile(xin_t, xin_key, out_t, out_key, par=0, eps=1e-5):
            o = par * 8
            sk = f"stat{par}"
            ssx, smx, mneg, m2n2, rs = (stat[:, o + i:o + i + 1] for i in range(5))
            P.op("s", lambda e: e.activation(
                out_t[:], xin_t[:], AF.Identity, accum_out=smx),
                reads=[xin_key], writes=[out_key, sk])
            P.op("s", lambda e: e.activation(
                out_t[:], xin_t[:], AF.Square, accum_out=ssx),
                reads=[xin_key], writes=[out_key, sk])
            P.op("s", lambda e: e.mul(mneg, smx, -1.0 / D), reads=[sk], writes=[sk])
            P.op("s", lambda e: e.activation(m2n2, smx, AF.Square, scale=1.0 / D),
                reads=[sk], writes=[sk])
            P.op("s", lambda e: e.activation(m2n2, m2n2, AF.Identity,
                                             scale=-1.0, bias=cst[:, 2:3]),
                reads=[sk, "cst"], writes=[sk])
            P.op("s", lambda e: e.activation(rs, ssx, AF.Sqrt, bias=m2n2, scale=1.0 / D),
                reads=[sk], writes=[sk])
            P.op("v", lambda e: e.reciprocal(rs, rs), reads=[sk], writes=[sk])
            P.op("v", lambda e: e.tensor_scalar(out_t[:], xin_t[:], mneg, rs, ALU.add, ALU.mult),
                reads=[xin_key, sk], writes=[out_key])

        def xpose_to_fp8(src_t, src_key, tc):
            for dq in range(4):
                bi = 5 + ((tc * 4 + dq) % 2)
                bank = p[bi]
                bk = f"p{bi}"
                def tp4(e, dq=dq, bank=bank, src_t=src_t):
                    for j in range(4):
                        dc = dq * 4 + j
                        ii = e.transpose(bank[:, j * 128:(j + 1) * 128],
                                         src_t[:, dc * 128:(dc + 1) * 128], identf[:])
                    return ii
                P.op("t", tp4, reads=[src_key, "identf"], writes=[bk])
                def ev4(e, dq=dq, bank=bank, tc=tc):
                    for j in range(4):
                        dc = dq * 4 + j
                        s_, a_ = dc // 2, dc % 2
                        ii = e.copy(xln8[:, s_, a_, tc * 128:(tc + 1) * 128],
                                    bank[:, j * 128:(j + 1) * 128])
                    return ii
                P.op("s", ev4, reads=[bk], writes=["xln8"])

        for tc in range(NT):
            b = tc % 2
            P.op("q", lambda e, tc=tc, b=b: e.dma_start(
                out=xin[b][:], in_=x_in[tc * 128:(tc + 1) * 128, :]),
                writes=[f"xin{b}"], dma=True)
            ln_tile(xin[b], f"xin{b}", xl[b], f"xl{b}", par=b)
            xpose_to_fp8(xl[b], f"xl{b}", tc)
        P.barrier()

        def ffn(base_dram, out_dram, cache_weights, wbuf, act8, wbk, a8k):
            for half in range(2):
                first = cache_weights and half == 0
                toff = half * 1024
                for hb in range(16):
                    wb = wbuf[hb % 2]
                    k = wbk[hb % 2]
                    for g in range(2):
                        dst = wb[:, g * 8:(g + 1) * 8]
                        cache = wg8c[g, hb].rearrange("s p a c -> p s a c")
                        if first:
                            src = wg[g][:, hb * 512:(hb + 1) * 512].rearrange(
                                "(s a p) c -> p s a c", s=8, a=2, p=128)
                            P.op("g", lambda e, src=src, dst=dst: e.dma_start(
                                out=dst, in_=src), writes=[k], dma=True)
                            P.op("g", lambda e, dst=dst, cache=cache: e.dma_start(
                                out=cache, in_=dst), reads=[k], dma=True)
                        else:
                            P.op("g", lambda e, dst=dst, cache=cache: e.dma_start(
                                out=dst, in_=cache), writes=[k], dma=True)
                    for hc in range(4):
                        hca = hb * 4 + hc
                        hs, aa = hca // 2, hca % 2
                        for blk in range(2):
                            t0 = toff + blk * 512
                            rot = (hca * 2 + blk) % 2
                            pg0, pg1 = p[rot * 2], p[rot * 2 + 1]
                            pk0, pk1 = f"p{rot*2}", f"p{rot*2+1}"
                            def mm_g(e, wb=wb, hc=hc, t0=t0, pg0=pg0, pg1=pg1):
                                for g in range(2):
                                    for s_ in range(8):
                                        ii = e.matmul(
                                            (pg0 if g == 0 else pg1)[:],
                                            wb[:, g * 8 + s_, :, hc * 128:(hc + 1) * 128],
                                            xln8[:, s_, :, t0:t0 + 512],
                                            start=(s_ == 0), stop=(s_ == 7),
                                            perf_mode=DR)
                                return ii
                            P.op("t", mm_g, reads=[k, "xln8"], writes=[pk0, pk1])
                            gk = f"gate16_{rot}"
                            P.op("s", lambda e, pg0=pg0, rot=rot: e.activation(
                                gate16[rot][:], pg0[:], AF.Gelu_apprx_tanh),
                                reads=[pk0], writes=[gk])
                            tk = f"tmp16_{rot}"
                            P.op("v", lambda e, pg1=pg1, rot=rot: e.tensor_tensor(
                                tmp16[rot][:], gate16[rot][:], pg1[:], ALU.mult),
                                reads=[gk, pk1], writes=[tk])
                            P.op("s", lambda e, rot=rot, hs=hs, aa=aa, blk=blk: e.copy(
                                act8[:, hs, aa, blk * 512:(blk + 1) * 512],
                                tmp16[rot][:]),
                                reads=[tk], writes=[a8k])
                for dob in range(4):
                    lb0, lb1 = wbuf[(2 * dob) % 3], wbuf[(2 * dob + 1) % 3]
                    k0, k1 = wbk[(2 * dob) % 3], wbk[(2 * dob + 1) % 3]
                    for half16, (lbx, kx) in enumerate(((lb0, k0), (lb1, k1))):
                        dst = lbx[:]
                        cache = wl8c[dob, half16 * 16:(half16 + 1) * 16].rearrange(
                            "hs p a c -> p hs a c")
                        if first:
                            src = wl[half16 * 4096:(half16 + 1) * 4096,
                                     dob * 512:(dob + 1) * 512].rearrange(
                                "(hs a p) c -> p hs a c", hs=16, a=2, p=128)
                            P.op("g", lambda e, src=src, dst=dst: e.dma_start(
                                out=dst, in_=src), writes=[kx], dma=True)
                            P.op("g", lambda e, dst=dst, cache=cache: e.dma_start(
                                out=cache, in_=dst), reads=[kx], dma=True)
                        else:
                            P.op("g", lambda e, dst=dst, cache=cache: e.dma_start(
                                out=dst, in_=cache), writes=[kx], dma=True)
                    for tcl in range(8):
                        tca = half * 8 + tcl
                        rot = tcl % 2
                        po = p[4 + rot]
                        pok = f"p{4+rot}"
                        def mm_l(e, tcl=tcl, po=po, lb0=lb0, lb1=lb1):
                            for hs in range(32):
                                lbx = lb0 if hs < 16 else lb1
                                ii = e.matmul(
                                    po[:], act8[:, hs, :, tcl * 128:(tcl + 1) * 128],
                                    lbx[:, hs % 16], start=(hs == 0), stop=(hs == 31),
                                    perf_mode=DR)
                            return ii
                        P.op("t", mm_l, reads=[k0, k1, a8k], writes=[pok])
                        xrk = f"xres{rot}"
                        P.op("q", lambda e, rot=rot, tca=tca, dob=dob: e.dma_start(
                            out=xres[rot][:], in_=base_dram[
                                tca * 128:(tca + 1) * 128, dob * 512:(dob + 1) * 512]),
                            writes=[xrk], dma=True)
                        rk = f"rpiece{rot}"
                        P.op("v", lambda e, rot=rot, po=po: e.scalar_tensor_tensor(
                            rpiece[rot][:], po[:], 0.5, xres[rot][:], ALU.mult, ALU.add),
                            reads=[pok, xrk], writes=[rk])
                        P.op("q", lambda e, rot=rot, tca=tca, dob=dob: e.dma_start(
                            out=out_dram[tca * 128:(tca + 1) * 128,
                                         dob * 512:(dob + 1) * 512],
                            in_=rpiece[rot][:]), reads=[rk], dma=True)

        es1 = ExitStack()
        act8 = es1.enter_context(nc.sbuf_tensor("act8", [128, 32, 2, 1024], E4))
        wbuf = [es1.enter_context(nc.sbuf_tensor(f"wbuf{i}", [128, 16, 2, 512], E4))
                for i in range(3)]
        ffn(x_in, r1, True, wbuf, act8, ["wbuf0", "wbuf1", "wbuf2"], "act8")
        P.barrier()
        es1.close()

        if debug_stage == 1:
            for tc in range(NT):
                b = tc % 2
                P.op("q", lambda e, tc=tc, b=b: e.dma_start(
                    out=xin[b][:], in_=r1[tc * 128:(tc + 1) * 128, :]),
                    writes=[f"xin{b}"], dma=True)
                P.op("q", lambda e, tc=tc, b=b: e.dma_start(
                    out=y_out[tc * 128:(tc + 1) * 128, :], in_=xin[b][:]),
                    reads=[f"xin{b}"], dma=True)
            P.barrier()
        else:
            _build_attention(P, nc, locals())
            P.barrier()
            es3 = ExitStack()
            act8b = es3.enter_context(nc.sbuf_tensor("act8b", [128, 32, 2, 1024], E4))
            wbufb = [es3.enter_context(nc.sbuf_tensor(f"wbufb{i}", [128, 16, 2, 512], E4))
                     for i in range(3)]
            if debug_stage == 2:
                for tc in range(NT):
                    b = tc % 2
                    P.op("q", lambda e, tc=tc, b=b: e.dma_start(
                        out=xin[b][:], in_=r2[tc * 128:(tc + 1) * 128, :]),
                        writes=[f"xin{b}"], dma=True)
                    P.op("q", lambda e, tc=tc, b=b: e.dma_start(
                        out=y_out[tc * 128:(tc + 1) * 128, :], in_=xin[b][:]),
                        reads=[f"xin{b}"], dma=True)
                P.barrier()
            else:
                ffn(r2, y_out, False, wbufb, act8b, ["wbufb0", "wbufb1", "wbufb2"], "act8b")
                P.barrier()
            es3.close()

        P.emit(block, sems, dsems)
        ndma_q = sum(1 for (eng, _, _, _, dma) in P.ops if eng == "q" and dma)
        ndma_g = sum(1 for (eng, _, _, _, dma) in P.ops if eng == "g" and dma)

        def final_wait(e):
            if ndma_q:
                e.wait_ge(dsems["q"], 16 * ndma_q)
            if ndma_g:
                e.wait_ge(dsems["g"], 16 * ndma_g)
        block.sync(final_wait)

    return nc


def _build_attention(P, nc, env):
    p = env["p"]; pt16 = env["pt16"]
    xin = env["xin"]; xl = env["xl"]; stat = env["stat"]
    xln8 = env["xln8"]; dm = env["dm"]; ccol = env["ccol"]; acol = env["acol"]
    identf = env["identf"]; onesf = env["onesf"]; ones1 = env["ones1"]
    srow = env["srow"]
    rpiece = env["rpiece"]; xres = env["xres"]
    r1 = env["r1"]; r2 = env["r2"]; x2t_d = env["x2t_d"]; qt_d = env["qt_d"]
    cin = env["cin"]; cout = env["cout"]; wqc_d = env["wqc_d"]
    cosr = env["cosr"]; sinr = env["sinr"]
    wq = env["wq"]; wk = env["wk"]; wv = env["wv"]; wo = env["wo"]
    ln_tile = env["ln_tile"]; xpose_to_fp8 = env["xpose_to_fp8"]
    SBx = env["SBx"]; cst = env["cst"]

    esB = ExitStack()
    x2s = [SBx(esB, f"x2s{i}", [128, DC, 128], BF16) for i in range(2)]
    x2tile = [SBx(esB, f"x2tile{i}", [128, DC, 128], BF16) for i in range(2)]
    wpro = [SBx(esB, f"wpro{i}", [128, 16, 512], BF16) for i in range(2)]
    cost = SBx(esB, "cost", [128, 256], F32)
    sint = SBx(esB, "sint", [128, 256], F32)
    pjf = [SBx(esB, f"pjf{i}", [128, 512], F32) for i in range(2)]
    rsc = [SBx(esB, f"rsc{i}", [128, 512], F32) for i in range(2)]
    tt1 = [SBx(esB, f"tt1_{i}", [128, 4, 64], F32) for i in range(2)]
    tt2 = [SBx(esB, f"tt2_{i}", [128, 4, 64], F32) for i in range(2)]
    ror = [SBx(esB, f"ror{i}", [128, 512], F32) for i in range(2)]
    ror2 = [SBx(esB, f"ror2_{i}", [128, 512], F32) for i in range(2)]
    stg = [SBx(esB, f"stg{i}", [128, 4, 128], BF16) for i in range(2)]
    cost2 = [SBx(esB, f"cost2_{i}", [128, 256], F32) for i in range(2)]
    sint2 = [SBx(esB, f"sint2_{i}", [128, 256], F32) for i in range(2)]
    aco = SBx(esB, "aco", [128, 4, 16], F32)
    wqb = SBx(esB, "wqb", [128, 16, 512], BF16)
    ahi = SBx(esB, "ahi", [128, 64], BF16)
    alo32 = SBx(esB, "alo32", [128, 64], F32)
    alo = SBx(esB, "alo", [128, 64], BF16)

    def load_cossin(tc):
        P.op("q", lambda e, tc=tc: e.dma_start(
            out=cost[:], in_=cosr[tc * 128:(tc + 1) * 128, :]),
            writes=["cost"], dma=True)
        P.op("q", lambda e, tc=tc: e.dma_start(
            out=sint[:], in_=sinr[tc * 128:(tc + 1) * 128, :]),
            writes=["sint"], dma=True)

    def rope(src, src_key, dst, dst_key):
        sh = src[:].rearrange("p (h d) -> p h d", h=4)
        dh = dst[:].rearrange("p (h d) -> p h d", h=4)
        ch = cost[:].rearrange("p (h d) -> p h d", h=4)
        sn = sint[:].rearrange("p (h d) -> p h d", h=4)
        P.op("v", lambda e: e.tensor_mul(tt1[:], sh[:, :, 0:64], ch[:]),
            reads=[src_key, "cost"], writes=["tt1"])
        P.op("v", lambda e: e.tensor_mul(tt2[:], sh[:, :, 64:128], sn[:]),
            reads=[src_key, "sint"], writes=["tt2"])
        P.op("v", lambda e: e.tensor_add(dh[:, :, 0:64], tt1[:], tt2[:]),
            reads=["tt1", "tt2"], writes=[dst_key])
        P.op("v", lambda e: e.tensor_mul(tt1[:], sh[:, :, 64:128], ch[:]),
            reads=[src_key, "cost"], writes=["tt1"])
        P.op("v", lambda e: e.tensor_mul(tt2[:], sh[:, :, 0:64], sn[:]),
            reads=[src_key, "sint"], writes=["tt2"])
        P.op("v", lambda e: e.tensor_sub(dh[:, :, 64:128], tt1[:], tt2[:]),
            reads=["tt1", "tt2"], writes=[dst_key])

    def sumsq4(src_key):
        for h4 in range(4):
            P.op("s", lambda e, h4=h4: e.activation(
                rsc[:, h4 * 128:(h4 + 1) * 128], pjf[:, h4 * 128:(h4 + 1) * 128],
                AF.Square, accum_out=stat[:, h4:h4 + 1]),
                reads=[src_key], writes=["rsc", "stat"])

    for tc in range(NT):
        b = tc % 2
        P.op("q", lambda e, tc=tc, b=b: e.dma_start(
            out=xin[b][:], in_=r1[tc * 128:(tc + 1) * 128, :]),
            writes=[f"xin{b}"], dma=True)
        ln_tile(xin[b], f"xin{b}", xl[b], f"xl{b}", par=b)
        for dq in range(4):
            bi = 5 + ((tc * 4 + dq) % 2)
            bk = f"p{bi}"
            def tp4(e, dq=dq, bi=bi, b=b):
                for j in range(4):
                    dc = dq * 4 + j
                    ii = e.transpose(p[bi][:, j * 128:(j + 1) * 128],
                                     xl[b][:, dc * 128:(dc + 1) * 128], identf[:])
                return ii
            P.op("t", tp4, reads=[f"xl{b}", "identf"], writes=[bk])
            def ev4(e, dq=dq, bi=bi, b=b):
                for j in range(4):
                    dc = dq * 4 + j
                    ii = e.copy(x2s[b][:, dc, :], p[bi][:, j * 128:(j + 1) * 128])
                return ii
            P.op("s", ev4, reads=[bk], writes=[f"x2s{b}"])
        P.op("q", lambda e, tc=tc, b=b: e.dma_start(
            out=x2t_d[tc].rearrange("dc p c -> p dc c"), in_=x2s[b][:]),
            reads=[f"x2s{b}"], writes=["x2td"], dma=True)
    P.barrier()

    P.op("g", lambda e: e.dma_start(
        out=wpro[0][:], in_=wk.rearrange("(vc p) c -> p vc c", p=128)),
        writes=["wpro0"], dma=True)
    P.op("g", lambda e: e.dma_start(
        out=wpro[1][:], in_=wv.rearrange("(vc p) c -> p vc c", p=128)),
        writes=["wpro1"], dma=True)

    def load_cossin2(tc, b):
        P.op("q", lambda e, tc=tc, b=b: e.dma_start(
            out=cost2[b][:], in_=cosr[tc * 128:(tc + 1) * 128, :]),
            writes=[f"cost2_{b}"], dma=True)
        P.op("q", lambda e, tc=tc, b=b: e.dma_start(
            out=sint2[b][:], in_=sinr[tc * 128:(tc + 1) * 128, :]),
            writes=[f"sint2_{b}"], dma=True)

    def rope2(b, src, src_key, dst, dst_key):
        sh = src[:].rearrange("p (h d) -> p h d", h=4)
        dh = dst[:].rearrange("p (h d) -> p h d", h=4)
        ch = cost2[b][:].rearrange("p (h d) -> p h d", h=4)
        sn = sint2[b][:].rearrange("p (h d) -> p h d", h=4)
        ck, sk2 = f"cost2_{b}", f"sint2_{b}"
        t1k, t2k = f"tt1_{b}", f"tt2_{b}"
        P.op("v", lambda e: e.tensor_mul(tt1[b][:], sh[:, :, 0:64], ch[:]),
            reads=[src_key, ck], writes=[t1k])
        P.op("v", lambda e: e.tensor_mul(tt2[b][:], sh[:, :, 64:128], sn[:]),
            reads=[src_key, sk2], writes=[t2k])
        P.op("v", lambda e: e.tensor_add(dh[:, :, 0:64], tt1[b][:], tt2[b][:]),
            reads=[t1k, t2k], writes=[dst_key])
        P.op("v", lambda e: e.tensor_mul(tt1[b][:], sh[:, :, 64:128], ch[:]),
            reads=[src_key, ck], writes=[t1k])
        P.op("v", lambda e: e.tensor_mul(tt2[b][:], sh[:, :, 0:64], sn[:]),
            reads=[src_key, sk2], writes=[t2k])
        P.op("v", lambda e: e.tensor_sub(dh[:, :, 64:128], tt1[b][:], tt2[b][:]),
            reads=[t1k, t2k], writes=[dst_key])

    def sumsq4b(b, src_key, outslice, outkey):
        for h4 in range(4):
            P.op("s", lambda e, h4=h4, b=b: e.activation(
                rsc[b][:, h4 * 128:(h4 + 1) * 128], pjf[b][:, h4 * 128:(h4 + 1) * 128],
                AF.Square, accum_out=stat[:, b * 8 + h4:b * 8 + h4 + 1]),
                reads=[src_key], writes=[f"rsc{b}", f"stat{b}"])

    kv_back = [None]
    for tc in range(NT):
        if tc % 4 == 2:
            qdb = tc // 4
            wslot = qdb % 2
            P.op("g", lambda e, qdb=qdb: e.dma_start(
                out=wqb[:], in_=wq[:, qdb * 512:(qdb + 1) * 512].rearrange(
                    "(vc p) c -> p vc c", p=128)),
                writes=["wqb"], dma=True)
            P.op("g", lambda e, qdb=qdb: e.dma_start(
                out=wqc_d[qdb], in_=wqb[:]),
                reads=["wqb"], writes=[f"wqc{qdb}"], dma=True)
        b = tc % 2
        xk = f"x2tile{b}"
        pjk = f"pjf{b}"
        rk = f"ror{b}"
        sgk = f"stg{b}"
        pK, pKk = (p[0], "p0") if b == 0 else (p[2], "p2")
        pV, pVk = (p[1], "p1") if b == 0 else (p[3], "p3")
        pT = 5 + b
        pTk = f"p{pT}"
        P.op("g", lambda e, tc=tc, b=b: e.dma_start(
            out=x2tile[b][:], in_=x2t_d[tc].rearrange("dc p c -> p dc c")),
            reads=["x2td"], writes=[xk], dma=True)
        load_cossin2(tc, b)
        def mmk(e, b=b, pK=pK):
            for dc in range(DC):
                ii = e.matmul(pK[:], x2tile[b][:, dc, :], wpro[0][:, dc, :],
                              start=(dc == 0), stop=(dc == DC - 1))
            return ii
        P.op("t", mmk, reads=[xk, "wpro0"], writes=[pKk])
        P.op("s", lambda e, b=b, pK=pK: e.copy(pjf[b][:], pK[:]),
            reads=[pKk], writes=[pjk])
        def mmv(e, b=b, pV=pV):
            for dc in range(DC):
                ii = e.matmul(pV[:], x2tile[b][:, dc, :], wpro[1][:, dc, :],
                              start=(dc == 0), stop=(dc == DC - 1))
            return ii
        P.op("t", mmv, reads=[xk, "wpro1"], writes=[pVk])
        P.op("s", lambda e, b=b, pV=pV: e.copy(
            stg[b][:].rearrange("p h c -> p (h c)"), pV[:]),
            reads=[pVk], writes=[sgk])
        P.op("q", lambda e, tc=tc, b=b: e.dma_start(
            out=cin[512 + tc * 32:512 + (tc + 1) * 32, :].rearrange(
                "a (bb c) -> (a bb) c", bb=4),
            in_=stg[b][:].rearrange("p h c -> p (h c)")), reads=[sgk], writes=["cin"], dma=True)
        sumsq4b(b, pjk, None, None)
        P.op("s", lambda e, tc=tc, b=b: e.activation(
            aco[:, :, tc:tc + 1], stat[:, b * 8:b * 8 + 4], AF.Sqrt, bias=cst[:, 0:1]),
            reads=[f"stat{b}", "cst"], writes=["aco"])
        P.op("v", lambda e, tc=tc: e.reciprocal(aco[:, :, tc:tc + 1], aco[:, :, tc:tc + 1]),
            reads=["aco"], writes=["aco"])
        rope2(b, pjf[b], pjk, ror[b], rk)
        def back_kv(tc=tc, b=b, pT=pT, pTk=pTk, rk=rk, sgk=sgk):
            def tpk(e, b=b, pT=pT):
                for j in range(4):
                    ii = e.transpose(p[pT][:, j * 128:(j + 1) * 128],
                                     ror[b][:, j * 128:(j + 1) * 128], identf[:])
                return ii
            P.op("t", tpk, reads=[rk, "identf"], writes=[pTk])
            def evk(e, b=b, pT=pT):
                for j in range(4):
                    ii = e.copy(stg[b][:, j, :], p[pT][:, j * 128:(j + 1) * 128])
                return ii
            P.op("s", evk, reads=[pTk], writes=[sgk])
            P.op("q", lambda e, tc=tc, b=b: e.dma_start(
                out=cin[0:512, tc * 128:(tc + 1) * 128].rearrange(
                    "(kv p) c -> p kv c", p=128),
                in_=stg[b][:]), reads=[sgk], writes=["cin"], dma=True)
        if kv_back[0] is not None:
            kv_back[0]()
        kv_back[0] = back_kv
    if kv_back[0] is not None:
        kv_back[0]()
        kv_back[0] = None
    P.op("s", lambda e: e.copy(ahi[:], aco[:].rearrange("p a bb -> p (a bb)")),
        reads=["aco"], writes=["ahi"])
    P.op("v", lambda e: e.tensor_sub(alo32[:], aco[:].rearrange("p a bb -> p (a bb)"), ahi[:]),
        reads=["aco", "ahi"], writes=["alo32"])
    P.op("s", lambda e: e.copy(alo[:], alo32[:]), reads=["alo32"], writes=["alo"])
    P.op("q", lambda e: e.dma_start(
        out=cin[1024:1028, :].rearrange("r (pp f) -> (r pp) f", pp=32),
        in_=ahi[:]), reads=["ahi"], writes=["cin"], dma=True)
    P.op("q", lambda e: e.dma_start(
        out=cin[1028:1032, :].rearrange("r (pp f) -> (r pp) f", pp=32),
        in_=alo[:]), reads=["alo"], writes=["cin"], dma=True)

    groups = [[0, 1], [2, 3], [4, 5], [6, 7]]
    P.op("g", lambda e: e.collective_compute(
        "AllGather", ALU.bypass, groups, [cin], [cout]),
        reads=["cin"], writes=["cout"])

    for qdb in range(4):
        wslot = qdb % 2
        wpk = f"wpro{wslot}"
        P.op("q", lambda e, qdb=qdb, wslot=wslot: e.dma_start(
            out=wpro[wslot][:], in_=wqc_d[qdb]),
            reads=[f"wqc{qdb}"], writes=[wpk], dma=True)
        for tc in range(NT):
            b = tc % 2
            xk = f"x2tile{b}"
            pjk = f"pjf{b}"
            rk = f"ror{b}"
            r2k = f"ror2_{b}"
            sgk = f"stg{b}"
            pQ, pQk = (p[0], "p0") if b == 0 else (p[2], "p2")
            pT = 5 + b
            pTk = f"p{pT}"
            P.op("q", lambda e, tc=tc, b=b: e.dma_start(
                out=x2tile[b][:], in_=x2t_d[tc].rearrange("dc p c -> p dc c")),
                reads=["x2td"], writes=[xk], dma=True)
            load_cossin2(tc, b)
            def mmq(e, b=b, pQ=pQ, wslot=wslot):
                for dc in range(DC):
                    ii = e.matmul(pQ[:], x2tile[b][:, dc, :], wpro[wslot][:, dc, :],
                                  start=(dc == 0), stop=(dc == DC - 1))
                return ii
            P.op("t", mmq, reads=[xk, wpk], writes=[pQk])
            P.op("s", lambda e, b=b, pQ=pQ: e.copy(pjf[b][:], pQ[:]),
                reads=[pQk], writes=[pjk])
            sumsq4b(b, pjk, None, None)
            P.op("s", lambda e, b=b: e.activation(
                stat[:, b * 8 + 4:b * 8 + 8], stat[:, b * 8:b * 8 + 4], AF.Sqrt,
                bias=cst[:, 1:2], scale=1.0 / HD),
                reads=[f"stat{b}", "cst"], writes=[f"stat{b}"])
            P.op("v", lambda e, b=b: e.reciprocal(
                stat[:, b * 8 + 4:b * 8 + 8], stat[:, b * 8 + 4:b * 8 + 8]),
                reads=[f"stat{b}"], writes=[f"stat{b}"])
            rope2(b, pjf[b], pjk, ror[b], rk)
            def bq4(e, b=b):
                for h4 in range(4):
                    ii = e.tensor_scalar(
                        ror2[b][:, h4 * 128:(h4 + 1) * 128],
                        ror[b][:, h4 * 128:(h4 + 1) * 128],
                        stat[:, b * 8 + 4 + h4:b * 8 + 5 + h4], None, ALU.mult)
                return ii
            P.op("v", bq4, reads=[rk, f"stat{b}"], writes=[r2k])
            def back_q(tc=tc, qdb=qdb, b=b, pT=pT, pTk=pTk, r2k=r2k, sgk=sgk):
                def tpq(e, b=b, pT=pT):
                    for j in range(4):
                        ii = e.transpose(p[pT][:, j * 128:(j + 1) * 128],
                                         ror2[b][:, j * 128:(j + 1) * 128], identf[:])
                    return ii
                P.op("t", tpq, reads=[r2k, "identf"], writes=[pTk])
                def evq(e, b=b, pT=pT):
                    for j in range(4):
                        ii = e.copy(stg[b][:, j, :], p[pT][:, j * 128:(j + 1) * 128])
                    return ii
                P.op("s", evq, reads=[pTk], writes=[sgk])
                P.op("q", lambda e, tc=tc, qdb=qdb, b=b: e.dma_start(
                    out=qt_d[tc, qdb * 4:(qdb + 1) * 4].rearrange("q p c -> p q c"),
                    in_=stg[b][:]), reads=[sgk], writes=["qtd"], dma=True)
            if kv_back[0] is not None:
                kv_back[0]()
            kv_back[0] = back_q
    if kv_back[0] is not None:
        kv_back[0]()
        kv_back[0] = None
    P.barrier()
    esB.close()

    esC = ExitStack()
    vt = SBx(esC, "vt", [128, 32, 512], BF16)
    ktbuf = [SBx(esC, f"ktbuf{i}", [128, 4096], BF16) for i in range(2)]
    qtbuf = [SBx(esC, f"qtbuf{i}", [128, 4, 512], BF16) for i in range(2)]
    et = [SBx(esC, f"et{i}", [128, 512], BF16) for i in range(4)]
    s2f = [SBx(esC, f"s2f{i}", [128, 1024], F32) for i in range(2)]
    yt = SBx(esC, "yt", [128, 16, 512], BF16)
    rb = [SBx(esC, f"rb{i}", [128, 512], F32) for i in range(2)]
    wo16 = SBx(esC, "wo16", [128, 16, 512], BF16)
    ahl = [SBx(esC, f"ahl{i}", [128, 64], F32) for i in range(2)]
    esum = [SBx(esC, f"esum{i}", [128, 512], F32) for i in range(2)]
    onesfb = SBx(esC, "onesfbS", [128, 1], F32)

    P.op("v", lambda e: e.memset(onesfb[:], 1.0), writes=["onesfb"])
    for kc in range(32):
        slot, lt = kc // 16, kc % 16
        P.op("g", lambda e, kc=kc, slot=slot, lt=lt: e.dma_start(
            out=vt[:, kc, :],
            in_=cout[slot, 512 + lt * 32:512 + (lt + 1) * 32, :].rearrange(
                "a (b c) -> (a b) c", b=4)),
            reads=["cout"], writes=["vt"], dma=True)
    for slot in range(2):
        P.op("g", lambda e, slot=slot: e.dma_start(
            out=ahl[0][:], in_=cout[slot, 1024:1028, :].rearrange(
                "r (pp f) -> (r pp) f", pp=32)),
            reads=["cout"], writes=["ahl0"], dma=True)
        P.op("g", lambda e, slot=slot: e.dma_start(
            out=ahl[1][:], in_=cout[slot, 1028:1032, :].rearrange(
                "r (pp f) -> (r pp) f", pp=32)),
            reads=["cout"], writes=["ahl1"], dma=True)
        P.op("v", lambda e, slot=slot: e.tensor_add(
            acol[:].rearrange("p (a b) -> p a b", a=4)[:, :, slot * 16:(slot + 1) * 16],
            ahl[0][:].rearrange("p (a b) -> p a b", a=4),
            ahl[1][:].rearrange("p (a b) -> p a b", a=4)),
            reads=["ahl0", "ahl1"], writes=["acol"])

    pending_tail = []
    for qb in range(4):
        for kv in range(4):
            cmb = qb * 4 + kv
            rot = cmb % 2
            kk = f"ktbuf{rot}"
            qk = f"qtbuf{rot}"
            for slot in range(2):
                P.op("g", lambda e, kv=kv, slot=slot, rot=rot: e.dma_start(
                    out=ktbuf[rot][:, slot * 2048:(slot + 1) * 2048],
                    in_=cout[slot, kv * 128:(kv + 1) * 128, :]),
                    reads=["cout"], writes=[kk], dma=True)
            for tq in range(4):
                P.op("g", lambda e, qb=qb, kv=kv, rot=rot, tq=tq: e.dma_start(
                    out=qtbuf[rot][:, :, tq * 128:(tq + 1) * 128],
                    in_=qt_d[qb * 4 + tq, kv * 4:(kv + 1) * 4].rearrange(
                        "q p c -> p q c")),
                    reads=["qtd"], writes=[qk], dma=True)
            nkc = min(32, 4 * qb + 20)
            sbanks = (1, 2, 5, 6)
            LAG = 3
            for qh in range(4):
                py = p[0] if qh % 2 == 0 else p[3]
                pyk = "p0" if qh % 2 == 0 else "p3"
                esk = f"esum{qh % 2}"
                esb = esum[qh % 2]
                def emit_scores(e_kc, rot=rot, qh=qh):
                    bi = sbanks[e_kc % 4]
                    ps, psk = p[bi], f"p{bi}"
                    P.op("t", lambda e, ps=ps, rot=rot, kc=e_kc, qh=qh: e.matmul(
                        ps[:], ktbuf[rot][:, kc * 128:(kc + 1) * 128],
                        qtbuf[rot][:, qh, :], start=True, stop=True),
                        reads=[kk, qk], writes=[psk])
                def emit_tail(kc, rot=rot, qh=qh, qb=qb, kv=kv, esb=esb, esk=esk,
                              py=py, pyk=pyk):
                    bi = sbanks[kc % 4]
                    ps, psk = p[bi], f"p{bi}"
                    ee = kc % 4
                    ek = f"et{ee}"
                    j = kc - 4 * qb
                    jj = kc - 16 - 4 * qb
                    dslot = j if 0 <= j < 4 else (4 + jj if 0 <= jj < 4 else None)
                    if dslot is not None:
                        sk = f"s2f{ee % 2}"
                        P.op("v", lambda e, ps=ps, ee=ee, dslot=dslot: e.tensor_add(
                            s2f[ee % 2][:, (ee // 2) * 512:(ee // 2 + 1) * 512],
                            ps[:], dm[:, dslot, :]),
                            reads=[psk, "dm"], writes=[sk])
                        src = s2f[ee % 2][:, (ee // 2) * 512:(ee // 2 + 1) * 512]
                        srck = sk
                    else:
                        src, srck = ps[:], psk
                    P.op("s", lambda e, src=src, ee=ee, kv=kv, kc=kc, qb=qb: e.activation(
                        et[ee][:], src, AF.Exp,
                        bias=ccol[:, qb * 32 + kc:qb * 32 + kc + 1],
                        scale=acol[:].rearrange("p (a bb) -> p a bb", a=4)[:, kv, kc:kc + 1]),
                        reads=[srck, "ccol", "acol"], writes=[ek])
                    if kc == 0:
                        P.op("v", lambda e, ee=ee, esb=esb: e.tensor_copy(esb[:], et[ee][:]),
                            reads=[ek], writes=[esk])
                    else:
                        P.op("v", lambda e, ee=ee, esb=esb: e.tensor_add(esb[:], esb[:], et[ee][:]),
                            reads=[ek, esk], writes=[esk])
                    P.op("t", lambda e, ee=ee, kc=kc, kv=kv, py=py: e.matmul(
                        py[:], vt[:, kc, kv * 128:(kv + 1) * 128],
                        et[ee][:], start=(kc == 0), stop=(kc == nkc - 1)),
                        reads=[ek, "vt"], writes=[pyk])
                for i in range(nkc + LAG):
                    if i < nkc:
                        emit_scores(i)
                        if pending_tail and i >= 1:
                            P.op(*pending_tail.pop(0))
                    if i >= LAG:
                        emit_tail(i - LAG)
                rr = qh % 2
                srow_r = 32 * (qh % 3)
                srk = f"srow{rr}"
                rbk = f"rb{rr}"
                pending_tail.extend([
                    ("t", lambda e, srow_r=srow_r, esb=esb: e.matmul(
                        p[4][srow_r:srow_r + 1, :], onesfb[:], esb[:],
                        start=True, stop=True), (esk, "onesfb"), ("p4",), False),
                    ("s", lambda e, srow_r=srow_r, rr=rr: e.copy(
                        srow[rr][:], p[4][srow_r:srow_r + 1, :]), ("p4",), (srk,), False),
                    ("v", lambda e, rr=rr: e.reciprocal(srow[rr][:], srow[rr][:]),
                     (srk,), (srk,), False),
                    ("t", lambda e, rr=rr: e.matmul(
                        p[4][:], onesf[:], srow[rr][:], start=True, stop=True),
                     (srk, "onesf"), ("p4",), False),
                    ("s", lambda e, rr=rr: e.copy(rb[rr][:], p[4][:]),
                     ("p4",), (rbk,), False),
                    ("v", lambda e, qh=qh, kv=kv, rr=rr, py=py: e.tensor_mul(
                        yt[:, kv * 4 + qh, :], py[:], rb[rr][:]),
                     (pyk, rbk), ("yt",), False),
                ])
        for t_ in pending_tail:
            P.op(*t_)
        pending_tail.clear()
        for dob in range(4):
            P.op("g", lambda e, dob=dob: e.dma_start(
                out=wo16[:], in_=wo[:, dob * 512:(dob + 1) * 512].rearrange(
                    "(vc p) c -> p vc c", p=128)),
                writes=["wo16"], dma=True)
            for tcq in range(4):
                po = p[tcq]
                pok = f"p{tcq}"
                def mm_wo(e, tcq=tcq, po=po):
                    for vc in range(16):
                        ii = e.matmul(po[:], yt[:, vc, tcq * 128:(tcq + 1) * 128],
                                      wo16[:, vc, :], start=(vc == 0), stop=(vc == 15))
                    return ii
                P.op("t", mm_wo, reads=["yt", "wo16"], writes=[pok])
                rr = tcq % 2
                P.op("q", lambda e, qb=qb, tcq=tcq, dob=dob, rr=rr: e.dma_start(
                    out=xres[rr][:, 0:512], in_=r1[
                        qb * 512 + tcq * 128:qb * 512 + (tcq + 1) * 128,
                        dob * 512:(dob + 1) * 512]),
                    writes=[f"xres{rr}"], dma=True)
                P.op("v", lambda e, po=po, rr=rr: e.tensor_add(
                    rpiece[rr][:], po[:], xres[rr][:, 0:512]),
                    reads=[pok, f"xres{rr}"], writes=[f"rpiece{rr}"])
                P.op("q", lambda e, qb=qb, tcq=tcq, dob=dob, rr=rr: e.dma_start(
                    out=r2[qb * 512 + tcq * 128:qb * 512 + (tcq + 1) * 128,
                           dob * 512:(dob + 1) * 512],
                    in_=rpiece[rr][:]), reads=[f"rpiece{rr}"], dma=True)
    P.barrier()
    esC.close()

    for tc in range(NT):
        b = tc % 2
        P.op("q", lambda e, tc=tc, b=b: e.dma_start(
            out=xin[b][:], in_=r2[tc * 128:(tc + 1) * 128, :]),
            writes=[f"xin{b}"], dma=True)
        ln_tile(xin[b], f"xin{b}", xl[b], f"xl{b}", par=b)
        xpose_to_fp8(xl[b], f"xl{b}", tc)


def _host_inputs(core, full):
    b, h = core // 2, core % 2
    row0 = h * T
    x = np.ascontiguousarray(full["L0_attn_x"][b, row0:row0 + T, :])

    half = 64
    inv = 1.0 / (100000.0 ** (np.arange(half, dtype=np.float64) / half))
    pos = np.arange(row0, row0 + T, dtype=np.float64)[:, None] * inv[None, :]
    cos1 = np.cos(pos).astype(np.float32)
    sin1 = np.sin(pos).astype(np.float32)
    cosr = np.tile(cos1, (1, 4))
    sinr = np.tile(sin1, (1, 4))

    ccol = np.zeros((128, 4, 32), np.float32)
    for qb in range(4):
        for kc in range(32):
            if kc >= row0 // 128 + 4 * qb + 4:
                ccol[:, qb, kc] = -30000.0
    ccol = ccol.reshape(128, 128)
    dmask = np.zeros((128, 8, 512), ml_dtypes.bfloat16)
    r = np.arange(128)[:, None]
    c = np.arange(512)[None, :]
    for j in range(4):
        pat = np.where(j * 128 + r > c, -1e6, 0.0).astype(ml_dtypes.bfloat16)
        if h == 0:
            dmask[:, j, :] = pat
        dmask[:, 4 + j, :] = pat

    return {
        "x": x,
        "wq": full["L0_attn_wq"], "wk": full["L0_attn_wk"],
        "wv": full["L0_attn_wv"], "wo": full["L0_attn_wo"],
        "wg": full["L0_ffn_w_gating"], "wl": full["L0_ffn_w_linear"],
        "cosr": cosr, "sinr": sinr, "ccol": ccol, "dmask": np.asarray(dmask),
        "identf": np.eye(128, dtype=np.float32),
        "identb": np.asarray(np.eye(128, dtype=ml_dtypes.bfloat16)),
        "ones1": np.asarray(np.ones((128, 2), ml_dtypes.bfloat16)),
        "onesf": np.ones((1, 128), np.float32),
    }


def kernel(**inputs):
    nc = build_program(debug_stage=DEBUG_STAGE)
    in_maps = [_host_inputs(c, inputs) for c in range(8)]
    res = run_bass_kernel_spmd(nc, in_maps, core_ids=list(range(8)))
    out = np.empty((4, 4096, 2048), np.float32)
    for c in range(8):
        b, h = c // 2, c % 2
        out[b, h * T:(h + 1) * T, :] = res.results[c]["y"]
    return out


# version 15 (final)
# speedup vs baseline: 1.4024x; 1.0627x over previous
import numpy as np
import ml_dtypes
from contextlib import ExitStack
import concourse.bass as bass
import concourse.bass_isa as bass_isa
import concourse.mybir as mybir
from concourse.bass_utils import run_bass_kernel_spmd

F32 = mybir.dt.float32
BF16 = mybir.dt.bfloat16
E4 = mybir.dt.float8e4
AF = mybir.ActivationFunctionType
ALU = mybir.AluOpType
DR = mybir.MatmulPerfMode.DoubleRow
AXX = mybir.AxisListType.X

T = 2048
D = 2048
HID = 8192
NH, NKV, HD = 16, 4, 128
NT = T // 128
DC = D // 128
KTOT = 4096

DEBUG_STAGE = 0


class Prog:
    ENG = ("t", "v", "s", "g", "q")

    def __init__(self):
        self.ops = []
        self.barriers = []

    def op(self, eng, emit, reads=(), writes=(), dma=False):
        self.ops.append((eng, emit, tuple(reads), tuple(writes), dma))

    def barrier(self):
        self.barriers.append(len(self.ops))

    def emit(self, block, sems, dsems):
        n = len(self.ops)
        comp = [None] * n
        cnt = {e: 0 for e in self.ENG}
        dcnt = {e: 0 for e in self.ENG}
        for i, (eng, emit, reads, writes, dma) in enumerate(self.ops):
            if dma:
                dcnt[eng] += 1
                comp[i] = ("d", eng, 16 * dcnt[eng])
            else:
                cnt[eng] += 1
                comp[i] = ("c", eng, cnt[eng])
        last_writer = {}
        readers = {}
        waits = [[] for _ in range(n)]
        bar_idx = 0
        barrier_tokens = []
        run_c = {e: 0 for e in self.ENG}
        run_d = {e: 0 for e in self.ENG}
        for i, (eng, emit, reads, writes, dma) in enumerate(self.ops):
            while bar_idx < len(self.barriers) and self.barriers[bar_idx] == i:
                toks = []
                for e2 in self.ENG:
                    if run_c[e2]:
                        toks.append(("c", e2, run_c[e2]))
                    if run_d[e2]:
                        toks.append(("d", e2, run_d[e2]))
                barrier_tokens = toks
                last_writer = {}
                readers = {}
                bar_idx += 1
            w = list(barrier_tokens)
            for k in reads:
                if k in last_writer:
                    w.append(comp[last_writer[k]])
            for k in writes:
                if k in last_writer:
                    w.append(comp[last_writer[k]])
                for r in readers.get(k, ()):
                    w.append(comp[r])
            waits[i] = w
            for k in writes:
                last_writer[k] = i
                readers[k] = []
            for k in reads:
                readers.setdefault(k, []).append(i)
            kk, ee, vv = comp[i]
            (run_c if kk == "c" else run_d)[ee] = vv

        def body_for(ename):
            def body(e):
                water = {}
                for i, (eng, emit, reads, writes, dma) in enumerate(self.ops):
                    if eng != ename:
                        continue
                    for (k, e2, v) in sorted(set(waits[i])):
                        sem = sems[e2] if k == "c" else dsems[e2]
                        key = (k, e2)
                        if water.get(key, -1) >= v:
                            continue
                        e.wait_ge(sem, v)
                        water[key] = v
                    ins = emit(e)
                    if dma:
                        ins.then_inc(dsems[ename], 16)
                    else:
                        ins.then_inc(sems[ename], 1)
            return body
        block.tensor(body_for("t"))
        block.vector(body_for("v"))
        block.scalar(body_for("s"))
        block.gpsimd(body_for("g"))
        block.sync(body_for("q"))


def build_program(debug_stage=0):
    nc = bass.Bass("TRN2", target_bir_lowering=False, debug=False)
    x_in = nc.dram_tensor("x", [T, D], F32, kind="ExternalInput").ap()
    wq = nc.dram_tensor("wq", [D, D], F32, kind="ExternalInput").ap()
    wk = nc.dram_tensor("wk", [D, 512], F32, kind="ExternalInput").ap()
    wv = nc.dram_tensor("wv", [D, 512], F32, kind="ExternalInput").ap()
    wo = nc.dram_tensor("wo", [D, D], F32, kind="ExternalInput").ap()
    wg = nc.dram_tensor("wg", [2, D, HID], F32, kind="ExternalInput").ap()
    wl = nc.dram_tensor("wl", [HID, D], F32, kind="ExternalInput").ap()
    cosr = nc.dram_tensor("cosr", [T, 256], F32, kind="ExternalInput").ap()
    sinr = nc.dram_tensor("sinr", [T, 256], F32, kind="ExternalInput").ap()
    ccol_in = nc.dram_tensor("ccol", [128, 128], F32, kind="ExternalInput").ap()
    dmask_in = nc.dram_tensor("dmask", [128, 8, 512], BF16, kind="ExternalInput").ap()
    ident_in = nc.dram_tensor("identf", [128, 128], F32, kind="ExternalInput").ap()
    identb_in = nc.dram_tensor("identb", [128, 128], BF16, kind="ExternalInput").ap()
    ones_in = nc.dram_tensor("ones1", [128, 2], BF16, kind="ExternalInput").ap()
    onesf_in = nc.dram_tensor("onesf", [1, 128], F32, kind="ExternalInput").ap()

    y_out = nc.dram_tensor("y", [T, D], F32, kind="ExternalOutput").ap()

    r1 = nc.dram_tensor("r1", [T, D], F32, kind="Internal").ap()
    r2 = nc.dram_tensor("r2", [T, D], F32, kind="Internal").ap()
    x2t_d = nc.dram_tensor("x2t", [NT, DC, 128, 128], BF16, kind="Internal").ap()
    qt_d = nc.dram_tensor("qt_d", [NT, NH, 128, 128], BF16, kind="Internal").ap()
    wg8c = nc.dram_tensor("wg8c", [2, 16, 8, 128, 2, 512], E4, kind="Internal").ap()
    wl8c = nc.dram_tensor("wl8c", [4, 32, 128, 2, 512], E4, kind="Internal").ap()
    cin = nc.dram_tensor("cin", [1032, 2048], BF16, kind="Internal").ap()
    wqc_d = nc.dram_tensor("wqc", [4, 128, 16, 512], BF16, kind="Internal").ap()
    cout = nc.dram_tensor("cout", [2, 1032, 2048], BF16, kind="Internal").ap()

    es = ExitStack()
    SB = lambda nm, sh, dt: es.enter_context(nc.sbuf_tensor(nm, sh, dt))
    SBx = lambda st, nm, sh, dt: st.enter_context(nc.sbuf_tensor(nm, sh, dt))
    PS = lambda nm, sh, dt=F32: es.enter_context(nc.psum_tensor(nm, sh, dt))
    SEM = lambda nm: es.enter_context(nc.semaphore(nm))

    with es:
        xln8 = SB("xln8", [128, 8, 2, T], E4)
        xin = [SB(f"xin{i}", [128, D], F32) for i in range(2)]
        xl = [SB(f"xl{i}", [128, D], F32) for i in range(2)]
        stat = SB("stat", [128, 16], F32)
        ccol = SB("ccolS", [128, 128], F32)
        acol = SB("acolS", [128, 128], F32)
        onesf = SB("onesfS", [1, 128], F32)
        srow = SB("srowS", [1, 512], F32)
        cst = SB("cstS", [128, 4], F32)
        dm = SB("dmS", [128, 8, 512], BF16)
        identf = SB("identfS", [128, 128], F32)
        identb = SB("identbS", [128, 128], BF16)
        ones1 = SB("ones1S", [128, 2], BF16)
        zero1 = SB("zero1", [128, 1], F32)
        gate16 = [SB(f"gate16_{i}", [128, 512], BF16) for i in range(2)]
        tmp16 = [SB(f"tmp16_{i}", [128, 512], BF16) for i in range(2)]
        rpiece = [SB(f"rpiece{i}", [128, 512], F32) for i in range(2)]
        xres = [SB(f"xres{i}", [128, 512], F32) for i in range(2)]
        p = [PS(f"p{i}", [128, 512]) for i in range(7)]
        pt16 = PS("pt16", [128, 1024], BF16)
        sems = {e: SEM(f"sem_{e}") for e in Prog.ENG}
        dsems = {e: SEM(f"dsem_{e}") for e in Prog.ENG}
        rsem = SEM("rsem")
        block = es.enter_context(nc.Block())

        P = Prog()

        P.op("q", lambda e: e.dma_start(out=ccol[:], in_=ccol_in), writes=["ccol"], dma=True)
        P.op("q", lambda e: e.dma_start(out=dm[:], in_=dmask_in), writes=["dm"], dma=True)
        P.op("q", lambda e: e.dma_start(out=identf[:], in_=ident_in), writes=["identf"], dma=True)
        P.op("q", lambda e: e.dma_start(out=identb[:], in_=identb_in), writes=["identb"], dma=True)
        P.op("q", lambda e: e.dma_start(out=ones1[:], in_=ones_in), writes=["ones1"], dma=True)
        P.op("q", lambda e: e.dma_start(out=onesf[:], in_=onesf_in), writes=["onesf"], dma=True)
        P.op("v", lambda e: e.memset(zero1[:], 0.0), writes=["zero1"])
        P.op("v", lambda e: e.memset(cst[:, 0:1], 1.28e-4), writes=["cst"])
        P.op("v", lambda e: e.memset(cst[:, 1:2], 1e-6), writes=["cst"])
        P.op("v", lambda e: e.memset(cst[:, 2:3], 1e-5), writes=["cst"])

        def ln_tile(xin_t, xin_key, out_t, out_key, par=0, eps=1e-5):
            o = par * 8
            sk = f"stat{par}"
            ssx, smx, mneg, m2n2, rs = (stat[:, o + i:o + i + 1] for i in range(5))
            P.op("s", lambda e: e.activation(
                out_t[:], xin_t[:], AF.Identity, accum_out=smx),
                reads=[xin_key], writes=[out_key, sk])
            P.op("s", lambda e: e.activation(
                out_t[:], xin_t[:], AF.Square, accum_out=ssx),
                reads=[xin_key], writes=[out_key, sk])
            P.op("s", lambda e: e.mul(mneg, smx, -1.0 / D), reads=[sk], writes=[sk])
            P.op("s", lambda e: e.activation(m2n2, smx, AF.Square, scale=1.0 / D),
                reads=[sk], writes=[sk])
            P.op("s", lambda e: e.activation(m2n2, m2n2, AF.Identity,
                                             scale=-1.0, bias=cst[:, 2:3]),
                reads=[sk, "cst"], writes=[sk])
            P.op("s", lambda e: e.activation(rs, ssx, AF.Sqrt, bias=m2n2, scale=1.0 / D),
                reads=[sk], writes=[sk])
            P.op("v", lambda e: e.reciprocal(rs, rs), reads=[sk], writes=[sk])
            P.op("v", lambda e: e.tensor_scalar(out_t[:], xin_t[:], mneg, rs, ALU.add, ALU.mult),
                reads=[xin_key, sk], writes=[out_key])

        def xpose_to_fp8(src_t, src_key, tc):
            for dq in range(4):
                bi = 5 + ((tc * 4 + dq) % 2)
                bank = p[bi]
                bk = f"p{bi}"
                def tp4(e, dq=dq, bank=bank, src_t=src_t):
                    for j in range(4):
                        dc = dq * 4 + j
                        ii = e.transpose(bank[:, j * 128:(j + 1) * 128],
                                         src_t[:, dc * 128:(dc + 1) * 128], identf[:])
                    return ii
                P.op("t", tp4, reads=[src_key, "identf"], writes=[bk])
                def ev4(e, dq=dq, bank=bank, tc=tc):
                    for j in range(4):
                        dc = dq * 4 + j
                        s_, a_ = dc // 2, dc % 2
                        ii = e.copy(xln8[:, s_, a_, tc * 128:(tc + 1) * 128],
                                    bank[:, j * 128:(j + 1) * 128])
                    return ii
                P.op("s", ev4, reads=[bk], writes=["xln8"])

        for tc in range(NT):
            b = tc % 2
            P.op("q", lambda e, tc=tc, b=b: e.dma_start(
                out=xin[b][:], in_=x_in[tc * 128:(tc + 1) * 128, :]),
                writes=[f"xin{b}"], dma=True)
            ln_tile(xin[b], f"xin{b}", xl[b], f"xl{b}", par=b)
            xpose_to_fp8(xl[b], f"xl{b}", tc)
        P.barrier()

        def ffn(base_dram, out_dram, cache_weights, wbuf, act8, wbk, a8k):
            for half in range(2):
                first = cache_weights and half == 0
                toff = half * 1024
                for hb in range(16):
                    wb = wbuf[hb % 2]
                    k = wbk[hb % 2]
                    for g in range(2):
                        dst = wb[:, g * 8:(g + 1) * 8]
                        cache = wg8c[g, hb].rearrange("s p a c -> p s a c")
                        if first:
                            src = wg[g][:, hb * 512:(hb + 1) * 512].rearrange(
                                "(s a p) c -> p s a c", s=8, a=2, p=128)
                            P.op("g", lambda e, src=src, dst=dst: e.dma_start(
                                out=dst, in_=src), writes=[k], dma=True)
                            P.op("g", lambda e, dst=dst, cache=cache: e.dma_start(
                                out=cache, in_=dst), reads=[k], dma=True)
                        else:
                            P.op("g", lambda e, dst=dst, cache=cache: e.dma_start(
                                out=dst, in_=cache), writes=[k], dma=True)
                    for hc in range(4):
                        hca = hb * 4 + hc
                        hs, aa = hca // 2, hca % 2
                        for blk in range(2):
                            t0 = toff + blk * 512
                            rot = (hca * 2 + blk) % 2
                            pg0, pg1 = p[rot * 2], p[rot * 2 + 1]
                            pk0, pk1 = f"p{rot*2}", f"p{rot*2+1}"
                            def mm_g(e, wb=wb, hc=hc, t0=t0, pg0=pg0, pg1=pg1):
                                for g in range(2):
                                    for s_ in range(8):
                                        ii = e.matmul(
                                            (pg0 if g == 0 else pg1)[:],
                                            wb[:, g * 8 + s_, :, hc * 128:(hc + 1) * 128],
                                            xln8[:, s_, :, t0:t0 + 512],
                                            start=(s_ == 0), stop=(s_ == 7),
                                            perf_mode=DR)
                                return ii
                            P.op("t", mm_g, reads=[k, "xln8"], writes=[pk0, pk1])
                            gk = f"gate16_{rot}"
                            P.op("s", lambda e, pg0=pg0, rot=rot: e.activation(
                                gate16[rot][:], pg0[:], AF.Gelu_apprx_tanh),
                                reads=[pk0], writes=[gk])
                            tk = f"tmp16_{rot}"
                            P.op("v", lambda e, pg1=pg1, rot=rot: e.tensor_tensor(
                                tmp16[rot][:], gate16[rot][:], pg1[:], ALU.mult),
                                reads=[gk, pk1], writes=[tk])
                            P.op("s", lambda e, rot=rot, hs=hs, aa=aa, blk=blk: e.copy(
                                act8[:, hs, aa, blk * 512:(blk + 1) * 512],
                                tmp16[rot][:]),
                                reads=[tk], writes=[a8k])
                for dob in range(4):
                    lb0, lb1 = wbuf[(2 * dob) % 3], wbuf[(2 * dob + 1) % 3]
                    k0, k1 = wbk[(2 * dob) % 3], wbk[(2 * dob + 1) % 3]
                    for half16, (lbx, kx) in enumerate(((lb0, k0), (lb1, k1))):
                        dst = lbx[:]
                        cache = wl8c[dob, half16 * 16:(half16 + 1) * 16].rearrange(
                            "hs p a c -> p hs a c")
                        if first:
                            src = wl[half16 * 4096:(half16 + 1) * 4096,
                                     dob * 512:(dob + 1) * 512].rearrange(
                                "(hs a p) c -> p hs a c", hs=16, a=2, p=128)
                            P.op("g", lambda e, src=src, dst=dst: e.dma_start(
                                out=dst, in_=src), writes=[kx], dma=True)
                            P.op("g", lambda e, dst=dst, cache=cache: e.dma_start(
                                out=cache, in_=dst), reads=[kx], dma=True)
                        else:
                            P.op("g", lambda e, dst=dst, cache=cache: e.dma_start(
                                out=dst, in_=cache), writes=[kx], dma=True)
                    for tcl in range(8):
                        tca = half * 8 + tcl
                        rot = tcl % 2
                        po = p[4 + rot]
                        pok = f"p{4+rot}"
                        def mm_l(e, tcl=tcl, po=po, lb0=lb0, lb1=lb1):
                            for hs in range(32):
                                lbx = lb0 if hs < 16 else lb1
                                ii = e.matmul(
                                    po[:], act8[:, hs, :, tcl * 128:(tcl + 1) * 128],
                                    lbx[:, hs % 16], start=(hs == 0), stop=(hs == 31),
                                    perf_mode=DR)
                            return ii
                        P.op("t", mm_l, reads=[k0, k1, a8k], writes=[pok])
                        xrk = f"xres{rot}"
                        P.op("q", lambda e, rot=rot, tca=tca, dob=dob: e.dma_start(
                            out=xres[rot][:], in_=base_dram[
                                tca * 128:(tca + 1) * 128, dob * 512:(dob + 1) * 512]),
                            writes=[xrk], dma=True)
                        rk = f"rpiece{rot}"
                        P.op("v", lambda e, rot=rot, po=po: e.scalar_tensor_tensor(
                            rpiece[rot][:], po[:], 0.5, xres[rot][:], ALU.mult, ALU.add),
                            reads=[pok, xrk], writes=[rk])
                        P.op("q", lambda e, rot=rot, tca=tca, dob=dob: e.dma_start(
                            out=out_dram[tca * 128:(tca + 1) * 128,
                                         dob * 512:(dob + 1) * 512],
                            in_=rpiece[rot][:]), reads=[rk], dma=True)

        es1 = ExitStack()
        act8 = es1.enter_context(nc.sbuf_tensor("act8", [128, 32, 2, 1024], E4))
        wbuf = [es1.enter_context(nc.sbuf_tensor(f"wbuf{i}", [128, 16, 2, 512], E4))
                for i in range(3)]
        ffn(x_in, r1, True, wbuf, act8, ["wbuf0", "wbuf1", "wbuf2"], "act8")
        P.barrier()
        es1.close()

        if debug_stage == 1:
            for tc in range(NT):
                b = tc % 2
                P.op("q", lambda e, tc=tc, b=b: e.dma_start(
                    out=xin[b][:], in_=r1[tc * 128:(tc + 1) * 128, :]),
                    writes=[f"xin{b}"], dma=True)
                P.op("q", lambda e, tc=tc, b=b: e.dma_start(
                    out=y_out[tc * 128:(tc + 1) * 128, :], in_=xin[b][:]),
                    reads=[f"xin{b}"], dma=True)
            P.barrier()
        else:
            _build_attention(P, nc, locals())
            P.barrier()
            es3 = ExitStack()
            act8b = es3.enter_context(nc.sbuf_tensor("act8b", [128, 32, 2, 1024], E4))
            wbufb = [es3.enter_context(nc.sbuf_tensor(f"wbufb{i}", [128, 16, 2, 512], E4))
                     for i in range(3)]
            if debug_stage == 2:
                for tc in range(NT):
                    b = tc % 2
                    P.op("q", lambda e, tc=tc, b=b: e.dma_start(
                        out=xin[b][:], in_=r2[tc * 128:(tc + 1) * 128, :]),
                        writes=[f"xin{b}"], dma=True)
                    P.op("q", lambda e, tc=tc, b=b: e.dma_start(
                        out=y_out[tc * 128:(tc + 1) * 128, :], in_=xin[b][:]),
                        reads=[f"xin{b}"], dma=True)
                P.barrier()
            else:
                ffn(r2, y_out, False, wbufb, act8b, ["wbufb0", "wbufb1", "wbufb2"], "act8b")
                P.barrier()
            es3.close()

        P.emit(block, sems, dsems)
        ndma_q = sum(1 for (eng, _, _, _, dma) in P.ops if eng == "q" and dma)
        ndma_g = sum(1 for (eng, _, _, _, dma) in P.ops if eng == "g" and dma)

        def final_wait(e):
            if ndma_q:
                e.wait_ge(dsems["q"], 16 * ndma_q)
            if ndma_g:
                e.wait_ge(dsems["g"], 16 * ndma_g)
        block.sync(final_wait)

    return nc


def _build_attention(P, nc, env):
    p = env["p"]; pt16 = env["pt16"]
    xin = env["xin"]; xl = env["xl"]; stat = env["stat"]
    xln8 = env["xln8"]; dm = env["dm"]; ccol = env["ccol"]; acol = env["acol"]
    identf = env["identf"]; onesf = env["onesf"]; ones1 = env["ones1"]
    srow = env["srow"]
    rpiece = env["rpiece"]; xres = env["xres"]
    r1 = env["r1"]; r2 = env["r2"]; x2t_d = env["x2t_d"]; qt_d = env["qt_d"]
    cin = env["cin"]; cout = env["cout"]; wqc_d = env["wqc_d"]
    cosr = env["cosr"]; sinr = env["sinr"]
    wq = env["wq"]; wk = env["wk"]; wv = env["wv"]; wo = env["wo"]
    ln_tile = env["ln_tile"]; xpose_to_fp8 = env["xpose_to_fp8"]
    SBx = env["SBx"]; cst = env["cst"]

    esB = ExitStack()
    x2s = [SBx(esB, f"x2s{i}", [128, DC, 128], BF16) for i in range(2)]
    x2tile = [SBx(esB, f"x2tile{i}", [128, DC, 128], BF16) for i in range(2)]
    wpro = [SBx(esB, f"wpro{i}", [128, 16, 512], BF16) for i in range(2)]
    cost = SBx(esB, "cost", [128, 256], F32)
    sint = SBx(esB, "sint", [128, 256], F32)
    pjf = [SBx(esB, f"pjf{i}", [128, 512], F32) for i in range(2)]
    rsc = [SBx(esB, f"rsc{i}", [128, 512], F32) for i in range(2)]
    tt1 = [SBx(esB, f"tt1_{i}", [128, 4, 64], F32) for i in range(2)]
    tt2 = [SBx(esB, f"tt2_{i}", [128, 4, 64], F32) for i in range(2)]
    ror = [SBx(esB, f"ror{i}", [128, 512], F32) for i in range(2)]
    ror2 = [SBx(esB, f"ror2_{i}", [128, 512], F32) for i in range(2)]
    stg = [SBx(esB, f"stg{i}", [128, 4, 128], BF16) for i in range(2)]
    cost2 = [SBx(esB, f"cost2_{i}", [128, 256], F32) for i in range(2)]
    sint2 = [SBx(esB, f"sint2_{i}", [128, 256], F32) for i in range(2)]
    aco = SBx(esB, "aco", [128, 4, 16], F32)
    wqb = SBx(esB, "wqb", [128, 16, 512], BF16)
    wq4x = SBx(esB, "wq4x", [128, 16, 512], BF16)
    wq4 = [wpro[0], wpro[1], wqb, wq4x]
    wq4_alias = ["wpro0", "wpro1", "wqb", None]
    ahi = SBx(esB, "ahi", [128, 64], BF16)
    alo32 = SBx(esB, "alo32", [128, 64], F32)
    alo = SBx(esB, "alo", [128, 64], BF16)

    def load_cossin(tc):
        P.op("q", lambda e, tc=tc: e.dma_start(
            out=cost[:], in_=cosr[tc * 128:(tc + 1) * 128, :]),
            writes=["cost"], dma=True)
        P.op("q", lambda e, tc=tc: e.dma_start(
            out=sint[:], in_=sinr[tc * 128:(tc + 1) * 128, :]),
            writes=["sint"], dma=True)

    def rope(src, src_key, dst, dst_key):
        sh = src[:].rearrange("p (h d) -> p h d", h=4)
        dh = dst[:].rearrange("p (h d) -> p h d", h=4)
        ch = cost[:].rearrange("p (h d) -> p h d", h=4)
        sn = sint[:].rearrange("p (h d) -> p h d", h=4)
        P.op("v", lambda e: e.tensor_mul(tt1[:], sh[:, :, 0:64], ch[:]),
            reads=[src_key, "cost"], writes=["tt1"])
        P.op("v", lambda e: e.tensor_mul(tt2[:], sh[:, :, 64:128], sn[:]),
            reads=[src_key, "sint"], writes=["tt2"])
        P.op("v", lambda e: e.tensor_add(dh[:, :, 0:64], tt1[:], tt2[:]),
            reads=["tt1", "tt2"], writes=[dst_key])
        P.op("v", lambda e: e.tensor_mul(tt1[:], sh[:, :, 64:128], ch[:]),
            reads=[src_key, "cost"], writes=["tt1"])
        P.op("v", lambda e: e.tensor_mul(tt2[:], sh[:, :, 0:64], sn[:]),
            reads=[src_key, "sint"], writes=["tt2"])
        P.op("v", lambda e: e.tensor_sub(dh[:, :, 64:128], tt1[:], tt2[:]),
            reads=["tt1", "tt2"], writes=[dst_key])

    def sumsq4(src_key):
        for h4 in range(4):
            P.op("s", lambda e, h4=h4: e.activation(
                rsc[:, h4 * 128:(h4 + 1) * 128], pjf[:, h4 * 128:(h4 + 1) * 128],
                AF.Square, accum_out=stat[:, h4:h4 + 1]),
                reads=[src_key], writes=["rsc", "stat"])

    for tc in range(NT):
        b = tc % 2
        P.op("q", lambda e, tc=tc, b=b: e.dma_start(
            out=xin[b][:], in_=r1[tc * 128:(tc + 1) * 128, :]),
            writes=[f"xin{b}"], dma=True)
        ln_tile(xin[b], f"xin{b}", xl[b], f"xl{b}", par=b)
        for dq in range(4):
            bi = 5 + ((tc * 4 + dq) % 2)
            bk = f"p{bi}"
            def tp4(e, dq=dq, bi=bi, b=b):
                for j in range(4):
                    dc = dq * 4 + j
                    ii = e.transpose(p[bi][:, j * 128:(j + 1) * 128],
                                     xl[b][:, dc * 128:(dc + 1) * 128], identf[:])
                return ii
            P.op("t", tp4, reads=[f"xl{b}", "identf"], writes=[bk])
            def ev4(e, dq=dq, bi=bi, b=b):
                for j in range(4):
                    dc = dq * 4 + j
                    ii = e.copy(x2s[b][:, dc, :], p[bi][:, j * 128:(j + 1) * 128])
                return ii
            P.op("s", ev4, reads=[bk], writes=[f"x2s{b}"])
        P.op("q", lambda e, tc=tc, b=b: e.dma_start(
            out=x2t_d[tc].rearrange("dc p c -> p dc c"), in_=x2s[b][:]),
            reads=[f"x2s{b}"], writes=[f"x2td{tc}"], dma=True)

    P.op("g", lambda e: e.dma_start(
        out=wpro[0][:], in_=wk.rearrange("(vc p) c -> p vc c", p=128)),
        writes=["wpro0"], dma=True)
    P.op("g", lambda e: e.dma_start(
        out=wpro[1][:], in_=wv.rearrange("(vc p) c -> p vc c", p=128)),
        writes=["wpro1"], dma=True)

    def load_cossin2(tc, b):
        P.op("q", lambda e, tc=tc, b=b: e.dma_start(
            out=cost2[b][:], in_=cosr[tc * 128:(tc + 1) * 128, :]),
            writes=[f"cost2_{b}"], dma=True)
        P.op("q", lambda e, tc=tc, b=b: e.dma_start(
            out=sint2[b][:], in_=sinr[tc * 128:(tc + 1) * 128, :]),
            writes=[f"sint2_{b}"], dma=True)

    def rope2(b, src, src_key, dst, dst_key, tb=None):
        tb = b if tb is None else tb
        sh = src[:].rearrange("p (h d) -> p h d", h=4)
        dh = dst[:].rearrange("p (h d) -> p h d", h=4)
        ch = cost2[tb][:].rearrange("p (h d) -> p h d", h=4)
        sn = sint2[tb][:].rearrange("p (h d) -> p h d", h=4)
        ck, sk2 = f"cost2_{tb}", f"sint2_{tb}"
        t1k, t2k = f"tt1_{b}", f"tt2_{b}"
        P.op("v", lambda e: e.tensor_mul(tt1[b][:], sh[:, :, 0:64], ch[:]),
            reads=[src_key, ck], writes=[t1k])
        P.op("v", lambda e: e.tensor_mul(tt2[b][:], sh[:, :, 64:128], sn[:]),
            reads=[src_key, sk2], writes=[t2k])
        P.op("v", lambda e: e.tensor_add(dh[:, :, 0:64], tt1[b][:], tt2[b][:]),
            reads=[t1k, t2k], writes=[dst_key])
        P.op("v", lambda e: e.tensor_mul(tt1[b][:], sh[:, :, 64:128], ch[:]),
            reads=[src_key, ck], writes=[t1k])
        P.op("v", lambda e: e.tensor_mul(tt2[b][:], sh[:, :, 0:64], sn[:]),
            reads=[src_key, sk2], writes=[t2k])
        P.op("v", lambda e: e.tensor_sub(dh[:, :, 64:128], tt1[b][:], tt2[b][:]),
            reads=[t1k, t2k], writes=[dst_key])

    def sumsq4b(b, src_key, outslice, outkey):
        for h4 in range(4):
            P.op("s", lambda e, h4=h4, b=b: e.activation(
                rsc[b][:, h4 * 128:(h4 + 1) * 128], pjf[b][:, h4 * 128:(h4 + 1) * 128],
                AF.Square, accum_out=stat[:, b * 8 + h4:b * 8 + h4 + 1]),
                reads=[src_key], writes=[f"rsc{b}", f"stat{b}"])

    kv_back = [None]
    for tc in range(NT):
        if tc % 4 == 2:
            qdb = tc // 4
            wslot = qdb % 2
            P.op("g", lambda e, qdb=qdb: e.dma_start(
                out=wqb[:], in_=wq[:, qdb * 512:(qdb + 1) * 512].rearrange(
                    "(vc p) c -> p vc c", p=128)),
                writes=["wqb"], dma=True)
            P.op("g", lambda e, qdb=qdb: e.dma_start(
                out=wqc_d[qdb], in_=wqb[:]),
                reads=["wqb"], writes=[f"wqc{qdb}"], dma=True)
        b = tc % 2
        xk = f"x2tile{b}"
        pjk = f"pjf{b}"
        rk = f"ror{b}"
        sgk = f"stg{b}"
        pK, pKk = (p[0], "p0") if b == 0 else (p[2], "p2")
        pV, pVk = (p[1], "p1") if b == 0 else (p[3], "p3")
        pT = 5 + b
        pTk = f"p{pT}"
        P.op("g", lambda e, tc=tc, b=b: e.dma_start(
            out=x2tile[b][:], in_=x2t_d[tc].rearrange("dc p c -> p dc c")),
            reads=[f"x2td{tc}"], writes=[xk], dma=True)
        load_cossin2(tc, b)
        def mmk(e, b=b, pK=pK):
            for dc in range(DC):
                ii = e.matmul(pK[:], x2tile[b][:, dc, :], wpro[0][:, dc, :],
                              start=(dc == 0), stop=(dc == DC - 1))
            return ii
        P.op("t", mmk, reads=[xk, "wpro0"], writes=[pKk])
        P.op("s", lambda e, b=b, pK=pK: e.copy(pjf[b][:], pK[:]),
            reads=[pKk], writes=[pjk])
        def mmv(e, b=b, pV=pV):
            for dc in range(DC):
                ii = e.matmul(pV[:], x2tile[b][:, dc, :], wpro[1][:, dc, :],
                              start=(dc == 0), stop=(dc == DC - 1))
            return ii
        P.op("t", mmv, reads=[xk, "wpro1"], writes=[pVk])
        P.op("s", lambda e, b=b, pV=pV: e.copy(
            stg[b][:].rearrange("p h c -> p (h c)"), pV[:]),
            reads=[pVk], writes=[sgk])
        P.op("q", lambda e, tc=tc, b=b: e.dma_start(
            out=cin[512 + tc * 32:512 + (tc + 1) * 32, :].rearrange(
                "a (bb c) -> (a bb) c", bb=4),
            in_=stg[b][:].rearrange("p h c -> p (h c)")), reads=[sgk], writes=["cin"], dma=True)
        sumsq4b(b, pjk, None, None)
        P.op("s", lambda e, tc=tc, b=b: e.activation(
            aco[:, :, tc:tc + 1], stat[:, b * 8:b * 8 + 4], AF.Sqrt, bias=cst[:, 0:1]),
            reads=[f"stat{b}", "cst"], writes=["aco"])
        P.op("v", lambda e, tc=tc: e.reciprocal(aco[:, :, tc:tc + 1], aco[:, :, tc:tc + 1]),
            reads=["aco"], writes=["aco"])
        rope2(b, pjf[b], pjk, ror[b], rk)
        def back_kv(tc=tc, b=b, pT=pT, pTk=pTk, rk=rk, sgk=sgk):
            def tpk(e, b=b, pT=pT):
                for j in range(4):
                    ii = e.transpose(p[pT][:, j * 128:(j + 1) * 128],
                                     ror[b][:, j * 128:(j + 1) * 128], identf[:])
                return ii
            P.op("t", tpk, reads=[rk, "identf"], writes=[pTk])
            def evk(e, b=b, pT=pT):
                for j in range(4):
                    ii = e.copy(stg[b][:, j, :], p[pT][:, j * 128:(j + 1) * 128])
                return ii
            P.op("s", evk, reads=[pTk], writes=[sgk])
            P.op("q", lambda e, tc=tc, b=b: e.dma_start(
                out=cin[0:512, tc * 128:(tc + 1) * 128].rearrange(
                    "(kv p) c -> p kv c", p=128),
                in_=stg[b][:]), reads=[sgk], writes=["cin"], dma=True)
        if kv_back[0] is not None:
            kv_back[0]()
        kv_back[0] = back_kv
    if kv_back[0] is not None:
        kv_back[0]()
        kv_back[0] = None
    P.op("s", lambda e: e.copy(ahi[:], aco[:].rearrange("p a bb -> p (a bb)")),
        reads=["aco"], writes=["ahi"])
    P.op("v", lambda e: e.tensor_sub(alo32[:], aco[:].rearrange("p a bb -> p (a bb)"), ahi[:]),
        reads=["aco", "ahi"], writes=["alo32"])
    P.op("s", lambda e: e.copy(alo[:], alo32[:]), reads=["alo32"], writes=["alo"])
    P.op("q", lambda e: e.dma_start(
        out=cin[1024:1028, :].rearrange("r (pp f) -> (r pp) f", pp=32),
        in_=ahi[:]), reads=["ahi"], writes=["cin"], dma=True)
    P.op("q", lambda e: e.dma_start(
        out=cin[1028:1032, :].rearrange("r (pp f) -> (r pp) f", pp=32),
        in_=alo[:]), reads=["alo"], writes=["cin"], dma=True)

    groups = [[0, 1], [2, 3], [4, 5], [6, 7]]
    P.op("g", lambda e: e.collective_compute(
        "AllGather", ALU.bypass, groups, [cin], [cout]),
        reads=["cin"], writes=["cout"])

    for qdb in range(4):
        wr = [f"wq4_{qdb}"] + ([wq4_alias[qdb]] if wq4_alias[qdb] else [])
        P.op("q", lambda e, qdb=qdb: e.dma_start(
            out=wq4[qdb][:], in_=wqc_d[qdb]),
            reads=[f"wqc{qdb}"], writes=wr, dma=True)
    for tc in range(NT):
        b = tc % 2
        xk = f"x2tile{b}"
        P.op("q", lambda e, tc=tc, b=b: e.dma_start(
            out=x2tile[b][:], in_=x2t_d[tc].rearrange("dc p c -> p dc c")),
            reads=[f"x2td{tc}"], writes=[xk], dma=True)
        load_cossin2(tc, b)
        for qdb in range(4):
            u = (tc * 4 + qdb) % 2
            pjk = f"pjf{u}"
            rk = f"ror{u}"
            r2k = f"ror2_{u}"
            sgk = f"stg{u}"
            pQ, pQk = (p[0], "p0") if u == 0 else (p[2], "p2")
            pT = 5 + u
            pTk = f"p{pT}"
            def mmq(e, b=b, pQ=pQ, qdb=qdb):
                for dc in range(DC):
                    ii = e.matmul(pQ[:], x2tile[b][:, dc, :], wq4[qdb][:, dc, :],
                                  start=(dc == 0), stop=(dc == DC - 1))
                return ii
            P.op("t", mmq, reads=[xk, f"wq4_{qdb}"], writes=[pQk])
            P.op("s", lambda e, u=u, pQ=pQ: e.copy(pjf[u][:], pQ[:]),
                reads=[pQk], writes=[pjk])
            sumsq4b(u, pjk, None, None)
            P.op("s", lambda e, u=u: e.activation(
                stat[:, u * 8 + 4:u * 8 + 8], stat[:, u * 8:u * 8 + 4], AF.Sqrt,
                bias=cst[:, 1:2], scale=1.0 / HD),
                reads=[f"stat{u}", "cst"], writes=[f"stat{u}"])
            P.op("v", lambda e, u=u: e.reciprocal(
                stat[:, u * 8 + 4:u * 8 + 8], stat[:, u * 8 + 4:u * 8 + 8]),
                reads=[f"stat{u}"], writes=[f"stat{u}"])
            rope2(u, pjf[u], pjk, ror[u], rk, tb=b)
            def bq4(e, u=u):
                for h4 in range(4):
                    ii = e.tensor_scalar(
                        ror2[u][:, h4 * 128:(h4 + 1) * 128],
                        ror[u][:, h4 * 128:(h4 + 1) * 128],
                        stat[:, u * 8 + 4 + h4:u * 8 + 5 + h4], None, ALU.mult)
                return ii
            P.op("v", bq4, reads=[rk, f"stat{u}"], writes=[r2k])
            def back_q(tc=tc, qdb=qdb, u=u, pT=pT, pTk=pTk, r2k=r2k, sgk=sgk):
                def tpq(e, u=u, pT=pT):
                    for j in range(4):
                        ii = e.transpose(p[pT][:, j * 128:(j + 1) * 128],
                                         ror2[u][:, j * 128:(j + 1) * 128], identf[:])
                    return ii
                P.op("t", tpq, reads=[r2k, "identf"], writes=[pTk])
                def evq(e, u=u, pT=pT):
                    for j in range(4):
                        ii = e.copy(stg[u][:, j, :], p[pT][:, j * 128:(j + 1) * 128])
                    return ii
                P.op("s", evq, reads=[pTk], writes=[sgk])
                P.op("q", lambda e, tc=tc, qdb=qdb, u=u: e.dma_start(
                    out=qt_d[tc, qdb * 4:(qdb + 1) * 4].rearrange("q p c -> p q c"),
                    in_=stg[u][:]), reads=[sgk], writes=["qtd"], dma=True)
            if kv_back[0] is not None:
                kv_back[0]()
            kv_back[0] = back_q
    if kv_back[0] is not None:
        kv_back[0]()
        kv_back[0] = None
    P.barrier()
    esB.close()

    esC = ExitStack()
    vt = SBx(esC, "vt", [128, 32, 512], BF16)
    ktbuf = [SBx(esC, f"ktbuf{i}", [128, 4096], BF16) for i in range(2)]
    qtbuf = [SBx(esC, f"qtbuf{i}", [128, 4, 512], BF16) for i in range(2)]
    et = [SBx(esC, f"et{i}", [128, 512], BF16) for i in range(4)]
    s2f = [SBx(esC, f"s2f{i}", [128, 1024], F32) for i in range(2)]
    yt = SBx(esC, "yt", [128, 16, 512], BF16)
    rb = [SBx(esC, f"rb{i}", [128, 512], F32) for i in range(2)]
    wo16 = SBx(esC, "wo16", [128, 16, 512], BF16)
    ahl = [SBx(esC, f"ahl{i}", [128, 64], F32) for i in range(2)]
    esum = [SBx(esC, f"esum{i}", [128, 512], F32) for i in range(2)]
    onesfb = SBx(esC, "onesfbS", [128, 1], F32)

    P.op("v", lambda e: e.memset(onesfb[:], 1.0), writes=["onesfb"])
    for kc in range(32):
        slot, lt = kc // 16, kc % 16
        P.op("g", lambda e, kc=kc, slot=slot, lt=lt: e.dma_start(
            out=vt[:, kc, :],
            in_=cout[slot, 512 + lt * 32:512 + (lt + 1) * 32, :].rearrange(
                "a (b c) -> (a b) c", b=4)),
            reads=["cout"], writes=["vt"], dma=True)
    for slot in range(2):
        P.op("g", lambda e, slot=slot: e.dma_start(
            out=ahl[0][:], in_=cout[slot, 1024:1028, :].rearrange(
                "r (pp f) -> (r pp) f", pp=32)),
            reads=["cout"], writes=["ahl0"], dma=True)
        P.op("g", lambda e, slot=slot: e.dma_start(
            out=ahl[1][:], in_=cout[slot, 1028:1032, :].rearrange(
                "r (pp f) -> (r pp) f", pp=32)),
            reads=["cout"], writes=["ahl1"], dma=True)
        P.op("v", lambda e, slot=slot: e.tensor_add(
            acol[:].rearrange("p (a b) -> p a b", a=4)[:, :, slot * 16:(slot + 1) * 16],
            ahl[0][:].rearrange("p (a b) -> p a b", a=4),
            ahl[1][:].rearrange("p (a b) -> p a b", a=4)),
            reads=["ahl0", "ahl1"], writes=["acol"])

    for qb in range(4):
        for kv in range(4):
            cmb = qb * 4 + kv
            rot = cmb % 2
            kk = f"ktbuf{rot}"
            qk = f"qtbuf{rot}"
            for slot in range(2):
                P.op("g", lambda e, kv=kv, slot=slot, rot=rot: e.dma_start(
                    out=ktbuf[rot][:, slot * 2048:(slot + 1) * 2048],
                    in_=cout[slot, kv * 128:(kv + 1) * 128, :]),
                    reads=["cout"], writes=[kk], dma=True)
            for tq in range(4):
                P.op("g", lambda e, qb=qb, kv=kv, rot=rot, tq=tq: e.dma_start(
                    out=qtbuf[rot][:, :, tq * 128:(tq + 1) * 128],
                    in_=qt_d[qb * 4 + tq, kv * 4:(kv + 1) * 4].rearrange(
                        "q p c -> p q c")),
                    reads=["qtd"], writes=[qk], dma=True)
            nkc = min(32, 4 * qb + 20)
            sbanks = (1, 2, 4, 5, 6)
            LAG = 3
            for qh in range(4):
                py = p[0] if qh % 2 == 0 else p[3]
                pyk = "p0" if qh % 2 == 0 else "p3"
                esk = f"esum{qh % 2}"
                esb = esum[qh % 2]
                def emit_scores(e_kc, rot=rot, qh=qh):
                    bi = sbanks[e_kc % 5]
                    ps, psk = p[bi], f"p{bi}"
                    P.op("t", lambda e, ps=ps, rot=rot, kc=e_kc, qh=qh: e.matmul(
                        ps[:], ktbuf[rot][:, kc * 128:(kc + 1) * 128],
                        qtbuf[rot][:, qh, :], start=True, stop=True),
                        reads=[kk, qk], writes=[psk])
                def emit_tail(kc, rot=rot, qh=qh, qb=qb, kv=kv):
                    bi = sbanks[kc % 5]
                    ps, psk = p[bi], f"p{bi}"
                    ee = kc % 4
                    ek = f"et{ee}"
                    j = kc - 4 * qb
                    jj = kc - 16 - 4 * qb
                    dslot = j if 0 <= j < 4 else (4 + jj if 0 <= jj < 4 else None)
                    if dslot is not None:
                        sk = f"s2f{ee % 2}"
                        P.op("v", lambda e, ps=ps, ee=ee, dslot=dslot: e.tensor_add(
                            s2f[ee % 2][:, (ee // 2) * 512:(ee // 2 + 1) * 512],
                            ps[:], dm[:, dslot, :]),
                            reads=[psk, "dm"], writes=[sk])
                        src = s2f[ee % 2][:, (ee // 2) * 512:(ee // 2 + 1) * 512]
                        srck = sk
                    else:
                        src, srck = ps[:], psk
                    P.op("s", lambda e, src=src, ee=ee, kv=kv, kc=kc, qb=qb: e.activation(
                        et[ee][:], src, AF.Exp,
                        bias=ccol[:, qb * 32 + kc:qb * 32 + kc + 1],
                        scale=acol[:].rearrange("p (a bb) -> p a bb", a=4)[:, kv, kc:kc + 1]),
                        reads=[srck, "ccol", "acol"], writes=[ek])
                    if kc == 0:
                        P.op("v", lambda e, ee=ee: e.tensor_copy(esb[:], et[ee][:]),
                            reads=[ek], writes=[esk])
                    else:
                        P.op("v", lambda e, ee=ee: e.tensor_add(esb[:], esb[:], et[ee][:]),
                            reads=[ek, esk], writes=[esk])
                    P.op("t", lambda e, ee=ee, kc=kc, kv=kv, py=py: e.matmul(
                        py[:], vt[:, kc, kv * 128:(kv + 1) * 128],
                        et[ee][:], start=(kc == 0), stop=(kc == nkc - 1)),
                        reads=[ek, "vt"], writes=[pyk])
                for i in range(nkc + LAG):
                    if i < nkc:
                        emit_scores(i)
                    if i >= LAG:
                        emit_tail(i - LAG)
                rr = qh % 2
                srow_r = 32 * (qh % 3)
                P.op("t", lambda e, srow_r=srow_r: e.matmul(
                    p[4][srow_r:srow_r + 1, :], onesfb[:], esb[:],
                    start=True, stop=True),
                    reads=[esk, "onesfb"], writes=["p4"])
                P.op("s", lambda e, srow_r=srow_r: e.copy(srow[:], p[4][srow_r:srow_r + 1, :]),
                    reads=["p4"], writes=["srow"])
                P.op("v", lambda e: e.reciprocal(srow[:], srow[:]),
                    reads=["srow"], writes=["srow"])
                P.op("t", lambda e: e.matmul(p[2][:], onesf[:], srow[:],
                                             start=True, stop=True),
                    reads=["srow", "onesf"], writes=["p2"])
                P.op("s", lambda e, rr=rr: e.copy(rb[rr][:], p[2][:]),
                    reads=["p2"], writes=[f"rb{rr}"])
                P.op("v", lambda e, qh=qh, kv=kv, rr=rr, py=py, pyk=pyk: e.tensor_mul(
                    yt[:, kv * 4 + qh, :], py[:], rb[rr][:]),
                    reads=[pyk, f"rb{rr}"], writes=["yt"])
        for dob in range(4):
            P.op("g", lambda e, dob=dob: e.dma_start(
                out=wo16[:], in_=wo[:, dob * 512:(dob + 1) * 512].rearrange(
                    "(vc p) c -> p vc c", p=128)),
                writes=["wo16"], dma=True)
            for tcq in range(4):
                po = p[tcq]
                pok = f"p{tcq}"
                def mm_wo(e, tcq=tcq, po=po):
                    for vc in range(16):
                        ii = e.matmul(po[:], yt[:, vc, tcq * 128:(tcq + 1) * 128],
                                      wo16[:, vc, :], start=(vc == 0), stop=(vc == 15))
                    return ii
                P.op("t", mm_wo, reads=["yt", "wo16"], writes=[pok])
                rr = tcq % 2
                P.op("q", lambda e, qb=qb, tcq=tcq, dob=dob, rr=rr: e.dma_start(
                    out=xres[rr][:, 0:512], in_=r1[
                        qb * 512 + tcq * 128:qb * 512 + (tcq + 1) * 128,
                        dob * 512:(dob + 1) * 512]),
                    writes=[f"xres{rr}"], dma=True)
                P.op("v", lambda e, po=po, rr=rr: e.tensor_add(
                    rpiece[rr][:], po[:], xres[rr][:, 0:512]),
                    reads=[pok, f"xres{rr}"], writes=[f"rpiece{rr}"])
                P.op("q", lambda e, qb=qb, tcq=tcq, dob=dob, rr=rr: e.dma_start(
                    out=r2[qb * 512 + tcq * 128:qb * 512 + (tcq + 1) * 128,
                           dob * 512:(dob + 1) * 512],
                    in_=rpiece[rr][:]), reads=[f"rpiece{rr}"], dma=True)
    P.barrier()
    esC.close()

    for tc in range(NT):
        b = tc % 2
        P.op("q", lambda e, tc=tc, b=b: e.dma_start(
            out=xin[b][:], in_=r2[tc * 128:(tc + 1) * 128, :]),
            writes=[f"xin{b}"], dma=True)
        ln_tile(xin[b], f"xin{b}", xl[b], f"xl{b}", par=b)
        xpose_to_fp8(xl[b], f"xl{b}", tc)


def _host_inputs(core, full):
    b, h = core // 2, core % 2
    row0 = h * T
    x = np.ascontiguousarray(full["L0_attn_x"][b, row0:row0 + T, :])

    half = 64
    inv = 1.0 / (100000.0 ** (np.arange(half, dtype=np.float64) / half))
    pos = np.arange(row0, row0 + T, dtype=np.float64)[:, None] * inv[None, :]
    cos1 = np.cos(pos).astype(np.float32)
    sin1 = np.sin(pos).astype(np.float32)
    cosr = np.tile(cos1, (1, 4))
    sinr = np.tile(sin1, (1, 4))

    ccol = np.zeros((128, 4, 32), np.float32)
    for qb in range(4):
        for kc in range(32):
            if kc >= row0 // 128 + 4 * qb + 4:
                ccol[:, qb, kc] = -30000.0
    ccol = ccol.reshape(128, 128)
    dmask = np.zeros((128, 8, 512), ml_dtypes.bfloat16)
    r = np.arange(128)[:, None]
    c = np.arange(512)[None, :]
    for j in range(4):
        pat = np.where(j * 128 + r > c, -1e6, 0.0).astype(ml_dtypes.bfloat16)
        if h == 0:
            dmask[:, j, :] = pat
        dmask[:, 4 + j, :] = pat

    return {
        "x": x,
        "wq": full["L0_attn_wq"], "wk": full["L0_attn_wk"],
        "wv": full["L0_attn_wv"], "wo": full["L0_attn_wo"],
        "wg": full["L0_ffn_w_gating"], "wl": full["L0_ffn_w_linear"],
        "cosr": cosr, "sinr": sinr, "ccol": ccol, "dmask": np.asarray(dmask),
        "identf": np.eye(128, dtype=np.float32),
        "identb": np.asarray(np.eye(128, dtype=ml_dtypes.bfloat16)),
        "ones1": np.asarray(np.ones((128, 2), ml_dtypes.bfloat16)),
        "onesf": np.ones((1, 128), np.float32),
    }


def kernel(**inputs):
    nc = build_program(debug_stage=DEBUG_STAGE)
    in_maps = [_host_inputs(c, inputs) for c in range(8)]
    res = run_bass_kernel_spmd(nc, in_maps, core_ids=list(range(8)))
    out = np.empty((4, 4096, 2048), np.float32)
    for c in range(8):
        b, h = c // 2, c % 2
        out[b, h * T:(h + 1) * T, :] = res.results[c]["y"]
    return out
